# version 05
# baseline (speedup 1.0000x reference)
import numpy as np
import ml_dtypes
import concourse.bass as bass
import concourse.mybir as mybir
from concourse.bass import IndirectOffsetOnAxis
from concourse.bass_utils import run_bass_kernel_spmd
from contextlib import ExitStack

F32 = mybir.dt.float32
BF16 = mybir.dt.bfloat16
I32 = mybir.dt.int32
U32 = mybir.dt.uint32
AF = mybir.ActivationFunctionType
ALU = mybir.AluOpType
AX = mybir.AxisListType

T = 2048
D = 2048
R = 2048
DH = 512
II = 5632
NK = 16
NDH = 4
NIC = 44
NTT = 16
NTB = 4
EPS = 1e-5


def build_nc(race_check=True):
    nc = bass.Bass("TRN2", target_bir_lowering=False, debug=False,
                   detect_race_conditions=race_check)

    xc = nc.dram_tensor("xc", [T, D], F32, kind="ExternalInput").ap()
    xrep = nc.dram_tensor("xrep", [T, D], F32, kind="ExternalInput").ap()
    xhl = nc.dram_tensor("xhl", [1, D], F32, kind="ExternalInput").ap()
    xhr = nc.dram_tensor("xhr", [1, D], F32, kind="ExternalInput").ap()
    wg1h = nc.dram_tensor("wg1h", [128, NK, DH], F32, kind="ExternalInput").ap()
    wg2h = nc.dram_tensor("wg2h", [128, NDH, R], F32, kind="ExternalInput").ap()
    wuch = nc.dram_tensor("wuch", [NK, 128, D], F32, kind="ExternalInput").ap()
    wvch = nc.dram_tensor("wvch", [NK, 128, D], F32, kind="ExternalInput").ap()
    wurh = nc.dram_tensor("wurh", [128, NK, D], F32, kind="ExternalInput").ap()
    wgch = nc.dram_tensor("wgch", [NIC, 128, D], F32, kind="ExternalInput").ap()
    wupch = nc.dram_tensor("wupch", [NIC, 128, D], F32, kind="ExternalInput").ap()
    wdch = nc.dram_tensor("wdch", [NK, 128, II], F32, kind="ExternalInput").ap()
    buh = nc.dram_tensor("buh", [128, NK], F32, kind="ExternalInput").ap()
    bvh = nc.dram_tensor("bvh", [128, NK], F32, kind="ExternalInput").ap()
    bg1h = nc.dram_tensor("bg1h", [128, NDH], F32, kind="ExternalInput").ap()
    bg2h = nc.dram_tensor("bg2h", [128, NK], F32, kind="ExternalInput").ap()
    cwh = nc.dram_tensor("cwh", [128, 3 * NK], F32, kind="ExternalInput").ap()
    cbh = nc.dram_tensor("cbh", [128, NK], F32, kind="ExternalInput").ap()
    gsh = nc.dram_tensor("gsh", [128, NIC], F32, kind="ExternalInput").ap()
    ush = nc.dram_tensor("ush", [128, NIC], F32, kind="ExternalInput").ap()
    dsh = nc.dram_tensor("dsh", [128, NK], F32, kind="ExternalInput").ap()
    identh = nc.dram_tensor("identh", [128, 128], BF16, kind="ExternalInput").ap()
    amh = nc.dram_tensor("amh", [128, 3], F32, kind="ExternalInput").ap()
    widxh = nc.dram_tensor("widxh", [128, 1], U32, kind="ExternalInput").ap()
    ridxh = nc.dram_tensor("ridxh", [128, 1], U32, kind="ExternalInput").ap()
    tagh = nc.dram_tensor("tagh", [128, 1], I32, kind="ExternalInput").ap()
    epsh = nc.dram_tensor("epsh", [128, 1], F32, kind="ExternalInput").ap()

    y = nc.dram_tensor("y", [T, D], F32, kind="ExternalOutput").ap()

    cxh = nc.dram_tensor("cxh", [D, T], BF16, kind="Internal").ap()
    vgh = nc.dram_tensor("vgh", [R, T], BF16, kind="Internal").ap()
    y1h = nc.dram_tensor("y1h", [T, D], BF16, kind="Internal").ap()
    z2h = nc.dram_tensor("z2h", [D, T], BF16, kind="Internal").ap()
    shS = nc.dram_tensor("shS", [256, 32], F32, kind="Internal", addr_space="Shared").ap()
    shF = nc.dram_tensor("shF", [256, 1], I32, kind="Internal", addr_space="Shared").ap()

    es0 = ExitStack()
    sb = lambda st, n, shape, dt, **kw: st.enter_context(nc.sbuf_tensor(n, shape, dt, **kw))
    psum = lambda st, n, shape, dt: st.enter_context(nc.psum_tensor(n, shape, dt))

    sem_names = [
        "sLD0", "sVE0",
        "sX1a", "sX1b", "sPE1", "sVE1",
        "sVE2", "sCX2a", "sCX2b",
        "sW3", "sPE3", "sSC3",
        "sWU4a", "sWU4b", "sPE4", "sSC4", "sVE4", "sVG4a", "sVG4b",
        "sXRa", "sXRb", "sPER_t", "sVER_t", "sWR1", "sWRa", "sWRb", "sPER", "sSCR", "sVER",
        "sEXw", "sPL", "sEXr", "sVEX",
        "sWUR", "sVG5a", "sVG5b", "sVE5",
        "sCXTa", "sCXTb", "sPE6", "sSC6", "sVE6", "sY1a", "sY1b",
        "sW7ga", "sW7gb", "sW7da", "sW7db", "sY1F", "sPE7", "sSC7", "sVE7",
        "sZ2a", "sZ2b", "sZ2Ta", "sZ2Tb", "sVE7L", "sSC7L", "sOUTa", "sOUTb",
    ]
    S = {n: es0.enter_context(nc.semaphore(n)) for n in sem_names}

    idb = sb(es0, "idb", [128, 128], BF16)
    buc = sb(es0, "buc", [128, NK], F32)
    bvc = sb(es0, "bvc", [128, NK], F32)
    bg1c = sb(es0, "bg1c", [128, NDH], F32)
    bg2c = sb(es0, "bg2c", [128, NK], F32)
    cwc = sb(es0, "cwc", [128, 3 * NK], F32)
    cbc = sb(es0, "cbc", [128, NK], F32)
    w1p1 = sb(es0, "w1p1", [128, NK], F32)
    gsc = sb(es0, "gsc", [128, NIC], F32)
    usc = sb(es0, "usc", [128, NIC], F32)
    dsc = sb(es0, "dsc", [128, NK], F32)
    amc = sb(es0, "amc", [128, 3], F32)
    widx = sb(es0, "widx", [128, 1], U32)
    ridx = sb(es0, "ridx", [128, 1], U32)
    tagc = sb(es0, "tagc", [128, 1], I32)
    Sboth = sb(es0, "Sboth", [128, 32], F32)
    Pboth = sb(es0, "Pboth", [128, 32], F32)
    Pcol = sb(es0, "Pcol", [128, NK], F32)
    S4 = sb(es0, "S4", [128, 64], F32)
    S4r = sb(es0, "S4r", [128, 64], F32)
    flg = sb(es0, "flg", [128, 1], I32)
    rowm = sb(es0, "rowm", [128, 1], F32)
    rowt = sb(es0, "rowt", [128, 1], F32)
    rowv = sb(es0, "rowv", [128, 1], F32)
    rowsd = sb(es0, "rowsd", [128, 1], F32)
    rowrs = sb(es0, "rowrs", [128, 1], F32)
    rowsum = sb(es0, "rowsum", [128, 1], F32)
    rowsq = sb(es0, "rowsq", [128, 1], F32)
    rowsumS = sb(es0, "rowsumS", [128, 1], F32)
    epsB = sb(es0, "epsB", [128, 1], F32)
    rowmS = sb(es0, "rowmS", [128, 1], F32)
    rowtS2 = sb(es0, "rowtS2", [128, 1], F32)
    rowP = sb(es0, "rowP", [128, 1], F32)
    rowQ = sb(es0, "rowQ", [128, 1], F32)
    rowL = sb(es0, "rowL", [128, 1], F32)
    rowH = sb(es0, "rowH", [128, 1], F32)
    rowRS = sb(es0, "rowRS", [128, 1], F32)

    N_LD0 = 15 + 32

    es_cs = ExitStack()
    cs = sb(es_cs, "cs", [128, NK, T], BF16)
    es_xT = ExitStack()
    xT = sb(es_xT, "xT", [128, NK, T + 2], BF16)

    with nc.Block() as blk:
        @blk.gpsimd
        def _(g):
            for dst, src in [(idb, identh), (buc, buh), (bvc, bvh), (bg1c, bg1h),
                             (bg2c, bg2h), (cwc, cwh), (cbc, cbh), (gsc, gsh),
                             (usc, ush), (dsc, dsh), (amc, amh), (widx, widxh),
                             (ridx, ridxh), (tagc, tagh), (epsB, epsh)]:
                g.dma_start(out=dst[:], in_=src).then_inc(S["sLD0"], 16)
            for dj in range(NK):
                g.dma_start(out=xT[:, dj, 0:1],
                            in_=xhl[0:1, dj * 128:(dj + 1) * 128].rearrange("o (p u) -> (o p) u", u=1)).then_inc(S["sLD0"], 16)
                g.dma_start(out=xT[:, dj, T + 1:T + 2],
                            in_=xhr[0:1, dj * 128:(dj + 1) * 128].rearrange("o (p u) -> (o p) u", u=1)).then_inc(S["sLD0"], 16)

        @blk.vector
        def _(v):
            v.wait_ge(S["sLD0"], 16 * N_LD0)
            v.tensor_scalar_add(w1p1[:], cwc[:, NK:2 * NK], 1.0).then_inc(S["sVE0"], 1)

    es_p1 = ExitStack()
    xstg = sb(es_p1, "xstg", [128, 2, 4 * D], BF16)
    ps_x = [psum(es_p1, f"ps_x{i}", [128, 512], BF16) for i in range(2)]

    with nc.Block() as blk:
        @blk.gpsimd
        def _(g):
            for b in range(NTB):
                par = b % 2
                if b >= 2:
                    g.wait_ge(S["sPE1"], (b - 1) * 16)
                sx = S["sX1a"] if par == 0 else S["sX1b"]
                for j in range(4):
                    g.dma_start(out=xstg[:, par, j * D:(j + 1) * D],
                                in_=xc[(b * 4 + j) * 128:(b * 4 + j + 1) * 128, :]).then_inc(sx, 16)

        @blk.tensor
        def _(t):
            t.wait_ge(S["sLD0"], 16 * N_LD0)
            n = 0
            for b in range(NTB):
                par = b % 2
                t.wait_ge(S["sX1a"] if b % 2 == 0 else S["sX1b"], 64 * (b // 2 + 1))
                for dj in range(NK):
                    pp = n % 2
                    if n >= 2:
                        t.wait_ge(S["sVE1"], n - 1)
                    for j in range(4):
                        ins = t.transpose(ps_x[pp][:, j * 128:(j + 1) * 128],
                                          xstg[:, par, j * D + dj * 128: j * D + dj * 128 + 128],
                                          idb[:])
                    ins.then_inc(S["sPE1"], 1)
                    n += 1

        @blk.vector
        def _(v):
            n = 0
            for b in range(NTB):
                for dj in range(NK):
                    pp = n % 2
                    v.wait_ge(S["sPE1"], n + 1)
                    v.tensor_copy(xT[:, dj, 1 + b * 512: 1 + b * 512 + 512],
                                  ps_x[pp][:]).then_inc(S["sVE1"], 1)
                    n += 1

    es_p1.close()

    es_b = ExitStack()
    hT = sb(es_b, "hT", [128, NDH, T], BF16)
    wg2 = sb(es_b, "wg2", [128, NDH, R], BF16)
    es_p3 = ExitStack()
    wg1 = sb(es_p3, "wg1", [128, NK, DH], BF16)
    ps_h = [psum(es_p3, f"ps_h{i}", [128, T], F32) for i in range(2)]

    es_p2 = ExitStack()
    cxstg = sb(es_p2, "cxstg", [128, 2, T], BF16)
    tmpa = sb(es_p2, "tmpa", [128, 512], F32)
    tmpb = sb(es_p2, "tmpb", [128, 512], F32)

    with nc.Block() as blk:
        @blk.vector
        def _(v):
            v.wait_ge(S["sVE0"], 1)
            for dj in range(NK):
                par = dj % 2
                if dj >= 2:
                    v.wait_ge(S["sCX2a"] if dj % 2 == 0 else S["sCX2b"], 16 * (dj // 2))
                for tb in range(NTB):
                    x_m1 = xT[:, dj, tb * 512: tb * 512 + 512]
                    x_0 = xT[:, dj, 1 + tb * 512: 1 + tb * 512 + 512]
                    x_p1 = xT[:, dj, 2 + tb * 512: 2 + tb * 512 + 512]
                    v.tensor_scalar(out=tmpa[:], in0=x_p1,
                                    scalar1=cwc[:, 2 * NK + dj:2 * NK + dj + 1],
                                    scalar2=cbc[:, dj:dj + 1], op0=ALU.mult, op1=ALU.add)
                    v.scalar_tensor_tensor(out=tmpb[:], in0=x_m1, scalar=cwc[:, dj:dj + 1],
                                           in1=tmpa[:], op0=ALU.mult, op1=ALU.add)
                    ins = v.scalar_tensor_tensor(out=cxstg[:, par, tb * 512:(tb + 1) * 512], in0=x_0,
                                                 scalar=w1p1[:, dj:dj + 1], in1=tmpb[:],
                                                 op0=ALU.mult, op1=ALU.add)
                ins.then_inc(S["sVE2"], 1)

        @blk.sync
        def _(sy):
            for dj in range(NK):
                par = dj % 2
                sy.wait_ge(S["sVE2"], dj + 1)
                sy.dma_start(out=cxh[dj * 128:(dj + 1) * 128, :],
                             in_=cxstg[:, par, :]).then_inc(S["sCX2a"] if par == 0 else S["sCX2b"], 16)

    es_p2.close()

    with nc.Block() as blk:
        @blk.gpsimd
        def _(g):
            g.wait_ge(S["sPE1"], 64)
            g.dma_start(out=wg1[:], in_=wg1h).then_inc(S["sW3"], 16)
            g.dma_start(out=wg2[:], in_=wg2h).then_inc(S["sW3"], 16)

        @blk.tensor
        def _(t):
            t.wait_ge(S["sW3"], 32)
            t.wait_ge(S["sVE1"], 64)
            for m in range(NDH):
                pp = m % 2
                if m >= 2:
                    t.wait_ge(S["sSC3"], (m - 1) * 4)
                for k in range(NK):
                    for tb in range(NTB):
                        ins = t.matmul(ps_h[pp][:, tb * 512:(tb + 1) * 512],
                                       wg1[:, k, m * 128:(m + 1) * 128],
                                       xT[:, k, 1 + tb * 512: 1 + tb * 512 + 512],
                                       start=(k == 0), stop=(k == NK - 1))
                ins.then_inc(S["sPE3"], 1)

        @blk.scalar
        def _(s):
            for m in range(NDH):
                pp = m % 2
                s.wait_ge(S["sPE3"], m + 1)
                for tb in range(NTB):
                    s.activation(hT[:, m, tb * 512:(tb + 1) * 512],
                                 ps_h[pp][:, tb * 512:(tb + 1) * 512],
                                 AF.Relu, bias=bg1c[:, m:m + 1]).then_inc(S["sSC3"], 1)

    es_p3.close()

    es_p4 = ExitStack()
    wu_stg = sb(es_p4, "wu_stg", [128, 2, D], BF16)
    wv_stg = sb(es_p4, "wv_stg", [128, 2, D], BF16)
    grow = sb(es_p4, "grow", [128, 2, T], BF16)
    ugrow = sb(es_p4, "ugrow", [128, T], BF16)
    vgrow = sb(es_p4, "vgrow", [128, 2, T], BF16)
    ps_g = psum(es_p4, "ps_g", [128, T], F32)
    ps_u = psum(es_p4, "ps_u", [128, T], F32)

    def p4_gpsimd(g):
        g.wait_ge(S["sPE3"], 4)
        for rc in range(NK):
            par = rc % 2
            if rc >= 2:
                g.wait_ge(S["sPE4"], 3 * (rc - 2) + 3)
            sw = S["sWU4a"] if par == 0 else S["sWU4b"]
            g.dma_start(out=wu_stg[:, par, :], in_=wuch[rc]).then_inc(sw, 16)
            g.dma_start(out=wv_stg[:, par, :], in_=wvch[rc]).then_inc(sw, 16)

    def p4_tensor(t):
        t.wait_ge(S["sSC3"], 16)
        for rc in range(NK):
            par = rc % 2
            if rc >= 1:
                t.wait_ge(S["sVE4"], 3 * (rc - 1) + 3)
            for k in range(NDH):
                for tb in range(NTB):
                    ins = t.matmul(ps_g[:, tb * 512:(tb + 1) * 512],
                                   wg2[:, k, rc * 128:(rc + 1) * 128],
                                   hT[:, k, tb * 512:(tb + 1) * 512],
                                   start=(k == 0), stop=(k == NDH - 1))
            ins.then_inc(S["sPE4"], 1)
            t.wait_ge(S["sWU4a"] if rc % 2 == 0 else S["sWU4b"], 32 * (rc // 2 + 1))
            if rc >= 1:
                t.wait_ge(S["sVE4"], 3 * (rc - 1) + 1)
            for k in range(NK):
                for tb in range(NTB):
                    ins = t.matmul(ps_u[:, tb * 512:(tb + 1) * 512],
                                   wu_stg[:, par, k * 128:(k + 1) * 128],
                                   xT[:, k, 1 + tb * 512: 1 + tb * 512 + 512],
                                   start=(k == 0), stop=(k == NK - 1))
            ins.then_inc(S["sPE4"], 1)
            t.wait_ge(S["sSC4"], rc + 1)
            for k in range(NK):
                for tb in range(NTB):
                    ins = t.matmul(ps_g[:, tb * 512:(tb + 1) * 512],
                                   wv_stg[:, par, k * 128:(k + 1) * 128],
                                   xT[:, k, 1 + tb * 512: 1 + tb * 512 + 512],
                                   start=(k == 0), stop=(k == NK - 1))
            ins.then_inc(S["sPE4"], 1)

    def p4_scalar(s):
        s.wait_ge(S["sLD0"], 16 * N_LD0)
        for rc in range(NK):
            par = rc % 2
            s.wait_ge(S["sPE4"], 3 * rc + 1)
            if rc >= 2:
                s.wait_ge(S["sVE4"], 3 * (rc - 2) + 3)
            for tb in range(NTB):
                ins = s.activation(grow[:, par, tb * 512:(tb + 1) * 512],
                                   ps_g[:, tb * 512:(tb + 1) * 512],
                                   AF.Sigmoid, bias=bg2c[:, rc:rc + 1])
            ins.then_inc(S["sSC4"], 1)

    def p4_vector(v):
        v.wait_ge(S["sLD0"], 16 * N_LD0)
        for rc in range(NK):
            par = rc % 2
            v.wait_ge(S["sPE4"], 3 * rc + 2)
            v.wait_ge(S["sSC4"], rc + 1)
            for tb in range(NTB):
                ins = v.scalar_tensor_tensor(out=ugrow[:, tb * 512:(tb + 1) * 512],
                                             in0=ps_u[:, tb * 512:(tb + 1) * 512],
                                             scalar=buc[:, rc:rc + 1],
                                             in1=grow[:, par, tb * 512:(tb + 1) * 512],
                                             op0=ALU.add, op1=ALU.mult,
                                             accum_out=S4[:, rc * 4 + tb: rc * 4 + tb + 1])
            ins.then_inc(S["sVE4"], 1)
            v.tensor_tensor_scan(out=cs[:, rc, :], data0=ugrow[:], data1=ugrow[:],
                                 initial=0.0, op0=ALU.add, op1=ALU.bypass).then_inc(S["sVE4"], 1)
            v.wait_ge(S["sPE4"], 3 * rc + 3)
            if rc >= 2:
                v.wait_ge(S["sVG4a"] if rc % 2 == 0 else S["sVG4b"], 16 * (rc // 2))
            for tb in range(NTB):
                ins = v.scalar_tensor_tensor(out=vgrow[:, par, tb * 512:(tb + 1) * 512],
                                             in0=ps_g[:, tb * 512:(tb + 1) * 512],
                                             scalar=bvc[:, rc:rc + 1],
                                             in1=grow[:, par, tb * 512:(tb + 1) * 512],
                                             op0=ALU.add, op1=ALU.mult)
            ins.then_inc(S["sVE4"], 1)
        v.reduce_sum(Sboth[:, 0:16], S4[:].rearrange("p (rc tb) -> p rc tb", tb=4),
                     axis=AX.X).then_inc(S["sVE4"], 1)

    def p4_sync(sy):
        for rc in range(NK):
            par = rc % 2
            sy.wait_ge(S["sVE4"], 3 * rc + 3)
            sy.dma_start(out=vgh[rc * 128:(rc + 1) * 128, :],
                         in_=vgrow[:, par, :]).then_inc(S["sVG4a"] if par == 0 else S["sVG4b"], 16)

    with nc.Block() as blk:
        blk.gpsimd(p4_gpsimd)
        blk.tensor(p4_tensor)
        blk.scalar(p4_scalar)
        blk.vector(p4_vector)
        blk.sync(p4_sync)

    es_p4.close()

    es_ra1 = ExitStack()
    xstgr = sb(es_ra1, "xstgr", [128, 2, 4 * D], BF16)
    ps_xr = [psum(es_ra1, f"ps_xr{i}", [128, 512], BF16) for i in range(2)]

    with nc.Block() as blk:
        @blk.gpsimd
        def _(g):
            g.wait_ge(S["sPE4"], 48)
            g.wait_ge(S["sVE4"], 49)
            g.wait_ge(S["sSC4"], 16)
            g.wait_ge(S["sVG4a"], 128)
            g.wait_ge(S["sVG4b"], 128)
            for b in range(NTB):
                par = b % 2
                if b >= 2:
                    g.wait_ge(S["sPER_t"], (b - 1) * 16)
                sx = S["sXRa"] if par == 0 else S["sXRb"]
                for j in range(4):
                    g.dma_start(out=xstgr[:, par, j * D:(j + 1) * D],
                                in_=xrep[(b * 4 + j) * 128:(b * 4 + j + 1) * 128, :]).then_inc(sx, 16)

        @blk.tensor
        def _(t):
            t.wait_ge(S["sVE4"], 49)
            n = 0
            for b in range(NTB):
                par = b % 2
                t.wait_ge(S["sXRa"] if par == 0 else S["sXRb"], 64 * (b // 2 + 1))
                for dj in range(NK):
                    pp = n % 2
                    if n >= 2:
                        t.wait_ge(S["sVER_t"], n - 1)
                    for j in range(4):
                        ins = t.transpose(ps_xr[pp][:, j * 128:(j + 1) * 128],
                                          xstgr[:, par, j * D + dj * 128: j * D + dj * 128 + 128],
                                          idb[:])
                    ins.then_inc(S["sPER_t"], 1)
                    n += 1

        @blk.vector
        def _(v):
            n = 0
            for b in range(NTB):
                for dj in range(NK):
                    pp = n % 2
                    v.wait_ge(S["sPER_t"], n + 1)
                    v.tensor_copy(xT[:, dj, 1 + b * 512: 1 + b * 512 + 512],
                                  ps_xr[pp][:]).then_inc(S["sVER_t"], 1)
                    n += 1

    es_ra1.close()

    es_ra2 = ExitStack()
    wg1r = sb(es_ra2, "wg1r", [128, NK, DH], BF16)
    ps_hr = [psum(es_ra2, f"ps_hr{i}", [128, T], F32) for i in range(2)]

    with nc.Block() as blk:
        @blk.gpsimd
        def _(g):
            g.wait_ge(S["sPER_t"], 64)
            g.dma_start(out=wg1r[:], in_=wg1h).then_inc(S["sWR1"], 16)

        @blk.tensor
        def _(t):
            t.wait_ge(S["sWR1"], 16)
            t.wait_ge(S["sVER_t"], 64)
            for m in range(NDH):
                pp = m % 2
                if m >= 2:
                    t.wait_ge(S["sSCR"], m - 1)
                for k in range(NK):
                    for tb in range(NTB):
                        ins = t.matmul(ps_hr[pp][:, tb * 512:(tb + 1) * 512],
                                       wg1r[:, k, m * 128:(m + 1) * 128],
                                       xT[:, k, 1 + tb * 512: 1 + tb * 512 + 512],
                                       start=(k == 0), stop=(k == NK - 1))
                ins.then_inc(S["sPER"], 1)

        @blk.scalar
        def _(s):
            s.wait_ge(S["sPE4"], 48)
            for m in range(NDH):
                pp = m % 2
                s.wait_ge(S["sPER"], m + 1)
                for tb in range(NTB):
                    ins = s.activation(hT[:, m, tb * 512:(tb + 1) * 512],
                                       ps_hr[pp][:, tb * 512:(tb + 1) * 512],
                                       AF.Relu, bias=bg1c[:, m:m + 1])
                ins.then_inc(S["sSCR"], 1)

    es_ra2.close()

    es_rb = ExitStack()
    wur_stg = sb(es_rb, "wur_stg", [128, 2, D], BF16)
    growr = sb(es_rb, "growr", [128, 2, T], BF16)
    ugrowr = sb(es_rb, "ugrowr", [128, T], BF16)
    ps_gr = psum(es_rb, "ps_gr", [128, T], F32)
    ps_ur = psum(es_rb, "ps_ur", [128, T], F32)

    with nc.Block() as blk:
        @blk.gpsimd
        def _(g):
            g.wait_ge(S["sPER"], 4)
            for rc in range(NK):
                par = rc % 2
                if rc >= 2:
                    g.wait_ge(S["sPER"], 4 + 2 * (rc - 2) + 2)
                g.dma_start(out=wur_stg[:, par, :],
                            in_=wuch[rc]).then_inc(S["sWRa"] if par == 0 else S["sWRb"], 16)

        @blk.tensor
        def _(t):
            t.wait_ge(S["sSCR"], 4)
            for rc in range(NK):
                par = rc % 2
                if rc >= 1:
                    t.wait_ge(S["sSCR"], 4 + rc)
                for k in range(NDH):
                    for tb in range(NTB):
                        ins = t.matmul(ps_gr[:, tb * 512:(tb + 1) * 512],
                                       wg2[:, k, rc * 128:(rc + 1) * 128],
                                       hT[:, k, tb * 512:(tb + 1) * 512],
                                       start=(k == 0), stop=(k == NDH - 1))
                ins.then_inc(S["sPER"], 1)
                t.wait_ge(S["sWRa"] if rc % 2 == 0 else S["sWRb"], 16 * (rc // 2 + 1))
                if rc >= 1:
                    t.wait_ge(S["sVER"], rc)
                for k in range(NK):
                    for tb in range(NTB):
                        ins = t.matmul(ps_ur[:, tb * 512:(tb + 1) * 512],
                                       wur_stg[:, par, k * 128:(k + 1) * 128],
                                       xT[:, k, 1 + tb * 512: 1 + tb * 512 + 512],
                                       start=(k == 0), stop=(k == NK - 1))
                ins.then_inc(S["sPER"], 1)

        @blk.scalar
        def _(s):
            for rc in range(NK):
                par = rc % 2
                s.wait_ge(S["sPER"], 4 + 2 * rc + 1)
                if rc >= 2:
                    s.wait_ge(S["sVER"], rc - 1)
                for tb in range(NTB):
                    ins = s.activation(growr[:, par, tb * 512:(tb + 1) * 512],
                                       ps_gr[:, tb * 512:(tb + 1) * 512],
                                       AF.Sigmoid, bias=bg2c[:, rc:rc + 1])
                ins.then_inc(S["sSCR"], 1)

        @blk.vector
        def _(v):
            for rc in range(NK):
                par = rc % 2
                v.wait_ge(S["sPER"], 4 + 2 * rc + 2)
                v.wait_ge(S["sSCR"], 4 + rc + 1)
                for tb in range(NTB):
                    ins = v.scalar_tensor_tensor(out=ugrowr[:, tb * 512:(tb + 1) * 512],
                                                 in0=ps_ur[:, tb * 512:(tb + 1) * 512],
                                                 scalar=buc[:, rc:rc + 1],
                                                 in1=growr[:, par, tb * 512:(tb + 1) * 512],
                                                 op0=ALU.add, op1=ALU.mult,
                                                 accum_out=S4r[:, rc * 4 + tb: rc * 4 + tb + 1])
                ins.then_inc(S["sVER"], 1)
            v.reduce_sum(Sboth[:, 16:32], S4r[:].rearrange("p (rc tb) -> p rc tb", tb=4),
                         axis=AX.X).then_inc(S["sVER"], 1)

    es_rb.close()
    es_b.close()
    es_xT.close()

    es_56 = ExitStack()
    wur = sb(es_56, "wur", [128, NK, D], BF16)
    vgld = sb(es_56, "vgld", [128, 2, T], BF16)
    cxt_stg = sb(es_56, "cxt_stg", [128, 2, D], BF16, is_transpose_dest=True)
    z1t = sb(es_56, "z1t", [128, D], F32)
    gotmp = sb(es_56, "gotmp", [128, T], BF16)
    zsq = sb(es_56, "zsq", [128, D], BF16)
    y1stg = sb(es_56, "y1stg", [128, 2, D], BF16)
    ps_uv = [psum(es_56, f"ps_uv{i}", [128, D], F32) for i in range(2)]

    with nc.Block() as blk:
        @blk.gpsimd
        def _(g):
            g.wait_ge(S["sLD0"], 16 * N_LD0)
            g.wait_ge(S["sPER"], 36)
            g.wait_ge(S["sSCR"], 20)
            g.wait_ge(S["sVER"], 17)
            g.dma_start(out=wur[:], in_=wurh).then_inc(S["sWUR"], 16)
            g.indirect_dma_start(out=shS, out_offset=IndirectOffsetOnAxis(widx[:], 0),
                                 in_=Sboth[:], in_offset=None).then_inc(S["sEXw"], 16)
            g.wait_ge(S["sEXw"], 16)
            g.indirect_dma_start(out=shF, out_offset=IndirectOffsetOnAxis(widx[:], 0),
                                 in_=tagc[:], in_offset=None).then_inc(S["sEXw"], 16)
            with g.register("nd") as nd, g.register("tgt") as tgt, g.register("f") as f:
                tv = g.value_load(tagc[0:1, 0:1])
                g.reg_mov(nd, 1)
                g.reg_mov(tgt, 0)
                with g.While(nd):
                    g.reg_add(tgt, tgt, 16)
                    g.indirect_dma_start(out=flg[:], out_offset=None,
                                         in_=shF,
                                         in_offset=IndirectOffsetOnAxis(ridx[:], 0)).then_inc(S["sPL"], 16)
                    g.wait_ge(S["sPL"], tgt)
                    fv = g.value_load(flg[0:1, 0:1])
                    g.reg_alu(f, fv, tv, ALU.is_equal)
                    g.reg_alu(nd, f, 1, ALU.bitwise_xor)
            g.indirect_dma_start(out=Pboth[:], out_offset=None,
                                 in_=shS,
                                 in_offset=IndirectOffsetOnAxis(ridx[:], 0)).then_inc(S["sEXr"], 16)

        @blk.vector
        def _(v):
            v.wait_ge(S["sEXr"], 16)
            v.tensor_scalar_mul(Pcol[:], Sboth[:, 16:32], amc[:, 0:1])
            v.scalar_tensor_tensor(out=Pcol[:], in0=Pboth[:, 16:32], scalar=amc[:, 1:2],
                                   in1=Pcol[:], op0=ALU.mult, op1=ALU.add)
            v.scalar_tensor_tensor(out=Pcol[:], in0=Pboth[:, 0:16], scalar=amc[:, 2:3],
                                   in1=Pcol[:], op0=ALU.mult, op1=ALU.add).then_inc(S["sVEX"], 1)

    with nc.Block() as blk:
        @blk.sync
        def _(sy):
            sy.wait_ge(S["sPER"], 36)
            sy.wait_ge(S["sSCR"], 20)
            sy.wait_ge(S["sVER"], 17)
            for rc in range(NK):
                par = rc % 2
                if rc >= 2:
                    sy.wait_ge(S["sVE5"], rc - 1)
                sy.dma_start(out=vgld[:, par, :],
                             in_=vgh[rc * 128:(rc + 1) * 128, :]).then_inc(S["sVG5a"] if par == 0 else S["sVG5b"], 16)

        @blk.vector
        def _(v):
            v.wait_ge(S["sVEX"], 1)
            for rc in range(NK):
                par = rc % 2
                v.wait_ge(S["sVG5a"] if rc % 2 == 0 else S["sVG5b"], 16 * (rc // 2 + 1))
                v.scalar_tensor_tensor(out=gotmp[:], in0=cs[:, rc, :],
                                       scalar=Pcol[:, rc:rc + 1], in1=vgld[:, par, :],
                                       op0=ALU.add, op1=ALU.mult)
                v.tensor_copy(cs[:, rc, :], gotmp[:]).then_inc(S["sVE5"], 1)

    with nc.Block() as blk:
        @blk.sync
        def _(sy):
            for tt in range(NTT):
                par = tt % 2
                if tt >= 2:
                    sy.wait_ge(S["sVE6"], 4 * (tt - 2) + 1)
                sy.dma_start_transpose(cxt_stg[:, par, :],
                                       cxh[:, tt * 128:(tt + 1) * 128]).then_inc(S["sCXTa"] if par == 0 else S["sCXTb"], 16)
                if tt >= 2:
                    ts = tt - 2
                    sy.wait_ge(S["sVE6"], 4 * ts + 4)
                    sy.dma_start(out=y1h[ts * 128:(ts + 1) * 128, :],
                                 in_=y1stg[:, ts % 2, :]).then_inc(S["sY1a"] if ts % 2 == 0 else S["sY1b"], 16)
            for ts in (NTT - 2, NTT - 1):
                sy.wait_ge(S["sVE6"], 4 * ts + 4)
                sy.dma_start(out=y1h[ts * 128:(ts + 1) * 128, :],
                             in_=y1stg[:, ts % 2, :]).then_inc(S["sY1a"] if ts % 2 == 0 else S["sY1b"], 16)

        @blk.tensor
        def _(t):
            t.wait_ge(S["sVE5"], 16)
            t.wait_ge(S["sWUR"], 16)
            for tt in range(NTT):
                pp = tt % 2
                if tt >= 2:
                    t.wait_ge(S["sVE6"], 4 * (tt - 2) + 1)
                for k in range(NK):
                    for db in range(4):
                        ins = t.matmul(ps_uv[pp][:, db * 512:(db + 1) * 512],
                                       cs[:, k, tt * 128:(tt + 1) * 128],
                                       wur[:, k, db * 512:(db + 1) * 512],
                                       start=(k == 0), stop=(k == NK - 1))
                ins.then_inc(S["sPE6"], 1)

        @blk.vector
        def _(v):
            for tt in range(NTT):
                pp = tt % 2
                v.wait_ge(S["sPE6"], tt + 1)
                v.wait_ge(S["sCXTa"] if tt % 2 == 0 else S["sCXTb"], 16 * (tt // 2 + 1))
                if tt >= 1:
                    v.wait_ge(S["sSC6"], 3 * (tt - 1) + 1)
                v.scalar_tensor_tensor(out=z1t[:], in0=ps_uv[pp][:], scalar=0.0,
                                       in1=cxt_stg[:, pp, :], op0=ALU.add,
                                       op1=ALU.add).then_inc(S["sVE6"], 1)
                v.wait_ge(S["sSC6"], 3 * tt + 1)
                v.tensor_scalar_mul(rowP[:], rowmS[:], rowmS[:])
                v.tensor_scalar_add(rowQ[:], rowtS2[:], EPS).then_inc(S["sVE6"], 1)
                v.wait_ge(S["sSC6"], 3 * tt + 2)
                v.tensor_scalar_mul(rowH[:], rowL[:], -0.5).then_inc(S["sVE6"], 1)
                v.wait_ge(S["sSC6"], 3 * tt + 3)
                if tt >= 2:
                    v.wait_ge(S["sY1a"] if tt % 2 == 0 else S["sY1b"], 16 * (tt // 2))
                v.tensor_scalar(out=y1stg[:, pp, :], in0=z1t[:], scalar1=rowmS[:],
                                scalar2=rowRS[:], op0=ALU.subtract, op1=ALU.mult).then_inc(S["sVE6"], 1)

        @blk.scalar
        def _(s):
            s.wait_ge(S["sPER"], 36)
            for tt in range(NTT):
                s.wait_ge(S["sVE6"], 4 * tt + 1)
                s.activation(zsq[:], z1t[:], AF.Square, accum_out=rowsq[:])
                s.activation(zsq[:], z1t[:], AF.Copy, accum_out=rowsumS[:])
                s.mul(rowmS[:], rowsumS[:], 1.0 / D)
                s.mul(rowtS2[:], rowsq[:], 1.0 / D).then_inc(S["sSC6"], 1)
                s.wait_ge(S["sVE6"], 4 * tt + 2)
                s.activation(rowL[:], rowP[:], AF.Ln, scale=-1.0, bias=rowQ[:]).then_inc(S["sSC6"], 1)
                s.wait_ge(S["sVE6"], 4 * tt + 3)
                s.activation(rowRS[:], rowH[:], AF.Exp).then_inc(S["sSC6"], 1)

    es_56.close()
    es_cs.close()

    es_p7 = ExitStack()
    y1f = sb(es_p7, "y1f", [128, NK, 1024], BF16, is_transpose_dest=True)
    hdn = sb(es_p7, "hdn", [128, NIC, 1024], BF16)
    wgf = sb(es_p7, "wgf", [128, 2, D], BF16)
    wuf = sb(es_p7, "wuf", [128, 2, D], BF16)
    wdf = sb(es_p7, "wdf", [128, 2, II], BF16)
    ggs = sb(es_p7, "ggs", [128, 1024], BF16)
    uus = sb(es_p7, "uus", [128, 1024], BF16)
    z2s = sb(es_p7, "z2s", [128, 2, 1024], BF16)
    z2t = sb(es_p7, "z2t", [128, 2, D], BF16, is_transpose_dest=True)
    zsq7 = sb(es_p7, "zsq7", [128, D], BF16)
    y2s = sb(es_p7, "y2s", [128, 2, D], F32)
    ps_a = [psum(es_p7, f"ps_a{i}", [128, 1024], F32) for i in range(2)]
    ps_b = [psum(es_p7, f"ps_b{i}", [128, 1024], F32) for i in range(2)]

    PEH = NIC + NK

    def p7_gpsimd(g):
        g.wait_ge(S["sPE6"], 16)
        g.wait_ge(S["sVE6"], 64)
        g.wait_ge(S["sSC6"], 48)
        g.wait_ge(S["sY1a"], 128)
        g.wait_ge(S["sY1b"], 128)
        for h in range(2):
            for ic in range(NIC):
                par = ic % 2
                n = h * NIC + ic
                if n >= 2:
                    hp, icp = divmod(n - 2, NIC)
                    g.wait_ge(S["sPE7"], PEH * hp + icp + 1)
                swg = S["sW7ga"] if par == 0 else S["sW7gb"]
                g.dma_start(out=wgf[:, par, :], in_=wgch[ic]).then_inc(swg, 16)
                g.dma_start(out=wuf[:, par, :], in_=wupch[ic]).then_inc(swg, 16)
            for dm in range(NK):
                par = dm % 2
                n = h * NK + dm
                if n >= 2:
                    hp, dmp = divmod(n - 2, NK)
                    g.wait_ge(S["sPE7"], PEH * hp + NIC + dmp + 1)
                g.dma_start(out=wdf[:, par, :],
                            in_=wdch[dm]).then_inc(S["sW7da"] if par == 0 else S["sW7db"], 16)

    def p7_tensor(t):
        for h in range(2):
            base = PEH * h
            t.wait_ge(S["sY1F"], 256 * (h + 1))
            if h == 1:
                t.wait_ge(S["sVE7"], PEH)
            for ic in range(NIC):
                par = ic % 2
                if ic >= 2:
                    t.wait_ge(S["sVE7"], base + (ic - 2) + 1)
                nw = h * NIC + ic
                t.wait_ge(S["sW7ga"] if ic % 2 == 0 else S["sW7gb"], 32 * (nw // 2 + 1))
                for k in range(NK):
                    for tb in range(2):
                        t.matmul(ps_a[par][:, tb * 512:(tb + 1) * 512],
                                 wgf[:, par, k * 128:(k + 1) * 128],
                                 y1f[:, k, tb * 512:(tb + 1) * 512],
                                 start=(k == 0), stop=(k == NK - 1))
                for k in range(NK):
                    for tb in range(2):
                        ins = t.matmul(ps_b[par][:, tb * 512:(tb + 1) * 512],
                                       wuf[:, par, k * 128:(k + 1) * 128],
                                       y1f[:, k, tb * 512:(tb + 1) * 512],
                                       start=(k == 0), stop=(k == NK - 1))
                ins.then_inc(S["sPE7"], 1)
            t.wait_ge(S["sVE7"], base + NIC)
            for dm in range(NK):
                par = dm % 2
                if dm >= 2:
                    t.wait_ge(S["sVE7"], base + NIC + (dm - 2) + 1)
                mw = h * NK + dm
                t.wait_ge(S["sW7da"] if dm % 2 == 0 else S["sW7db"], 16 * (mw // 2 + 1))
                for ki in range(NIC):
                    for tb in range(2):
                        ins = t.matmul(ps_a[par][:, tb * 512:(tb + 1) * 512],
                                       wdf[:, par, ki * 128:(ki + 1) * 128],
                                       hdn[:, ki, tb * 512:(tb + 1) * 512],
                                       start=(ki == 0), stop=(ki == NIC - 1))
                ins.then_inc(S["sPE7"], 1)

    def p7_scalar(s):
        s.wait_ge(S["sVE6"], 64)
        s.wait_ge(S["sLD0"], 16 * N_LD0)
        for h in range(2):
            base = PEH * h
            for ic in range(NIC):
                par = ic % 2
                s.wait_ge(S["sPE7"], base + ic + 1)
                if h * NIC + ic >= 1:
                    s.wait_ge(S["sVE7"], base + ic if ic >= 1 else PEH * h)
                s.activation(ggs[:], ps_a[par][:], AF.Silu,
                             scale=gsc[:, ic:ic + 1]).then_inc(S["sSC7"], 1)
            for t8 in range(8):
                tt = h * 8 + t8
                s.wait_ge(S["sZ2Ta"] if tt % 2 == 0 else S["sZ2Tb"], 16 * (tt // 2 + 1))
                if tt >= 1:
                    s.wait_ge(S["sVE7L"], 3 * (tt - 1) + 3)
                s.activation(zsq7[:], z2t[:, tt % 2, :], AF.Square, accum_out=rowsq[:])
                s.activation(zsq7[:], z2t[:, tt % 2, :], AF.Copy, accum_out=rowsumS[:])
                s.mul(rowmS[:], rowsumS[:], 1.0 / D)
                s.mul(rowtS2[:], rowsq[:], 1.0 / D).then_inc(S["sSC7L"], 1)
                s.wait_ge(S["sVE7L"], 3 * tt + 1)
                s.activation(rowL[:], rowP[:], AF.Ln, scale=-1.0, bias=rowQ[:]).then_inc(S["sSC7L"], 1)
                s.wait_ge(S["sVE7L"], 3 * tt + 2)
                s.activation(rowRS[:], rowH[:], AF.Exp).then_inc(S["sSC7L"], 1)

    def p7_vector(v):
        v.wait_ge(S["sLD0"], 16 * N_LD0)
        for h in range(2):
            base = PEH * h
            for ic in range(NIC):
                par = ic % 2
                v.wait_ge(S["sPE7"], base + ic + 1)
                v.wait_ge(S["sSC7"], h * NIC + ic + 1)
                v.tensor_scalar_mul(uus[:], ps_b[par][:], usc[:, ic:ic + 1])
                v.tensor_tensor(out=hdn[:, ic, :], in0=ggs[:], in1=uus[:],
                                op=ALU.mult).then_inc(S["sVE7"], 1)
            for dm in range(NK):
                par = dm % 2
                v.wait_ge(S["sPE7"], base + NIC + dm + 1)
                if dm >= 2 or h == 1:
                    mz = h * NK + dm
                    v.wait_ge(S["sZ2a"] if dm % 2 == 0 else S["sZ2b"], 16 * (mz // 2))
                v.scalar_tensor_tensor(out=z2s[:, dm % 2, :], in0=ps_a[par][:],
                                       scalar=dsc[:, dm:dm + 1], in1=y1f[:, dm, :],
                                       op0=ALU.mult, op1=ALU.add).then_inc(S["sVE7"], 1)
            for t8 in range(8):
                tt = h * 8 + t8
                pp = tt % 2
                v.wait_ge(S["sSC7L"], 3 * tt + 1)
                v.tensor_scalar_mul(rowP[:], rowmS[:], rowmS[:])
                v.tensor_scalar_add(rowQ[:], rowtS2[:], EPS).then_inc(S["sVE7L"], 1)
                v.wait_ge(S["sSC7L"], 3 * tt + 2)
                v.tensor_scalar_mul(rowH[:], rowL[:], -0.5).then_inc(S["sVE7L"], 1)
                v.wait_ge(S["sSC7L"], 3 * tt + 3)
                if tt >= 2:
                    v.wait_ge(S["sOUTa"] if tt % 2 == 0 else S["sOUTb"], 16 * (tt // 2))
                v.tensor_scalar(out=y2s[:, pp, :], in0=z2t[:, pp, :], scalar1=rowmS[:],
                                scalar2=rowRS[:], op0=ALU.subtract, op1=ALU.mult).then_inc(S["sVE7L"], 1)

    def p7_sync(sy):
        sy.wait_ge(S["sVE6"], 64)
        sy.wait_ge(S["sSC6"], 48)
        sy.wait_ge(S["sY1a"], 128)
        sy.wait_ge(S["sY1b"], 128)
        for h in range(2):
            if h == 1:
                sy.wait_ge(S["sVE7"], PEH)
            for k in range(NK):
                sy.dma_start_transpose(y1f[:, k, :],
                                       y1h[h * 1024:(h + 1) * 1024, k * 128:(k + 1) * 128]).then_inc(S["sY1F"], 16)
            for dm in range(NK):
                sy.wait_ge(S["sVE7"], PEH * h + NIC + dm + 1)
                sy.dma_start(out=z2h[dm * 128:(dm + 1) * 128, h * 1024:(h + 1) * 1024],
                             in_=z2s[:, dm % 2, :]).then_inc(S["sZ2a"] if dm % 2 == 0 else S["sZ2b"], 16)
            for t8 in range(8):
                tt = h * 8 + t8
                pp = tt % 2
                sy.wait_ge(S["sZ2a"], 128 * (h + 1))
                sy.wait_ge(S["sZ2b"], 128 * (h + 1))
                if tt >= 2:
                    sy.wait_ge(S["sVE7L"], 3 * (tt - 2) + 3)
                    sy.wait_ge(S["sSC7L"], 3 * (tt - 2) + 1)
                sy.dma_start_transpose(z2t[:, pp, :],
                                       z2h[:, tt * 128:(tt + 1) * 128]).then_inc(S["sZ2Ta"] if pp == 0 else S["sZ2Tb"], 16)
                if tt >= 2:
                    ts = tt - 2
                    sy.wait_ge(S["sVE7L"], 3 * ts + 3)
                    sy.dma_start(out=y[ts * 128:(ts + 1) * 128, :],
                                 in_=y2s[:, ts % 2, :]).then_inc(S["sOUTa"] if ts % 2 == 0 else S["sOUTb"], 16)
        for ts in (NTT - 2, NTT - 1):
            sy.wait_ge(S["sVE7L"], 3 * ts + 3)
            sy.dma_start(out=y[ts * 128:(ts + 1) * 128, :],
                         in_=y2s[:, ts % 2, :]).then_inc(S["sOUTa"] if ts % 2 == 0 else S["sOUTb"], 16)
        sy.wait_ge(S["sOUTa"], 128)
        sy.wait_ge(S["sOUTb"], 128)

    with nc.Block() as blk:
        blk.gpsimd(p7_gpsimd)
        blk.tensor(p7_tensor)
        blk.scalar(p7_scalar)
        blk.vector(p7_vector)
        blk.sync(p7_sync)

    es_p7.close()
    es0.close()
    return nc


def _prep_host(inputs):
    x = np.ascontiguousarray(np.asarray(inputs["L0_attn_x"], dtype=np.float32).reshape(16384, D))
    Wu = np.asarray(inputs["L0_attn_Wu"], np.float32)
    Wv = np.asarray(inputs["L0_attn_Wv"], np.float32)
    Wg1 = np.asarray(inputs["L0_attn_Wg1"], np.float32)
    Wg2 = np.asarray(inputs["L0_attn_Wg2"], np.float32)
    bu = np.asarray(inputs["L0_attn_bu"], np.float32)
    bv = np.asarray(inputs["L0_attn_bv"], np.float32)
    bg1 = np.asarray(inputs["L0_attn_bg1"], np.float32)
    bg2 = np.asarray(inputs["L0_attn_bg2"], np.float32)
    conv_w = np.asarray(inputs["L0_attn_conv_w"], np.float32)
    conv_b = np.asarray(inputs["L0_attn_conv_b"], np.float32)
    gw = np.asarray(inputs["L0_ffn_gate_w"], np.float32)
    uw = np.asarray(inputs["L0_ffn_up_w"], np.float32)
    dw = np.asarray(inputs["L0_ffn_down_w"], np.float32)
    gs = np.asarray(inputs["L0_ffn_gate_s"], np.float32)
    us = np.asarray(inputs["L0_ffn_up_s"], np.float32)
    ds = np.asarray(inputs["L0_ffn_down_s"], np.float32)

    C = np.ascontiguousarray
    wg1h = C(Wg1.T.reshape(NK, 128, DH).transpose(1, 0, 2))
    wg2h = C(Wg2.T.reshape(NDH, 128, R).transpose(1, 0, 2))
    WuT = C(Wu.T)
    wuch = C(WuT.reshape(NK, 128, NK, 128).transpose(2, 1, 0, 3)).reshape(NK, 128, D)
    WvT = C(Wv.T)
    wvch = C(WvT.reshape(NK, 128, NK, 128).transpose(2, 1, 0, 3)).reshape(NK, 128, D)
    wurh = C(WuT.reshape(NK, 128, D).transpose(1, 0, 2))
    wgch = C(gw.T.reshape(NK, 128, NIC, 128).transpose(2, 1, 0, 3)).reshape(NIC, 128, D)
    wupch = C(uw.T.reshape(NK, 128, NIC, 128).transpose(2, 1, 0, 3)).reshape(NIC, 128, D)
    wdch = C(dw.T.reshape(NIC, 128, NK, 128).transpose(2, 1, 0, 3)).reshape(NK, 128, II)
    buh = C(bu.reshape(NK, 128).T)
    bvh = C(bv.reshape(NK, 128).T)
    bg1h = C(bg1.reshape(NDH, 128).T)
    bg2h = C(bg2.reshape(NK, 128).T)
    cwh = C(np.concatenate([conv_w[:, 0, j].reshape(NK, 128).T for j in range(3)], axis=1))
    cbh = C(conv_b.reshape(NK, 128).T)
    gsh = C(gs.reshape(NIC, 128).T)
    ush = C(us.reshape(NIC, 128).T)
    dsh = C(ds.reshape(NK, 128).T)
    identh = np.eye(128).astype(ml_dtypes.bfloat16)
    tag = np.int32(np.random.randint(1, 2 ** 31 - 1))
    tagh = np.full((128, 1), tag, np.int32)
    epsh = np.full((128, 1), EPS, np.float32)
    ar = np.arange(128, dtype=np.uint32).reshape(128, 1)

    shared = dict(wg1h=wg1h, wg2h=wg2h, wuch=wuch, wvch=wvch, wurh=wurh,
                  wgch=wgch, wupch=wupch, wdch=wdch, buh=buh, bvh=bvh,
                  bg1h=bg1h, bg2h=bg2h, cwh=cwh, cbh=cbh, gsh=gsh, ush=ush,
                  dsh=dsh, identh=identh, tagh=tagh, epsh=epsh)

    in_maps = []
    for c in range(8):
        lo = c * T
        xc_ = C(x[lo:lo + T])
        b0 = (c // 4) * 4 * T
        xhl_ = C(x[lo - 1:lo]) if (c % 4) != 0 else np.zeros((1, D), np.float32)
        xhr_ = C(x[lo + T:lo + T + 1]) if (c % 4) != 3 else np.zeros((1, D), np.float32)
        if c % 4 == 2:
            xrep_ = C(x[b0:b0 + T])
        elif c % 4 == 3:
            xrep_ = C(x[b0 + T:b0 + 2 * T])
        else:
            xrep_ = xc_
        m = c % 4
        a1 = 1.0 if m >= 2 else 0.0
        a2 = 1.0 if m >= 2 else 0.0
        a3 = 1.0 if m in (1, 3) else 0.0
        amh = np.tile(np.array([[a1, a2, a3]], np.float32), (128, 1))
        p = c & 1
        in_maps.append(dict(shared, xc=xc_, xrep=xrep_, xhl=xhl_, xhr=xhr_, amh=amh,
                            widxh=(p * 128 + ar), ridxh=((1 - p) * 128 + ar)))
    return in_maps


def kernel(**inputs):
    in_maps = _prep_host(inputs)
    nc = build_nc()
    res = run_bass_kernel_spmd(nc, in_maps, core_ids=list(range(8)))
    out = np.concatenate([res.results[c]["y"] for c in range(8)], axis=0)
    return np.ascontiguousarray(out.reshape(2, 8192, D).astype(np.float32))


# version 07
# speedup vs baseline: 1.6061x; 1.6061x over previous
import numpy as np
import ml_dtypes
import concourse.bass as bass
import concourse.mybir as mybir
from concourse.bass import IndirectOffsetOnAxis
from concourse.bass_utils import run_bass_kernel_spmd
from contextlib import ExitStack

F32 = mybir.dt.float32
BF16 = mybir.dt.bfloat16
I32 = mybir.dt.int32
U32 = mybir.dt.uint32
AF = mybir.ActivationFunctionType
ALU = mybir.AluOpType
AX = mybir.AxisListType

T = 2048
D = 2048
R = 2048
DH = 512
II = 5632
NK = 16
NDH = 4
NIC = 44
NTT = 16
NTB = 4
EPS = 1e-5


def build_nc(race_check=True):
    nc = bass.Bass("TRN2", target_bir_lowering=False, debug=False,
                   detect_race_conditions=race_check)

    xc = nc.dram_tensor("xc", [T, D], F32, kind="ExternalInput").ap()
    xrep = nc.dram_tensor("xrep", [T, D], F32, kind="ExternalInput").ap()
    xhl = nc.dram_tensor("xhl", [1, D], F32, kind="ExternalInput").ap()
    xhr = nc.dram_tensor("xhr", [1, D], F32, kind="ExternalInput").ap()
    wg1h = nc.dram_tensor("wg1h", [128, NK, DH], F32, kind="ExternalInput").ap()
    wg2h = nc.dram_tensor("wg2h", [128, NDH, R], F32, kind="ExternalInput").ap()
    wuch = nc.dram_tensor("wuch", [NK, 128, D], F32, kind="ExternalInput").ap()
    wvch = nc.dram_tensor("wvch", [NK, 128, D], F32, kind="ExternalInput").ap()
    wurh = nc.dram_tensor("wurh", [128, NK, D], F32, kind="ExternalInput").ap()
    wgch = nc.dram_tensor("wgch", [NIC, 128, D], F32, kind="ExternalInput").ap()
    wupch = nc.dram_tensor("wupch", [NIC, 128, D], F32, kind="ExternalInput").ap()
    wdch = nc.dram_tensor("wdch", [NK, 128, II], F32, kind="ExternalInput").ap()
    buh = nc.dram_tensor("buh", [128, NK], F32, kind="ExternalInput").ap()
    bvh = nc.dram_tensor("bvh", [128, NK], F32, kind="ExternalInput").ap()
    bg1h = nc.dram_tensor("bg1h", [128, NDH], F32, kind="ExternalInput").ap()
    bg2h = nc.dram_tensor("bg2h", [128, NK], F32, kind="ExternalInput").ap()
    cwh = nc.dram_tensor("cwh", [128, 3 * NK], F32, kind="ExternalInput").ap()
    cbh = nc.dram_tensor("cbh", [128, NK], F32, kind="ExternalInput").ap()
    gsh = nc.dram_tensor("gsh", [128, NIC], F32, kind="ExternalInput").ap()
    ush = nc.dram_tensor("ush", [128, NIC], F32, kind="ExternalInput").ap()
    dsh = nc.dram_tensor("dsh", [128, NK], F32, kind="ExternalInput").ap()
    identh = nc.dram_tensor("identh", [128, 128], BF16, kind="ExternalInput").ap()
    amh = nc.dram_tensor("amh", [128, 3], F32, kind="ExternalInput").ap()
    widxh = nc.dram_tensor("widxh", [128, 1], U32, kind="ExternalInput").ap()
    ridxh = nc.dram_tensor("ridxh", [128, 1], U32, kind="ExternalInput").ap()
    ridxah = nc.dram_tensor("ridxah", [128, 1], U32, kind="ExternalInput").ap()
    ridxbh = nc.dram_tensor("ridxbh", [128, 1], U32, kind="ExternalInput").ap()
    tagh = nc.dram_tensor("tagh", [128, 1], I32, kind="ExternalInput").ap()
    epsh = nc.dram_tensor("epsh", [128, 1], F32, kind="ExternalInput").ap()
    ndah = nc.dram_tensor("ndah", [1, 1], I32, kind="ExternalInput").ap()
    ndbh = nc.dram_tensor("ndbh", [1, 1], I32, kind="ExternalInput").ap()

    y = nc.dram_tensor("y", [T, D], F32, kind="ExternalOutput").ap()

    cxh = nc.dram_tensor("cxh", [D, T], BF16, kind="Internal").ap()
    vgh = nc.dram_tensor("vgh", [R, T], BF16, kind="Internal").ap()
    y1h = nc.dram_tensor("y1h", [T, D], BF16, kind="Internal").ap()
    z2h = nc.dram_tensor("z2h", [D, T], BF16, kind="Internal").ap()
    shSA = nc.dram_tensor("shSA", [256, 16], F32, kind="Internal", addr_space="Shared").ap()
    shSB = nc.dram_tensor("shSB", [256, 16], F32, kind="Internal", addr_space="Shared").ap()
    shFA = nc.dram_tensor("shFA", [256, 1], I32, kind="Internal", addr_space="Shared").ap()
    shFB = nc.dram_tensor("shFB", [256, 1], I32, kind="Internal", addr_space="Shared").ap()

    es0 = ExitStack()
    sb = lambda st, n, shape, dt, **kw: st.enter_context(nc.sbuf_tensor(n, shape, dt, **kw))
    psum = lambda st, n, shape, dt: st.enter_context(nc.psum_tensor(n, shape, dt))

    sem_names = [
        "sLD0", "sVE0",
        "sX1a", "sX1b", "sPE1", "sVE1",
        "sVE2", "sCX2a", "sCX2b",
        "sW3", "sPE3", "sSC3",
        "sWU4a", "sWU4b", "sPE4", "sSC4", "sVE4", "sVG4a", "sVG4b",
        "sXRa", "sXRb", "sPER_t", "sVER_t", "sWR1", "sWRa", "sWRb", "sPER", "sSCR", "sVER",
        "sEXw", "sPL", "sEXr", "sVEX",
        "sWUR", "sVG5a", "sVG5b", "sVE5",
        "sCXTa", "sCXTb", "sPE6", "sSC6", "sVE6", "sY1a", "sY1b",
        "sW7ga", "sW7gb", "sW7da", "sW7db", "sY1F", "sPE7", "sSC7", "sVE7",
        "sZ2a", "sZ2b", "sZ2Ta", "sZ2Tb", "sVE7L", "sSC7L", "sOUTa", "sOUTb",
    ]
    S = {n: es0.enter_context(nc.semaphore(n)) for n in sem_names}

    idb = sb(es0, "idb", [128, 128], BF16)
    buc = sb(es0, "buc", [128, NK], F32)
    bvc = sb(es0, "bvc", [128, NK], F32)
    bg1c = sb(es0, "bg1c", [128, NDH], F32)
    bg2c = sb(es0, "bg2c", [128, NK], F32)
    cwc = sb(es0, "cwc", [128, 3 * NK], F32)
    cbc = sb(es0, "cbc", [128, NK], F32)
    w1p1 = sb(es0, "w1p1", [128, NK], F32)
    gsc = sb(es0, "gsc", [128, NIC], F32)
    usc = sb(es0, "usc", [128, NIC], F32)
    dsc = sb(es0, "dsc", [128, NK], F32)
    amc = sb(es0, "amc", [128, 3], F32)
    widx = sb(es0, "widx", [128, 1], U32)
    ridx = sb(es0, "ridx", [128, 1], U32)
    ridxA = sb(es0, "ridxA", [128, 1], U32)
    ridxB = sb(es0, "ridxB", [128, 1], U32)
    tagc = sb(es0, "tagc", [128, 1], I32)
    Sboth = sb(es0, "Sboth", [128, 32], F32)
    Pboth = sb(es0, "Pboth", [128, 32], F32)
    Pcol = sb(es0, "Pcol", [128, NK], F32)
    S4 = sb(es0, "S4", [128, 64], F32)
    S4r = sb(es0, "S4r", [128, 64], F32)
    flg = sb(es0, "flg", [128, 1], I32)
    ndab = sb(es0, "ndab", [1, 2], I32)
    rowm = sb(es0, "rowm", [128, 1], F32)
    rowt = sb(es0, "rowt", [128, 1], F32)
    rowv = sb(es0, "rowv", [128, 1], F32)
    rowsd = sb(es0, "rowsd", [128, 1], F32)
    rowrs = sb(es0, "rowrs", [128, 1], F32)
    rowsum = sb(es0, "rowsum", [128, 1], F32)
    rowsq = sb(es0, "rowsq", [128, 1], F32)
    rowsumS = sb(es0, "rowsumS", [128, 1], F32)
    epsB = sb(es0, "epsB", [128, 1], F32)
    rowmS = sb(es0, "rowmS", [128, 1], F32)
    rowtS2 = sb(es0, "rowtS2", [128, 1], F32)
    rowP = sb(es0, "rowP", [128, 1], F32)
    rowQ = sb(es0, "rowQ", [128, 1], F32)
    rowL = sb(es0, "rowL", [128, 1], F32)
    rowH = sb(es0, "rowH", [128, 1], F32)
    rowRS = sb(es0, "rowRS", [128, 1], F32)

    N_LD0 = 19 + 32

    es_cs = ExitStack()
    cs = sb(es_cs, "cs", [128, NK, T], BF16)
    es_xT = ExitStack()
    xT = sb(es_xT, "xT", [128, NK, T + 2], BF16)

    with nc.Block() as blk:
        @blk.gpsimd
        def _(g):
            for dst, src in [(idb, identh), (buc, buh), (bvc, bvh), (bg1c, bg1h),
                             (bg2c, bg2h), (cwc, cwh), (cbc, cbh), (gsc, gsh),
                             (usc, ush), (dsc, dsh), (amc, amh), (widx, widxh),
                             (ridx, ridxh), (ridxA, ridxah), (ridxB, ridxbh),
                             (tagc, tagh), (epsB, epsh)]:
                g.dma_start(out=dst[:], in_=src).then_inc(S["sLD0"], 16)
            g.dma_start(out=ndab[:, 0:1], in_=ndah).then_inc(S["sLD0"], 16)
            g.dma_start(out=ndab[:, 1:2], in_=ndbh).then_inc(S["sLD0"], 16)
            for dj in range(NK):
                g.dma_start(out=xT[:, dj, 0:1],
                            in_=xhl[0:1, dj * 128:(dj + 1) * 128].rearrange("o (p u) -> (o p) u", u=1)).then_inc(S["sLD0"], 16)
                g.dma_start(out=xT[:, dj, T + 1:T + 2],
                            in_=xhr[0:1, dj * 128:(dj + 1) * 128].rearrange("o (p u) -> (o p) u", u=1)).then_inc(S["sLD0"], 16)

        @blk.vector
        def _(v):
            v.wait_ge(S["sLD0"], 16 * N_LD0)
            v.tensor_scalar_add(w1p1[:], cwc[:, NK:2 * NK], 1.0).then_inc(S["sVE0"], 1)

    es_p1 = ExitStack()
    xstg = sb(es_p1, "xstg", [128, 2, 4 * D], BF16)
    ps_x = [psum(es_p1, f"ps_x{i}", [128, 512], BF16) for i in range(2)]

    with nc.Block() as blk:
        @blk.gpsimd
        def _(g):
            for b in range(NTB):
                par = b % 2
                if b >= 2:
                    g.wait_ge(S["sPE1"], (b - 1) * 16)
                sx = S["sX1a"] if par == 0 else S["sX1b"]
                for j in range(4):
                    g.dma_start(out=xstg[:, par, j * D:(j + 1) * D],
                                in_=xc[(b * 4 + j) * 128:(b * 4 + j + 1) * 128, :]).then_inc(sx, 16)

        @blk.tensor
        def _(t):
            t.wait_ge(S["sLD0"], 16 * N_LD0)
            n = 0
            for b in range(NTB):
                par = b % 2
                t.wait_ge(S["sX1a"] if b % 2 == 0 else S["sX1b"], 64 * (b // 2 + 1))
                for dj in range(NK):
                    pp = n % 2
                    if n >= 2:
                        t.wait_ge(S["sVE1"], n - 1)
                    for j in range(4):
                        ins = t.transpose(ps_x[pp][:, j * 128:(j + 1) * 128],
                                          xstg[:, par, j * D + dj * 128: j * D + dj * 128 + 128],
                                          idb[:])
                    ins.then_inc(S["sPE1"], 1)
                    n += 1

        @blk.vector
        def _(v):
            n = 0
            for b in range(NTB):
                for dj in range(NK):
                    pp = n % 2
                    v.wait_ge(S["sPE1"], n + 1)
                    v.tensor_copy(xT[:, dj, 1 + b * 512: 1 + b * 512 + 512],
                                  ps_x[pp][:]).then_inc(S["sVE1"], 1)
                    n += 1

    es_p1.close()

    es_b = ExitStack()
    hT = sb(es_b, "hT", [128, NDH, T], BF16)
    wg2 = sb(es_b, "wg2", [128, NDH, R], BF16)
    es_p3 = ExitStack()
    wg1 = sb(es_p3, "wg1", [128, NK, DH], BF16)
    ps_h = [psum(es_p3, f"ps_h{i}", [128, T], F32) for i in range(2)]

    es_p2 = ExitStack()
    cxstg = sb(es_p2, "cxstg", [128, 2, T], BF16)
    tmpa = sb(es_p2, "tmpa", [128, 512], F32)
    tmpb = sb(es_p2, "tmpb", [128, 512], F32)

    with nc.Block() as blk:
        @blk.vector
        def _(v):
            v.wait_ge(S["sVE0"], 1)
            for dj in range(NK):
                par = dj % 2
                if dj >= 2:
                    v.wait_ge(S["sCX2a"] if dj % 2 == 0 else S["sCX2b"], 16 * (dj // 2))
                for tb in range(NTB):
                    x_m1 = xT[:, dj, tb * 512: tb * 512 + 512]
                    x_0 = xT[:, dj, 1 + tb * 512: 1 + tb * 512 + 512]
                    x_p1 = xT[:, dj, 2 + tb * 512: 2 + tb * 512 + 512]
                    v.tensor_scalar(out=tmpa[:], in0=x_p1,
                                    scalar1=cwc[:, 2 * NK + dj:2 * NK + dj + 1],
                                    scalar2=cbc[:, dj:dj + 1], op0=ALU.mult, op1=ALU.add)
                    v.scalar_tensor_tensor(out=tmpb[:], in0=x_m1, scalar=cwc[:, dj:dj + 1],
                                           in1=tmpa[:], op0=ALU.mult, op1=ALU.add)
                    ins = v.scalar_tensor_tensor(out=cxstg[:, par, tb * 512:(tb + 1) * 512], in0=x_0,
                                                 scalar=w1p1[:, dj:dj + 1], in1=tmpb[:],
                                                 op0=ALU.mult, op1=ALU.add)
                ins.then_inc(S["sVE2"], 1)

        @blk.sync
        def _(sy):
            for dj in range(NK):
                par = dj % 2
                sy.wait_ge(S["sVE2"], dj + 1)
                sy.dma_start(out=cxh[dj * 128:(dj + 1) * 128, :],
                             in_=cxstg[:, par, :]).then_inc(S["sCX2a"] if par == 0 else S["sCX2b"], 16)

    es_p2.close()

    with nc.Block() as blk:
        @blk.gpsimd
        def _(g):
            g.wait_ge(S["sPE1"], 64)
            g.dma_start(out=wg1[:], in_=wg1h).then_inc(S["sW3"], 16)
            g.dma_start(out=wg2[:], in_=wg2h).then_inc(S["sW3"], 16)

        @blk.tensor
        def _(t):
            t.wait_ge(S["sW3"], 32)
            t.wait_ge(S["sVE1"], 64)
            for m in range(NDH):
                pp = m % 2
                if m >= 2:
                    t.wait_ge(S["sSC3"], (m - 1) * 4)
                for k in range(NK):
                    for tb in range(NTB):
                        ins = t.matmul(ps_h[pp][:, tb * 512:(tb + 1) * 512],
                                       wg1[:, k, m * 128:(m + 1) * 128],
                                       xT[:, k, 1 + tb * 512: 1 + tb * 512 + 512],
                                       start=(k == 0), stop=(k == NK - 1))
                ins.then_inc(S["sPE3"], 1)

        @blk.scalar
        def _(s):
            for m in range(NDH):
                pp = m % 2
                s.wait_ge(S["sPE3"], m + 1)
                for tb in range(NTB):
                    s.activation(hT[:, m, tb * 512:(tb + 1) * 512],
                                 ps_h[pp][:, tb * 512:(tb + 1) * 512],
                                 AF.Relu, bias=bg1c[:, m:m + 1]).then_inc(S["sSC3"], 1)

    es_p3.close()

    es_p4 = ExitStack()
    wu_stg = sb(es_p4, "wu_stg", [128, 2, D], BF16)
    wv_stg = sb(es_p4, "wv_stg", [128, 2, D], BF16)
    grow = sb(es_p4, "grow", [128, 2, T], BF16)
    ugrow = sb(es_p4, "ugrow", [128, T], BF16)
    vgrow = sb(es_p4, "vgrow", [128, 2, T], BF16)
    ps_g = psum(es_p4, "ps_g", [128, T], F32)
    ps_u = psum(es_p4, "ps_u", [128, T], F32)

    def p4_gpsimd(g):
        g.wait_ge(S["sPE3"], 4)
        for rc in range(NK):
            par = rc % 2
            if rc >= 2:
                g.wait_ge(S["sPE4"], 3 * (rc - 2) + 3)
            sw = S["sWU4a"] if par == 0 else S["sWU4b"]
            g.dma_start(out=wu_stg[:, par, :], in_=wuch[rc]).then_inc(sw, 16)
            g.dma_start(out=wv_stg[:, par, :], in_=wvch[rc]).then_inc(sw, 16)
        g.wait_ge(S["sVE4"], 49)
        g.indirect_dma_start(out=shSA, out_offset=IndirectOffsetOnAxis(widx[:], 0),
                             in_=Sboth[:, 0:16], in_offset=None).then_inc(S["sEXw"], 16)
        g.wait_ge(S["sEXw"], 16)
        g.indirect_dma_start(out=shFA, out_offset=IndirectOffsetOnAxis(widx[:], 0),
                             in_=tagc[:], in_offset=None).then_inc(S["sEXw"], 16)

    def p4_tensor(t):
        t.wait_ge(S["sSC3"], 16)
        for rc in range(NK):
            par = rc % 2
            if rc >= 1:
                t.wait_ge(S["sVE4"], 3 * (rc - 1) + 3)
            for k in range(NDH):
                for tb in range(NTB):
                    ins = t.matmul(ps_g[:, tb * 512:(tb + 1) * 512],
                                   wg2[:, k, rc * 128:(rc + 1) * 128],
                                   hT[:, k, tb * 512:(tb + 1) * 512],
                                   start=(k == 0), stop=(k == NDH - 1))
            ins.then_inc(S["sPE4"], 1)
            t.wait_ge(S["sWU4a"] if rc % 2 == 0 else S["sWU4b"], 32 * (rc // 2 + 1))
            if rc >= 1:
                t.wait_ge(S["sVE4"], 3 * (rc - 1) + 1)
            for k in range(NK):
                for tb in range(NTB):
                    ins = t.matmul(ps_u[:, tb * 512:(tb + 1) * 512],
                                   wu_stg[:, par, k * 128:(k + 1) * 128],
                                   xT[:, k, 1 + tb * 512: 1 + tb * 512 + 512],
                                   start=(k == 0), stop=(k == NK - 1))
            ins.then_inc(S["sPE4"], 1)
            t.wait_ge(S["sSC4"], rc + 1)
            for k in range(NK):
                for tb in range(NTB):
                    ins = t.matmul(ps_g[:, tb * 512:(tb + 1) * 512],
                                   wv_stg[:, par, k * 128:(k + 1) * 128],
                                   xT[:, k, 1 + tb * 512: 1 + tb * 512 + 512],
                                   start=(k == 0), stop=(k == NK - 1))
            ins.then_inc(S["sPE4"], 1)

    def p4_scalar(s):
        s.wait_ge(S["sLD0"], 16 * N_LD0)
        for rc in range(NK):
            par = rc % 2
            s.wait_ge(S["sPE4"], 3 * rc + 1)
            if rc >= 2:
                s.wait_ge(S["sVE4"], 3 * (rc - 2) + 3)
            for tb in range(NTB):
                ins = s.activation(grow[:, par, tb * 512:(tb + 1) * 512],
                                   ps_g[:, tb * 512:(tb + 1) * 512],
                                   AF.Sigmoid, bias=bg2c[:, rc:rc + 1])
            ins.then_inc(S["sSC4"], 1)

    def p4_vector(v):
        v.wait_ge(S["sLD0"], 16 * N_LD0)
        for rc in range(NK):
            par = rc % 2
            v.wait_ge(S["sPE4"], 3 * rc + 2)
            v.wait_ge(S["sSC4"], rc + 1)
            for tb in range(NTB):
                ins = v.scalar_tensor_tensor(out=ugrow[:, tb * 512:(tb + 1) * 512],
                                             in0=ps_u[:, tb * 512:(tb + 1) * 512],
                                             scalar=buc[:, rc:rc + 1],
                                             in1=grow[:, par, tb * 512:(tb + 1) * 512],
                                             op0=ALU.add, op1=ALU.mult,
                                             accum_out=S4[:, rc * 4 + tb: rc * 4 + tb + 1])
            ins.then_inc(S["sVE4"], 1)
            v.tensor_tensor_scan(out=cs[:, rc, :], data0=ugrow[:], data1=ugrow[:],
                                 initial=0.0, op0=ALU.add, op1=ALU.bypass).then_inc(S["sVE4"], 1)
            v.wait_ge(S["sPE4"], 3 * rc + 3)
            if rc >= 2:
                v.wait_ge(S["sVG4a"] if rc % 2 == 0 else S["sVG4b"], 16 * (rc // 2))
            for tb in range(NTB):
                ins = v.scalar_tensor_tensor(out=vgrow[:, par, tb * 512:(tb + 1) * 512],
                                             in0=ps_g[:, tb * 512:(tb + 1) * 512],
                                             scalar=bvc[:, rc:rc + 1],
                                             in1=grow[:, par, tb * 512:(tb + 1) * 512],
                                             op0=ALU.add, op1=ALU.mult)
            ins.then_inc(S["sVE4"], 1)
        v.reduce_sum(Sboth[:, 0:16], S4[:].rearrange("p (rc tb) -> p rc tb", tb=4),
                     axis=AX.X).then_inc(S["sVE4"], 1)

    def p4_sync(sy):
        for rc in range(NK):
            par = rc % 2
            sy.wait_ge(S["sVE4"], 3 * rc + 3)
            sy.dma_start(out=vgh[rc * 128:(rc + 1) * 128, :],
                         in_=vgrow[:, par, :]).then_inc(S["sVG4a"] if par == 0 else S["sVG4b"], 16)

    with nc.Block() as blk:
        blk.gpsimd(p4_gpsimd)
        blk.tensor(p4_tensor)
        blk.scalar(p4_scalar)
        blk.vector(p4_vector)
        blk.sync(p4_sync)

    es_p4.close()

    es_ra1 = ExitStack()
    xstgr = sb(es_ra1, "xstgr", [128, 2, 4 * D], BF16)
    ps_xr = [psum(es_ra1, f"ps_xr{i}", [128, 512], BF16) for i in range(2)]

    with nc.Block() as blk:
        @blk.gpsimd
        def _(g):
            g.wait_ge(S["sPE4"], 48)
            g.wait_ge(S["sVE4"], 49)
            g.wait_ge(S["sSC4"], 16)
            g.wait_ge(S["sVG4a"], 128)
            g.wait_ge(S["sVG4b"], 128)
            for b in range(NTB):
                par = b % 2
                if b >= 2:
                    g.wait_ge(S["sPER_t"], (b - 1) * 16)
                sx = S["sXRa"] if par == 0 else S["sXRb"]
                for j in range(4):
                    g.dma_start(out=xstgr[:, par, j * D:(j + 1) * D],
                                in_=xrep[(b * 4 + j) * 128:(b * 4 + j + 1) * 128, :]).then_inc(sx, 16)

        @blk.tensor
        def _(t):
            t.wait_ge(S["sVE4"], 49)
            n = 0
            for b in range(NTB):
                par = b % 2
                t.wait_ge(S["sXRa"] if par == 0 else S["sXRb"], 64 * (b // 2 + 1))
                for dj in range(NK):
                    pp = n % 2
                    if n >= 2:
                        t.wait_ge(S["sVER_t"], n - 1)
                    for j in range(4):
                        ins = t.transpose(ps_xr[pp][:, j * 128:(j + 1) * 128],
                                          xstgr[:, par, j * D + dj * 128: j * D + dj * 128 + 128],
                                          idb[:])
                    ins.then_inc(S["sPER_t"], 1)
                    n += 1

        @blk.vector
        def _(v):
            n = 0
            for b in range(NTB):
                for dj in range(NK):
                    pp = n % 2
                    v.wait_ge(S["sPER_t"], n + 1)
                    v.tensor_copy(xT[:, dj, 1 + b * 512: 1 + b * 512 + 512],
                                  ps_xr[pp][:]).then_inc(S["sVER_t"], 1)
                    n += 1

    es_ra1.close()

    es_ra2 = ExitStack()
    wg1r = sb(es_ra2, "wg1r", [128, NK, DH], BF16)
    ps_hr = [psum(es_ra2, f"ps_hr{i}", [128, T], F32) for i in range(2)]

    with nc.Block() as blk:
        @blk.gpsimd
        def _(g):
            g.wait_ge(S["sPER_t"], 64)
            g.dma_start(out=wg1r[:], in_=wg1h).then_inc(S["sWR1"], 16)

        @blk.tensor
        def _(t):
            t.wait_ge(S["sWR1"], 16)
            t.wait_ge(S["sVER_t"], 64)
            for m in range(NDH):
                pp = m % 2
                if m >= 2:
                    t.wait_ge(S["sSCR"], m - 1)
                for k in range(NK):
                    for tb in range(NTB):
                        ins = t.matmul(ps_hr[pp][:, tb * 512:(tb + 1) * 512],
                                       wg1r[:, k, m * 128:(m + 1) * 128],
                                       xT[:, k, 1 + tb * 512: 1 + tb * 512 + 512],
                                       start=(k == 0), stop=(k == NK - 1))
                ins.then_inc(S["sPER"], 1)

        @blk.scalar
        def _(s):
            s.wait_ge(S["sPE4"], 48)
            for m in range(NDH):
                pp = m % 2
                s.wait_ge(S["sPER"], m + 1)
                for tb in range(NTB):
                    ins = s.activation(hT[:, m, tb * 512:(tb + 1) * 512],
                                       ps_hr[pp][:, tb * 512:(tb + 1) * 512],
                                       AF.Relu, bias=bg1c[:, m:m + 1])
                ins.then_inc(S["sSCR"], 1)

    es_ra2.close()

    es_rb = ExitStack()
    wur_stg = sb(es_rb, "wur_stg", [128, 2, D], BF16)
    growr = sb(es_rb, "growr", [128, 2, T], BF16)
    ugrowr = sb(es_rb, "ugrowr", [128, T], BF16)
    ps_gr = psum(es_rb, "ps_gr", [128, T], F32)
    ps_ur = psum(es_rb, "ps_ur", [128, T], F32)

    with nc.Block() as blk:
        @blk.gpsimd
        def _(g):
            g.wait_ge(S["sPER"], 4)
            for rc in range(NK):
                par = rc % 2
                if rc >= 2:
                    g.wait_ge(S["sPER"], 4 + 2 * (rc - 2) + 2)
                g.dma_start(out=wur_stg[:, par, :],
                            in_=wuch[rc]).then_inc(S["sWRa"] if par == 0 else S["sWRb"], 16)

        @blk.tensor
        def _(t):
            t.wait_ge(S["sSCR"], 4)
            for rc in range(NK):
                par = rc % 2
                if rc >= 1:
                    t.wait_ge(S["sSCR"], 4 + rc)
                for k in range(NDH):
                    for tb in range(NTB):
                        ins = t.matmul(ps_gr[:, tb * 512:(tb + 1) * 512],
                                       wg2[:, k, rc * 128:(rc + 1) * 128],
                                       hT[:, k, tb * 512:(tb + 1) * 512],
                                       start=(k == 0), stop=(k == NDH - 1))
                ins.then_inc(S["sPER"], 1)
                t.wait_ge(S["sWRa"] if rc % 2 == 0 else S["sWRb"], 16 * (rc // 2 + 1))
                if rc >= 1:
                    t.wait_ge(S["sVER"], rc)
                for k in range(NK):
                    for tb in range(NTB):
                        ins = t.matmul(ps_ur[:, tb * 512:(tb + 1) * 512],
                                       wur_stg[:, par, k * 128:(k + 1) * 128],
                                       xT[:, k, 1 + tb * 512: 1 + tb * 512 + 512],
                                       start=(k == 0), stop=(k == NK - 1))
                ins.then_inc(S["sPER"], 1)

        @blk.scalar
        def _(s):
            for rc in range(NK):
                par = rc % 2
                s.wait_ge(S["sPER"], 4 + 2 * rc + 1)
                if rc >= 2:
                    s.wait_ge(S["sVER"], rc - 1)
                for tb in range(NTB):
                    ins = s.activation(growr[:, par, tb * 512:(tb + 1) * 512],
                                       ps_gr[:, tb * 512:(tb + 1) * 512],
                                       AF.Sigmoid, bias=bg2c[:, rc:rc + 1])
                ins.then_inc(S["sSCR"], 1)

        @blk.vector
        def _(v):
            for rc in range(NK):
                par = rc % 2
                v.wait_ge(S["sPER"], 4 + 2 * rc + 2)
                v.wait_ge(S["sSCR"], 4 + rc + 1)
                for tb in range(NTB):
                    ins = v.scalar_tensor_tensor(out=ugrowr[:, tb * 512:(tb + 1) * 512],
                                                 in0=ps_ur[:, tb * 512:(tb + 1) * 512],
                                                 scalar=buc[:, rc:rc + 1],
                                                 in1=growr[:, par, tb * 512:(tb + 1) * 512],
                                                 op0=ALU.add, op1=ALU.mult,
                                                 accum_out=S4r[:, rc * 4 + tb: rc * 4 + tb + 1])
                ins.then_inc(S["sVER"], 1)
            v.reduce_sum(Sboth[:, 16:32], S4r[:].rearrange("p (rc tb) -> p rc tb", tb=4),
                         axis=AX.X).then_inc(S["sVER"], 1)

    es_rb.close()
    es_b.close()
    es_xT.close()

    es_56 = ExitStack()
    wur = sb(es_56, "wur", [128, NK, D], BF16)
    vgld = sb(es_56, "vgld", [128, 2, T], BF16)
    cxt_stg = sb(es_56, "cxt_stg", [128, 2, D], BF16, is_transpose_dest=True)
    z1t = sb(es_56, "z1t", [128, D], F32)
    gotmp = sb(es_56, "gotmp", [128, T], BF16)
    zsq = sb(es_56, "zsq", [128, D], BF16)
    y1stg = sb(es_56, "y1stg", [128, 2, D], BF16)
    ps_uv = [psum(es_56, f"ps_uv{i}", [128, D], F32) for i in range(2)]

    with nc.Block() as blk:
        @blk.gpsimd
        def _(g):
            g.wait_ge(S["sLD0"], 16 * N_LD0)
            g.wait_ge(S["sPER"], 36)
            g.wait_ge(S["sSCR"], 20)
            g.wait_ge(S["sVER"], 17)
            g.dma_start(out=wur[:], in_=wurh).then_inc(S["sWUR"], 16)
            g.indirect_dma_start(out=shSB, out_offset=IndirectOffsetOnAxis(widx[:], 0),
                                 in_=Sboth[:, 16:32], in_offset=None).then_inc(S["sEXw"], 16)
            g.wait_ge(S["sEXw"], 48)
            g.indirect_dma_start(out=shFB, out_offset=IndirectOffsetOnAxis(widx[:], 0),
                                 in_=tagc[:], in_offset=None).then_inc(S["sEXw"], 16)
            with g.register("nd") as nd, g.register("tgt") as tgt, g.register("f") as f:
                tv = g.value_load(tagc[0:1, 0:1])
                nda = g.value_load(ndab[0:1, 0:1])
                g.reg_mov(nd, nda)
                g.reg_mov(tgt, 0)
                with g.While(nd):
                    g.reg_add(tgt, tgt, 16)
                    g.indirect_dma_start(out=flg[:], out_offset=None,
                                         in_=shFA,
                                         in_offset=IndirectOffsetOnAxis(ridx[:], 0)).then_inc(S["sPL"], 16)
                    g.wait_ge(S["sPL"], tgt)
                    fv = g.value_load(flg[0:1, 0:1])
                    g.reg_alu(f, fv, tv, ALU.is_equal)
                    g.reg_alu(nd, f, 1, ALU.bitwise_xor)
                ndb = g.value_load(ndab[0:1, 1:2])
                g.reg_mov(nd, ndb)
                with g.While(nd):
                    g.reg_add(tgt, tgt, 16)
                    g.indirect_dma_start(out=flg[:], out_offset=None,
                                         in_=shFB,
                                         in_offset=IndirectOffsetOnAxis(ridx[:], 0)).then_inc(S["sPL"], 16)
                    g.wait_ge(S["sPL"], tgt)
                    fv = g.value_load(flg[0:1, 0:1])
                    g.reg_alu(f, fv, tv, ALU.is_equal)
                    g.reg_alu(nd, f, 1, ALU.bitwise_xor)
            g.indirect_dma_start(out=Pboth[:, 0:16], out_offset=None,
                                 in_=shSA,
                                 in_offset=IndirectOffsetOnAxis(ridxA[:], 0)).then_inc(S["sEXr"], 16)
            g.indirect_dma_start(out=Pboth[:, 16:32], out_offset=None,
                                 in_=shSB,
                                 in_offset=IndirectOffsetOnAxis(ridxB[:], 0)).then_inc(S["sEXr"], 16)

        @blk.vector
        def _(v):
            v.wait_ge(S["sEXr"], 32)
            v.tensor_scalar_mul(Pcol[:], Sboth[:, 16:32], amc[:, 0:1])
            v.scalar_tensor_tensor(out=Pcol[:], in0=Pboth[:, 16:32], scalar=amc[:, 1:2],
                                   in1=Pcol[:], op0=ALU.mult, op1=ALU.add)
            v.scalar_tensor_tensor(out=Pcol[:], in0=Pboth[:, 0:16], scalar=amc[:, 2:3],
                                   in1=Pcol[:], op0=ALU.mult, op1=ALU.add).then_inc(S["sVEX"], 1)

    with nc.Block() as blk:
        @blk.sync
        def _(sy):
            sy.wait_ge(S["sPER"], 36)
            sy.wait_ge(S["sSCR"], 20)
            sy.wait_ge(S["sVER"], 17)
            for rc in range(NK):
                par = rc % 2
                if rc >= 2:
                    sy.wait_ge(S["sVE5"], rc - 1)
                sy.dma_start(out=vgld[:, par, :],
                             in_=vgh[rc * 128:(rc + 1) * 128, :]).then_inc(S["sVG5a"] if par == 0 else S["sVG5b"], 16)

        @blk.vector
        def _(v):
            v.wait_ge(S["sVEX"], 1)
            for rc in range(NK):
                par = rc % 2
                v.wait_ge(S["sVG5a"] if rc % 2 == 0 else S["sVG5b"], 16 * (rc // 2 + 1))
                v.scalar_tensor_tensor(out=gotmp[:], in0=cs[:, rc, :],
                                       scalar=Pcol[:, rc:rc + 1], in1=vgld[:, par, :],
                                       op0=ALU.add, op1=ALU.mult)
                v.tensor_copy(cs[:, rc, :], gotmp[:]).then_inc(S["sVE5"], 1)

    with nc.Block() as blk:
        @blk.sync
        def _(sy):
            for tt in range(NTT):
                par = tt % 2
                if tt >= 2:
                    sy.wait_ge(S["sVE6"], 4 * (tt - 2) + 1)
                sy.dma_start_transpose(cxt_stg[:, par, :],
                                       cxh[:, tt * 128:(tt + 1) * 128]).then_inc(S["sCXTa"] if par == 0 else S["sCXTb"], 16)
                if tt >= 2:
                    ts = tt - 2
                    sy.wait_ge(S["sVE6"], 4 * ts + 4)
                    sy.dma_start(out=y1h[ts * 128:(ts + 1) * 128, :],
                                 in_=y1stg[:, ts % 2, :]).then_inc(S["sY1a"] if ts % 2 == 0 else S["sY1b"], 16)
            for ts in (NTT - 2, NTT - 1):
                sy.wait_ge(S["sVE6"], 4 * ts + 4)
                sy.dma_start(out=y1h[ts * 128:(ts + 1) * 128, :],
                             in_=y1stg[:, ts % 2, :]).then_inc(S["sY1a"] if ts % 2 == 0 else S["sY1b"], 16)

        @blk.tensor
        def _(t):
            t.wait_ge(S["sVE5"], 16)
            t.wait_ge(S["sWUR"], 16)
            for tt in range(NTT):
                pp = tt % 2
                if tt >= 2:
                    t.wait_ge(S["sVE6"], 4 * (tt - 2) + 1)
                for k in range(NK):
                    for db in range(4):
                        ins = t.matmul(ps_uv[pp][:, db * 512:(db + 1) * 512],
                                       cs[:, k, tt * 128:(tt + 1) * 128],
                                       wur[:, k, db * 512:(db + 1) * 512],
                                       start=(k == 0), stop=(k == NK - 1))
                ins.then_inc(S["sPE6"], 1)

        @blk.vector
        def _(v):
            for tt in range(NTT):
                pp = tt % 2
                v.wait_ge(S["sPE6"], tt + 1)
                v.wait_ge(S["sCXTa"] if tt % 2 == 0 else S["sCXTb"], 16 * (tt // 2 + 1))
                if tt >= 1:
                    v.wait_ge(S["sSC6"], 3 * (tt - 1) + 1)
                v.scalar_tensor_tensor(out=z1t[:], in0=ps_uv[pp][:], scalar=0.0,
                                       in1=cxt_stg[:, pp, :], op0=ALU.add,
                                       op1=ALU.add).then_inc(S["sVE6"], 1)
                v.wait_ge(S["sSC6"], 3 * tt + 1)
                v.tensor_scalar_mul(rowP[:], rowmS[:], rowmS[:])
                v.tensor_scalar_add(rowQ[:], rowtS2[:], EPS).then_inc(S["sVE6"], 1)
                v.wait_ge(S["sSC6"], 3 * tt + 2)
                v.tensor_scalar_mul(rowH[:], rowL[:], -0.5).then_inc(S["sVE6"], 1)
                v.wait_ge(S["sSC6"], 3 * tt + 3)
                if tt >= 2:
                    v.wait_ge(S["sY1a"] if tt % 2 == 0 else S["sY1b"], 16 * (tt // 2))
                v.tensor_scalar(out=y1stg[:, pp, :], in0=z1t[:], scalar1=rowmS[:],
                                scalar2=rowRS[:], op0=ALU.subtract, op1=ALU.mult).then_inc(S["sVE6"], 1)

        @blk.scalar
        def _(s):
            s.wait_ge(S["sPER"], 36)
            for tt in range(NTT):
                s.wait_ge(S["sVE6"], 4 * tt + 1)
                s.activation(zsq[:], z1t[:], AF.Square, accum_out=rowsq[:])
                s.activation(zsq[:], z1t[:], AF.Copy, accum_out=rowsumS[:])
                s.mul(rowmS[:], rowsumS[:], 1.0 / D)
                s.mul(rowtS2[:], rowsq[:], 1.0 / D).then_inc(S["sSC6"], 1)
                s.wait_ge(S["sVE6"], 4 * tt + 2)
                s.activation(rowL[:], rowP[:], AF.Ln, scale=-1.0, bias=rowQ[:]).then_inc(S["sSC6"], 1)
                s.wait_ge(S["sVE6"], 4 * tt + 3)
                s.activation(rowRS[:], rowH[:], AF.Exp).then_inc(S["sSC6"], 1)

    es_56.close()
    es_cs.close()

    es_p7 = ExitStack()
    y1f = sb(es_p7, "y1f", [128, NK, 1024], BF16, is_transpose_dest=True)
    hdn = sb(es_p7, "hdn", [128, NIC, 1024], BF16)
    wgf = sb(es_p7, "wgf", [128, 2, D], BF16)
    wuf = sb(es_p7, "wuf", [128, 2, D], BF16)
    wdf = sb(es_p7, "wdf", [128, 2, II], BF16)
    ggs = sb(es_p7, "ggs", [128, 1024], BF16)
    uus = sb(es_p7, "uus", [128, 1024], BF16)
    z2s = sb(es_p7, "z2s", [128, 2, 1024], BF16)
    z2t = sb(es_p7, "z2t", [128, 2, D], BF16, is_transpose_dest=True)
    zsq7 = sb(es_p7, "zsq7", [128, D], BF16)
    y2s = sb(es_p7, "y2s", [128, 2, D], F32)
    ps_a = [psum(es_p7, f"ps_a{i}", [128, 1024], F32) for i in range(2)]
    ps_b = [psum(es_p7, f"ps_b{i}", [128, 1024], F32) for i in range(2)]

    PEH = NIC + NK

    def p7_gpsimd(g):
        g.wait_ge(S["sPE6"], 16)
        g.wait_ge(S["sVE6"], 64)
        g.wait_ge(S["sSC6"], 48)
        g.wait_ge(S["sY1a"], 128)
        g.wait_ge(S["sY1b"], 128)
        for h in range(2):
            for ic in range(NIC):
                par = ic % 2
                n = h * NIC + ic
                if n >= 2:
                    hp, icp = divmod(n - 2, NIC)
                    g.wait_ge(S["sPE7"], PEH * hp + icp + 1)
                swg = S["sW7ga"] if par == 0 else S["sW7gb"]
                g.dma_start(out=wgf[:, par, :], in_=wgch[ic]).then_inc(swg, 16)
                g.dma_start(out=wuf[:, par, :], in_=wupch[ic]).then_inc(swg, 16)
            for dm in range(NK):
                par = dm % 2
                n = h * NK + dm
                if n >= 2:
                    hp, dmp = divmod(n - 2, NK)
                    g.wait_ge(S["sPE7"], PEH * hp + NIC + dmp + 1)
                g.dma_start(out=wdf[:, par, :],
                            in_=wdch[dm]).then_inc(S["sW7da"] if par == 0 else S["sW7db"], 16)

    def p7_tensor(t):
        for h in range(2):
            base = PEH * h
            t.wait_ge(S["sY1F"], 256 * (h + 1))
            if h == 1:
                t.wait_ge(S["sVE7"], PEH)
            for ic in range(NIC):
                par = ic % 2
                if ic >= 2:
                    t.wait_ge(S["sVE7"], base + (ic - 2) + 1)
                nw = h * NIC + ic
                t.wait_ge(S["sW7ga"] if ic % 2 == 0 else S["sW7gb"], 32 * (nw // 2 + 1))
                for k in range(NK):
                    for tb in range(2):
                        t.matmul(ps_a[par][:, tb * 512:(tb + 1) * 512],
                                 wgf[:, par, k * 128:(k + 1) * 128],
                                 y1f[:, k, tb * 512:(tb + 1) * 512],
                                 start=(k == 0), stop=(k == NK - 1))
                for k in range(NK):
                    for tb in range(2):
                        ins = t.matmul(ps_b[par][:, tb * 512:(tb + 1) * 512],
                                       wuf[:, par, k * 128:(k + 1) * 128],
                                       y1f[:, k, tb * 512:(tb + 1) * 512],
                                       start=(k == 0), stop=(k == NK - 1))
                ins.then_inc(S["sPE7"], 1)
            t.wait_ge(S["sVE7"], base + NIC)
            for dm in range(NK):
                par = dm % 2
                if dm >= 2:
                    t.wait_ge(S["sVE7"], base + NIC + (dm - 2) + 1)
                mw = h * NK + dm
                t.wait_ge(S["sW7da"] if dm % 2 == 0 else S["sW7db"], 16 * (mw // 2 + 1))
                for ki in range(NIC):
                    for tb in range(2):
                        ins = t.matmul(ps_a[par][:, tb * 512:(tb + 1) * 512],
                                       wdf[:, par, ki * 128:(ki + 1) * 128],
                                       hdn[:, ki, tb * 512:(tb + 1) * 512],
                                       start=(ki == 0), stop=(ki == NIC - 1))
                ins.then_inc(S["sPE7"], 1)

    def p7_scalar(s):
        s.wait_ge(S["sVE6"], 64)
        s.wait_ge(S["sLD0"], 16 * N_LD0)
        for h in range(2):
            base = PEH * h
            for ic in range(NIC):
                par = ic % 2
                s.wait_ge(S["sPE7"], base + ic + 1)
                if h * NIC + ic >= 1:
                    s.wait_ge(S["sVE7"], base + ic if ic >= 1 else PEH * h)
                s.activation(ggs[:], ps_a[par][:], AF.Silu,
                             scale=gsc[:, ic:ic + 1]).then_inc(S["sSC7"], 1)
            for t8 in range(8):
                tt = h * 8 + t8
                s.wait_ge(S["sZ2Ta"] if tt % 2 == 0 else S["sZ2Tb"], 16 * (tt // 2 + 1))
                if tt >= 1:
                    s.wait_ge(S["sVE7L"], 3 * (tt - 1) + 3)
                s.activation(zsq7[:], z2t[:, tt % 2, :], AF.Square, accum_out=rowsq[:])
                s.activation(zsq7[:], z2t[:, tt % 2, :], AF.Copy, accum_out=rowsumS[:])
                s.mul(rowmS[:], rowsumS[:], 1.0 / D)
                s.mul(rowtS2[:], rowsq[:], 1.0 / D).then_inc(S["sSC7L"], 1)
                s.wait_ge(S["sVE7L"], 3 * tt + 1)
                s.activation(rowL[:], rowP[:], AF.Ln, scale=-1.0, bias=rowQ[:]).then_inc(S["sSC7L"], 1)
                s.wait_ge(S["sVE7L"], 3 * tt + 2)
                s.activation(rowRS[:], rowH[:], AF.Exp).then_inc(S["sSC7L"], 1)

    def p7_vector(v):
        v.wait_ge(S["sLD0"], 16 * N_LD0)
        for h in range(2):
            base = PEH * h
            for ic in range(NIC):
                par = ic % 2
                v.wait_ge(S["sPE7"], base + ic + 1)
                v.wait_ge(S["sSC7"], h * NIC + ic + 1)
                v.tensor_scalar_mul(uus[:], ps_b[par][:], usc[:, ic:ic + 1])
                v.tensor_tensor(out=hdn[:, ic, :], in0=ggs[:], in1=uus[:],
                                op=ALU.mult).then_inc(S["sVE7"], 1)
            for dm in range(NK):
                par = dm % 2
                v.wait_ge(S["sPE7"], base + NIC + dm + 1)
                if dm >= 2 or h == 1:
                    mz = h * NK + dm
                    v.wait_ge(S["sZ2a"] if dm % 2 == 0 else S["sZ2b"], 16 * (mz // 2))
                v.scalar_tensor_tensor(out=z2s[:, dm % 2, :], in0=ps_a[par][:],
                                       scalar=dsc[:, dm:dm + 1], in1=y1f[:, dm, :],
                                       op0=ALU.mult, op1=ALU.add).then_inc(S["sVE7"], 1)
            for t8 in range(8):
                tt = h * 8 + t8
                pp = tt % 2
                v.wait_ge(S["sSC7L"], 3 * tt + 1)
                v.tensor_scalar_mul(rowP[:], rowmS[:], rowmS[:])
                v.tensor_scalar_add(rowQ[:], rowtS2[:], EPS).then_inc(S["sVE7L"], 1)
                v.wait_ge(S["sSC7L"], 3 * tt + 2)
                v.tensor_scalar_mul(rowH[:], rowL[:], -0.5).then_inc(S["sVE7L"], 1)
                v.wait_ge(S["sSC7L"], 3 * tt + 3)
                if tt >= 2:
                    v.wait_ge(S["sOUTa"] if tt % 2 == 0 else S["sOUTb"], 16 * (tt // 2))
                v.tensor_scalar(out=y2s[:, pp, :], in0=z2t[:, pp, :], scalar1=rowmS[:],
                                scalar2=rowRS[:], op0=ALU.subtract, op1=ALU.mult).then_inc(S["sVE7L"], 1)

    def p7_sync(sy):
        sy.wait_ge(S["sVE6"], 64)
        sy.wait_ge(S["sSC6"], 48)
        sy.wait_ge(S["sY1a"], 128)
        sy.wait_ge(S["sY1b"], 128)
        for h in range(2):
            if h == 1:
                sy.wait_ge(S["sVE7"], PEH)
            for k in range(NK):
                sy.dma_start_transpose(y1f[:, k, :],
                                       y1h[h * 1024:(h + 1) * 1024, k * 128:(k + 1) * 128]).then_inc(S["sY1F"], 16)
            for dm in range(NK):
                sy.wait_ge(S["sVE7"], PEH * h + NIC + dm + 1)
                sy.dma_start(out=z2h[dm * 128:(dm + 1) * 128, h * 1024:(h + 1) * 1024],
                             in_=z2s[:, dm % 2, :]).then_inc(S["sZ2a"] if dm % 2 == 0 else S["sZ2b"], 16)
            for t8 in range(8):
                tt = h * 8 + t8
                pp = tt % 2
                sy.wait_ge(S["sZ2a"], 128 * (h + 1))
                sy.wait_ge(S["sZ2b"], 128 * (h + 1))
                if tt >= 2:
                    sy.wait_ge(S["sVE7L"], 3 * (tt - 2) + 3)
                    sy.wait_ge(S["sSC7L"], 3 * (tt - 2) + 1)
                sy.dma_start_transpose(z2t[:, pp, :],
                                       z2h[:, tt * 128:(tt + 1) * 128]).then_inc(S["sZ2Ta"] if pp == 0 else S["sZ2Tb"], 16)
                if tt >= 2:
                    ts = tt - 2
                    sy.wait_ge(S["sVE7L"], 3 * ts + 3)
                    sy.dma_start(out=y[ts * 128:(ts + 1) * 128, :],
                                 in_=y2s[:, ts % 2, :]).then_inc(S["sOUTa"] if ts % 2 == 0 else S["sOUTb"], 16)
        for ts in (NTT - 2, NTT - 1):
            sy.wait_ge(S["sVE7L"], 3 * ts + 3)
            sy.dma_start(out=y[ts * 128:(ts + 1) * 128, :],
                         in_=y2s[:, ts % 2, :]).then_inc(S["sOUTa"] if ts % 2 == 0 else S["sOUTb"], 16)
        sy.wait_ge(S["sOUTa"], 128)
        sy.wait_ge(S["sOUTb"], 128)

    with nc.Block() as blk:
        blk.gpsimd(p7_gpsimd)
        blk.tensor(p7_tensor)
        blk.scalar(p7_scalar)
        blk.vector(p7_vector)
        blk.sync(p7_sync)

    es_p7.close()
    es0.close()
    return nc


def _prep_host(inputs):
    x = np.ascontiguousarray(np.asarray(inputs["L0_attn_x"], dtype=np.float32).reshape(16384, D))
    Wu = np.asarray(inputs["L0_attn_Wu"], np.float32)
    Wv = np.asarray(inputs["L0_attn_Wv"], np.float32)
    Wg1 = np.asarray(inputs["L0_attn_Wg1"], np.float32)
    Wg2 = np.asarray(inputs["L0_attn_Wg2"], np.float32)
    bu = np.asarray(inputs["L0_attn_bu"], np.float32)
    bv = np.asarray(inputs["L0_attn_bv"], np.float32)
    bg1 = np.asarray(inputs["L0_attn_bg1"], np.float32)
    bg2 = np.asarray(inputs["L0_attn_bg2"], np.float32)
    conv_w = np.asarray(inputs["L0_attn_conv_w"], np.float32)
    conv_b = np.asarray(inputs["L0_attn_conv_b"], np.float32)
    gw = np.asarray(inputs["L0_ffn_gate_w"], np.float32)
    uw = np.asarray(inputs["L0_ffn_up_w"], np.float32)
    dw = np.asarray(inputs["L0_ffn_down_w"], np.float32)
    gs = np.asarray(inputs["L0_ffn_gate_s"], np.float32)
    us = np.asarray(inputs["L0_ffn_up_s"], np.float32)
    ds = np.asarray(inputs["L0_ffn_down_s"], np.float32)

    C = np.ascontiguousarray
    wg1h = C(Wg1.T.reshape(NK, 128, DH).transpose(1, 0, 2))
    wg2h = C(Wg2.T.reshape(NDH, 128, R).transpose(1, 0, 2))
    WuT = C(Wu.T)
    wuch = C(WuT.reshape(NK, 128, NK, 128).transpose(2, 1, 0, 3)).reshape(NK, 128, D)
    WvT = C(Wv.T)
    wvch = C(WvT.reshape(NK, 128, NK, 128).transpose(2, 1, 0, 3)).reshape(NK, 128, D)
    wurh = C(WuT.reshape(NK, 128, D).transpose(1, 0, 2))
    wgch = C(gw.T.reshape(NK, 128, NIC, 128).transpose(2, 1, 0, 3)).reshape(NIC, 128, D)
    wupch = C(uw.T.reshape(NK, 128, NIC, 128).transpose(2, 1, 0, 3)).reshape(NIC, 128, D)
    wdch = C(dw.T.reshape(NIC, 128, NK, 128).transpose(2, 1, 0, 3)).reshape(NK, 128, II)
    buh = C(bu.reshape(NK, 128).T)
    bvh = C(bv.reshape(NK, 128).T)
    bg1h = C(bg1.reshape(NDH, 128).T)
    bg2h = C(bg2.reshape(NK, 128).T)
    cwh = C(np.concatenate([conv_w[:, 0, j].reshape(NK, 128).T for j in range(3)], axis=1))
    cbh = C(conv_b.reshape(NK, 128).T)
    gsh = C(gs.reshape(NIC, 128).T)
    ush = C(us.reshape(NIC, 128).T)
    dsh = C(ds.reshape(NK, 128).T)
    identh = np.eye(128).astype(ml_dtypes.bfloat16)
    tag = np.int32(np.random.randint(1, 2 ** 31 - 1))
    tagh = np.full((128, 1), tag, np.int32)
    epsh = np.full((128, 1), EPS, np.float32)
    ar = np.arange(128, dtype=np.uint32).reshape(128, 1)

    shared = dict(wg1h=wg1h, wg2h=wg2h, wuch=wuch, wvch=wvch, wurh=wurh,
                  wgch=wgch, wupch=wupch, wdch=wdch, buh=buh, bvh=bvh,
                  bg1h=bg1h, bg2h=bg2h, cwh=cwh, cbh=cbh, gsh=gsh, ush=ush,
                  dsh=dsh, identh=identh, tagh=tagh, epsh=epsh)

    in_maps = []
    for c in range(8):
        lo = c * T
        xc_ = C(x[lo:lo + T])
        b0 = (c // 4) * 4 * T
        xhl_ = C(x[lo - 1:lo]) if (c % 4) != 0 else np.zeros((1, D), np.float32)
        xhr_ = C(x[lo + T:lo + T + 1]) if (c % 4) != 3 else np.zeros((1, D), np.float32)
        if c % 4 == 2:
            xrep_ = C(x[b0:b0 + T])
        elif c % 4 == 3:
            xrep_ = C(x[b0 + T:b0 + 2 * T])
        else:
            xrep_ = xc_
        m = c % 4
        a1 = 1.0 if m >= 2 else 0.0
        a2 = 1.0 if m >= 2 else 0.0
        a3 = 1.0 if m in (1, 3) else 0.0
        amh = np.tile(np.array([[a1, a2, a3]], np.float32), (128, 1))
        p = c & 1
        nda = np.array([[1 if m == 1 else 0]], np.int32)
        ndb = np.array([[1 if m >= 2 else 0]], np.int32)
        wi = p * 128 + ar
        ri = (1 - p) * 128 + ar
        ria = ri if m >= 1 else wi
        rib = ri if m >= 2 else wi
        in_maps.append(dict(shared, xc=xc_, xrep=xrep_, xhl=xhl_, xhr=xhr_, amh=amh,
                            ndah=nda, ndbh=ndb,
                            widxh=wi, ridxh=ri, ridxah=ria, ridxbh=rib))
    return in_maps


def kernel(**inputs):
    in_maps = _prep_host(inputs)
    nc = build_nc()
    res = run_bass_kernel_spmd(nc, in_maps, core_ids=list(range(8)))
    out = np.concatenate([res.results[c]["y"] for c in range(8)], axis=0)
    return np.ascontiguousarray(out.reshape(2, 8192, D).astype(np.float32))


# version 11
# speedup vs baseline: 1.6135x; 1.0046x over previous
import numpy as np
import ml_dtypes
import concourse.bass as bass
import concourse.mybir as mybir
from concourse.bass import IndirectOffsetOnAxis
from concourse.bass_utils import run_bass_kernel_spmd
from contextlib import ExitStack

F32 = mybir.dt.float32
BF16 = mybir.dt.bfloat16
I32 = mybir.dt.int32
U32 = mybir.dt.uint32
AF = mybir.ActivationFunctionType
ALU = mybir.AluOpType
AX = mybir.AxisListType

T = 2048
D = 2048
R = 2048
DH = 512
II = 5632
NK = 16
NDH = 4
NIC = 44
NTT = 16
NTB = 4
EPS = 1e-5


def build_nc(race_check=True):
    nc = bass.Bass("TRN2", target_bir_lowering=False, debug=False,
                   detect_race_conditions=race_check)

    xc = nc.dram_tensor("xc", [T, D], F32, kind="ExternalInput").ap()
    xrep = nc.dram_tensor("xrep", [T, D], F32, kind="ExternalInput").ap()
    xhl = nc.dram_tensor("xhl", [1, D], F32, kind="ExternalInput").ap()
    xhr = nc.dram_tensor("xhr", [1, D], F32, kind="ExternalInput").ap()
    wg1h = nc.dram_tensor("wg1h", [128, 4, T], F32, kind="ExternalInput").ap()
    wg2h = nc.dram_tensor("wg2h", [128, NDH, R], F32, kind="ExternalInput").ap()
    wuch = nc.dram_tensor("wuch", [NK, 128, D], F32, kind="ExternalInput").ap()
    wvch = nc.dram_tensor("wvch", [NK, 128, D], F32, kind="ExternalInput").ap()
    wurh = nc.dram_tensor("wurh", [128, NK, D], F32, kind="ExternalInput").ap()
    wgch = nc.dram_tensor("wgch", [NIC, 128, D], F32, kind="ExternalInput").ap()
    wupch = nc.dram_tensor("wupch", [NIC, 128, D], F32, kind="ExternalInput").ap()
    wdch = nc.dram_tensor("wdch", [NK, 128, II], F32, kind="ExternalInput").ap()
    buh = nc.dram_tensor("buh", [128, NK], F32, kind="ExternalInput").ap()
    bvh = nc.dram_tensor("bvh", [128, NK], F32, kind="ExternalInput").ap()
    bg1h = nc.dram_tensor("bg1h", [128, NDH], F32, kind="ExternalInput").ap()
    bg2h = nc.dram_tensor("bg2h", [128, NK], F32, kind="ExternalInput").ap()
    cwh = nc.dram_tensor("cwh", [128, 3 * NK], F32, kind="ExternalInput").ap()
    cbh = nc.dram_tensor("cbh", [128, NK], F32, kind="ExternalInput").ap()
    gsh = nc.dram_tensor("gsh", [128, NIC], F32, kind="ExternalInput").ap()
    ush = nc.dram_tensor("ush", [128, NIC], F32, kind="ExternalInput").ap()
    dsh = nc.dram_tensor("dsh", [128, NK], F32, kind="ExternalInput").ap()
    identh = nc.dram_tensor("identh", [128, 128], BF16, kind="ExternalInput").ap()
    amh = nc.dram_tensor("amh", [128, 3], F32, kind="ExternalInput").ap()
    widxh = nc.dram_tensor("widxh", [128, 1], U32, kind="ExternalInput").ap()
    ridxh = nc.dram_tensor("ridxh", [128, 1], U32, kind="ExternalInput").ap()
    ridxah = nc.dram_tensor("ridxah", [128, 1], U32, kind="ExternalInput").ap()
    ridxbh = nc.dram_tensor("ridxbh", [128, 1], U32, kind="ExternalInput").ap()
    tagh = nc.dram_tensor("tagh", [128, 1], I32, kind="ExternalInput").ap()
    epsh = nc.dram_tensor("epsh", [128, 1], F32, kind="ExternalInput").ap()
    ndah = nc.dram_tensor("ndah", [1, 1], I32, kind="ExternalInput").ap()
    ndbh = nc.dram_tensor("ndbh", [1, 1], I32, kind="ExternalInput").ap()

    y = nc.dram_tensor("y", [T, D], F32, kind="ExternalOutput").ap()

    cxh = nc.dram_tensor("cxh", [D, T], BF16, kind="Internal").ap()
    vgh = nc.dram_tensor("vgh", [R, T], BF16, kind="Internal").ap()
    y1h = nc.dram_tensor("y1h", [T, D], BF16, kind="Internal").ap()
    z2h = nc.dram_tensor("z2h", [D, T], BF16, kind="Internal").ap()
    shSA = nc.dram_tensor("shSA", [256, 16], F32, kind="Internal", addr_space="Shared").ap()
    shSB = nc.dram_tensor("shSB", [256, 16], F32, kind="Internal", addr_space="Shared").ap()
    shFA = nc.dram_tensor("shFA", [256, 1], I32, kind="Internal", addr_space="Shared").ap()
    shFB = nc.dram_tensor("shFB", [256, 1], I32, kind="Internal", addr_space="Shared").ap()

    es0 = ExitStack()
    sb = lambda st, n, shape, dt, **kw: st.enter_context(nc.sbuf_tensor(n, shape, dt, **kw))
    psum = lambda st, n, shape, dt: st.enter_context(nc.psum_tensor(n, shape, dt))

    sem_names = [
        "sLD0", "sVE0",
        "sX1a", "sX1b", "sPE1", "sVE1",
        "sVE2", "sCX2a", "sCX2b",
        "sW3", "sPE3", "sSC3",
        "sWU4a", "sWU4b", "sPE4", "sSC4", "sVE4", "sVG4a", "sVG4b",
        "sXRa", "sXRb", "sPER_t", "sVER_t", "sWR1", "sWRa", "sWRb", "sPER", "sSCR", "sVER",
        "sEXw", "sPL", "sEXr", "sVEX",
        "sWUR", "sVG5a", "sVG5b", "sVE5",
        "sCXTa", "sCXTb", "sPE6", "sSC6", "sVE6", "sY1a", "sY1b",
        "sW7ga", "sW7gb", "sW7da", "sW7db", "sY1F", "sPE7", "sSC7", "sVE7",
        "sZ2a", "sZ2b", "sZ2Ta", "sZ2Tb", "sVE7L", "sSC7L", "sOUTa", "sOUTb",
    ]
    S = {n: es0.enter_context(nc.semaphore(n)) for n in sem_names}

    idb = sb(es0, "idb", [128, 128], BF16)
    buc = sb(es0, "buc", [128, NK], F32)
    bvc = sb(es0, "bvc", [128, NK], F32)
    bg1c = sb(es0, "bg1c", [128, NDH], F32)
    bg2c = sb(es0, "bg2c", [128, NK], F32)
    cwc = sb(es0, "cwc", [128, 3 * NK], F32)
    cbc = sb(es0, "cbc", [128, NK], F32)
    w1p1 = sb(es0, "w1p1", [128, NK], F32)
    gsc = sb(es0, "gsc", [128, NIC], F32)
    usc = sb(es0, "usc", [128, NIC], F32)
    dsc = sb(es0, "dsc", [128, NK], F32)
    amc = sb(es0, "amc", [128, 3], F32)
    widx = sb(es0, "widx", [128, 1], U32)
    ridx = sb(es0, "ridx", [128, 1], U32)
    ridxA = sb(es0, "ridxA", [128, 1], U32)
    ridxB = sb(es0, "ridxB", [128, 1], U32)
    tagc = sb(es0, "tagc", [128, 1], I32)
    Sboth = sb(es0, "Sboth", [128, 32], F32)
    Pboth = sb(es0, "Pboth", [128, 32], F32)
    Pcol = sb(es0, "Pcol", [128, NK], F32)
    S4 = sb(es0, "S4", [128, 64], F32)
    S4r = sb(es0, "S4r", [128, 64], F32)
    flg = sb(es0, "flg", [128, 1], I32)
    ndab = sb(es0, "ndab", [1, 2], I32)
    rowm = sb(es0, "rowm", [128, 1], F32)
    rowt = sb(es0, "rowt", [128, 1], F32)
    rowv = sb(es0, "rowv", [128, 1], F32)
    rowsd = sb(es0, "rowsd", [128, 1], F32)
    rowrs = sb(es0, "rowrs", [128, 1], F32)
    rowsum = sb(es0, "rowsum", [128, 1], F32)
    rowsq = sb(es0, "rowsq", [128, 1], F32)
    rowsumS = sb(es0, "rowsumS", [128, 1], F32)
    epsB = sb(es0, "epsB", [128, 1], F32)
    rowmS = sb(es0, "rowmS", [128, 1], F32)
    rowtS2 = sb(es0, "rowtS2", [128, 1], F32)
    rowP = sb(es0, "rowP", [128, 1], F32)
    rowQ = sb(es0, "rowQ", [128, 1], F32)
    rowL = sb(es0, "rowL", [128, 1], F32)
    rowH = sb(es0, "rowH", [128, 1], F32)
    rowRS = sb(es0, "rowRS", [128, 1], F32)

    N_LD0 = 19 + 32

    es_cs = ExitStack()
    cs = sb(es_cs, "cs", [128, NK, T], BF16)
    wg1 = cs[:, 0:4, :]
    es_xT = ExitStack()
    xT = sb(es_xT, "xT", [128, NK, T + 2], BF16)

    with nc.Block() as blk:
        @blk.gpsimd
        def _(g):
            for dst, src in [(idb, identh), (buc, buh), (bvc, bvh), (bg1c, bg1h),
                             (bg2c, bg2h), (cwc, cwh), (cbc, cbh), (gsc, gsh),
                             (usc, ush), (dsc, dsh), (amc, amh), (widx, widxh),
                             (ridx, ridxh), (ridxA, ridxah), (ridxB, ridxbh),
                             (tagc, tagh), (epsB, epsh)]:
                g.dma_start(out=dst[:], in_=src).then_inc(S["sLD0"], 16)
            g.dma_start(out=ndab[:, 0:1], in_=ndah).then_inc(S["sLD0"], 16)
            g.dma_start(out=ndab[:, 1:2], in_=ndbh).then_inc(S["sLD0"], 16)
            g.dma_start(out=wg1[:], in_=wg1h).then_inc(S["sW3"], 16)
            for dj in range(NK):
                g.dma_start(out=xT[:, dj, 0:1],
                            in_=xhl[0:1, dj * 128:(dj + 1) * 128].rearrange("o (p u) -> (o p) u", u=1)).then_inc(S["sLD0"], 16)
                g.dma_start(out=xT[:, dj, T + 1:T + 2],
                            in_=xhr[0:1, dj * 128:(dj + 1) * 128].rearrange("o (p u) -> (o p) u", u=1)).then_inc(S["sLD0"], 16)

        @blk.vector
        def _(v):
            v.wait_ge(S["sLD0"], 16 * N_LD0)
            v.tensor_scalar_add(w1p1[:], cwc[:, NK:2 * NK], 1.0).then_inc(S["sVE0"], 1)

    es_p1 = ExitStack()
    xstg = sb(es_p1, "xstg", [128, 2, 4 * D], BF16)
    ps_x = [psum(es_p1, f"ps_x{i}", [128, 512], BF16) for i in range(2)]

    with nc.Block() as blk:
        @blk.gpsimd
        def _(g):
            for b in range(NTB):
                par = b % 2
                if b >= 2:
                    g.wait_ge(S["sPE1"], (b - 1) * 16)
                sx = S["sX1a"] if par == 0 else S["sX1b"]
                for j in range(4):
                    g.dma_start(out=xstg[:, par, j * D:(j + 1) * D],
                                in_=xc[(b * 4 + j) * 128:(b * 4 + j + 1) * 128, :]).then_inc(sx, 16)

        @blk.tensor
        def _(t):
            t.wait_ge(S["sLD0"], 16 * N_LD0)
            n = 0
            for b in range(NTB):
                par = b % 2
                t.wait_ge(S["sX1a"] if b % 2 == 0 else S["sX1b"], 64 * (b // 2 + 1))
                for dj in range(NK):
                    pp = n % 2
                    if n >= 2:
                        t.wait_ge(S["sVE1"], n - 1)
                    for j in range(4):
                        ins = t.transpose(ps_x[pp][:, j * 128:(j + 1) * 128],
                                          xstg[:, par, j * D + dj * 128: j * D + dj * 128 + 128],
                                          idb[:])
                    ins.then_inc(S["sPE1"], 1)
                    n += 1

        @blk.vector
        def _(v):
            n = 0
            for b in range(NTB):
                for dj in range(NK):
                    pp = n % 2
                    v.wait_ge(S["sPE1"], n + 1)
                    v.tensor_copy(xT[:, dj, 1 + b * 512: 1 + b * 512 + 512],
                                  ps_x[pp][:]).then_inc(S["sVE1"], 1)
                    n += 1

    es_p1.close()

    es_b = ExitStack()
    hT = sb(es_b, "hT", [128, NDH, T], BF16)
    wg2 = sb(es_b, "wg2", [128, NDH, R], BF16)
    es_p3 = ExitStack()
    ps_h = [psum(es_p3, f"ps_h{i}", [128, T], F32) for i in range(2)]

    es_p2 = ExitStack()
    cxstg = sb(es_p2, "cxstg", [128, 2, T], BF16)
    tmpa = sb(es_p2, "tmpa", [128, 512], F32)
    tmpb = sb(es_p2, "tmpb", [128, 512], F32)

    with nc.Block() as blk:
        @blk.vector
        def _(v):
            v.wait_ge(S["sVE0"], 1)
            for dj in range(NK):
                par = dj % 2
                if dj >= 2:
                    v.wait_ge(S["sCX2a"] if dj % 2 == 0 else S["sCX2b"], 16 * (dj // 2))
                for tb in range(NTB):
                    x_m1 = xT[:, dj, tb * 512: tb * 512 + 512]
                    x_0 = xT[:, dj, 1 + tb * 512: 1 + tb * 512 + 512]
                    x_p1 = xT[:, dj, 2 + tb * 512: 2 + tb * 512 + 512]
                    v.tensor_scalar(out=tmpa[:], in0=x_p1,
                                    scalar1=cwc[:, 2 * NK + dj:2 * NK + dj + 1],
                                    scalar2=cbc[:, dj:dj + 1], op0=ALU.mult, op1=ALU.add)
                    v.scalar_tensor_tensor(out=tmpb[:], in0=x_m1, scalar=cwc[:, dj:dj + 1],
                                           in1=tmpa[:], op0=ALU.mult, op1=ALU.add)
                    ins = v.scalar_tensor_tensor(out=cxstg[:, par, tb * 512:(tb + 1) * 512], in0=x_0,
                                                 scalar=w1p1[:, dj:dj + 1], in1=tmpb[:],
                                                 op0=ALU.mult, op1=ALU.add)
                ins.then_inc(S["sVE2"], 1)

        @blk.sync
        def _(sy):
            for dj in range(NK):
                par = dj % 2
                sy.wait_ge(S["sVE2"], dj + 1)
                sy.dma_start(out=cxh[dj * 128:(dj + 1) * 128, :],
                             in_=cxstg[:, par, :]).then_inc(S["sCX2a"] if par == 0 else S["sCX2b"], 16)

    es_p2.close()

    with nc.Block() as blk:
        @blk.gpsimd
        def _(g):
            g.wait_ge(S["sPE1"], 64)
            g.dma_start(out=wg2[:], in_=wg2h).then_inc(S["sW3"], 16)

        @blk.tensor
        def _(t):
            t.wait_ge(S["sW3"], 16)
            t.wait_ge(S["sVE1"], 64)
            for m in range(NDH):
                pp = m % 2
                if m >= 2:
                    t.wait_ge(S["sSC3"], (m - 1) * 4)
                for k in range(NK):
                    for tb in range(NTB):
                        ins = t.matmul(ps_h[pp][:, tb * 512:(tb + 1) * 512],
                                       wg1[:, k // 4, (k % 4) * DH + m * 128:(k % 4) * DH + (m + 1) * 128],
                                       xT[:, k, 1 + tb * 512: 1 + tb * 512 + 512],
                                       start=(k == 0), stop=(k == NK - 1))
                ins.then_inc(S["sPE3"], 1)

        @blk.scalar
        def _(s):
            for m in range(NDH):
                pp = m % 2
                s.wait_ge(S["sPE3"], m + 1)
                for tb in range(NTB):
                    s.activation(hT[:, m, tb * 512:(tb + 1) * 512],
                                 ps_h[pp][:, tb * 512:(tb + 1) * 512],
                                 AF.Relu, bias=bg1c[:, m:m + 1]).then_inc(S["sSC3"], 1)

    es_p3.close()

    es_p4 = ExitStack()
    wu_stg = sb(es_p4, "wu_stg", [128, 2, D], BF16)
    wv_stg = sb(es_p4, "wv_stg", [128, 2, D], BF16)
    grow = sb(es_p4, "grow", [128, 2, T], BF16)
    ugrow = sb(es_p4, "ugrow", [128, T], BF16)
    vgrow = sb(es_p4, "vgrow", [128, 2, T], BF16)
    ps_g = psum(es_p4, "ps_g", [128, T], F32)
    ps_u = psum(es_p4, "ps_u", [128, T], F32)

    def p4_gpsimd(g):
        g.wait_ge(S["sPE3"], 4)
        for rc in range(NK):
            par = rc % 2
            if rc >= 2:
                g.wait_ge(S["sPE4"], 3 * (rc - 2) + 3)
            sw = S["sWU4a"] if par == 0 else S["sWU4b"]
            g.dma_start(out=wu_stg[:, par, :], in_=wuch[rc]).then_inc(sw, 16)
            g.dma_start(out=wv_stg[:, par, :], in_=wvch[rc]).then_inc(sw, 16)
        g.wait_ge(S["sVE4"], 49)
        g.indirect_dma_start(out=shSA, out_offset=IndirectOffsetOnAxis(widx[:], 0),
                             in_=Sboth[:, 0:16], in_offset=None).then_inc(S["sEXw"], 16)
        g.wait_ge(S["sEXw"], 16)
        g.indirect_dma_start(out=shFA, out_offset=IndirectOffsetOnAxis(widx[:], 0),
                             in_=tagc[:], in_offset=None).then_inc(S["sEXw"], 16)

    def p4_tensor(t):
        t.wait_ge(S["sSC3"], 16)
        t.wait_ge(S["sW3"], 32)
        for rc in range(NK):
            par = rc % 2
            if rc >= 1:
                t.wait_ge(S["sVE4"], 3 * (rc - 1) + 3)
            for k in range(NDH):
                for tb in range(NTB):
                    ins = t.matmul(ps_g[:, tb * 512:(tb + 1) * 512],
                                   wg2[:, k, rc * 128:(rc + 1) * 128],
                                   hT[:, k, tb * 512:(tb + 1) * 512],
                                   start=(k == 0), stop=(k == NDH - 1))
            ins.then_inc(S["sPE4"], 1)
            t.wait_ge(S["sWU4a"] if rc % 2 == 0 else S["sWU4b"], 32 * (rc // 2 + 1))
            if rc >= 1:
                t.wait_ge(S["sVE4"], 3 * (rc - 1) + 1)
            for k in range(NK):
                for tb in range(NTB):
                    ins = t.matmul(ps_u[:, tb * 512:(tb + 1) * 512],
                                   wu_stg[:, par, k * 128:(k + 1) * 128],
                                   xT[:, k, 1 + tb * 512: 1 + tb * 512 + 512],
                                   start=(k == 0), stop=(k == NK - 1))
            ins.then_inc(S["sPE4"], 1)
            t.wait_ge(S["sSC4"], rc + 1)
            for k in range(NK):
                for tb in range(NTB):
                    ins = t.matmul(ps_g[:, tb * 512:(tb + 1) * 512],
                                   wv_stg[:, par, k * 128:(k + 1) * 128],
                                   xT[:, k, 1 + tb * 512: 1 + tb * 512 + 512],
                                   start=(k == 0), stop=(k == NK - 1))
            ins.then_inc(S["sPE4"], 1)

    def p4_scalar(s):
        s.wait_ge(S["sLD0"], 16 * N_LD0)
        for rc in range(NK):
            par = rc % 2
            s.wait_ge(S["sPE4"], 3 * rc + 1)
            if rc >= 2:
                s.wait_ge(S["sVE4"], 3 * (rc - 2) + 3)
            for tb in range(NTB):
                ins = s.activation(grow[:, par, tb * 512:(tb + 1) * 512],
                                   ps_g[:, tb * 512:(tb + 1) * 512],
                                   AF.Sigmoid, bias=bg2c[:, rc:rc + 1])
            ins.then_inc(S["sSC4"], 1)

    def p4_vector(v):
        v.wait_ge(S["sLD0"], 16 * N_LD0)
        for rc in range(NK):
            par = rc % 2
            v.wait_ge(S["sPE4"], 3 * rc + 2)
            v.wait_ge(S["sSC4"], rc + 1)
            for tb in range(NTB):
                ins = v.scalar_tensor_tensor(out=ugrow[:, tb * 512:(tb + 1) * 512],
                                             in0=ps_u[:, tb * 512:(tb + 1) * 512],
                                             scalar=buc[:, rc:rc + 1],
                                             in1=grow[:, par, tb * 512:(tb + 1) * 512],
                                             op0=ALU.add, op1=ALU.mult,
                                             accum_out=S4[:, rc * 4 + tb: rc * 4 + tb + 1])
            ins.then_inc(S["sVE4"], 1)
            v.tensor_tensor_scan(out=cs[:, rc, :], data0=ugrow[:], data1=ugrow[:],
                                 initial=0.0, op0=ALU.add, op1=ALU.bypass).then_inc(S["sVE4"], 1)
            v.wait_ge(S["sPE4"], 3 * rc + 3)
            if rc >= 2:
                v.wait_ge(S["sVG4a"] if rc % 2 == 0 else S["sVG4b"], 16 * (rc // 2))
            for tb in range(NTB):
                ins = v.scalar_tensor_tensor(out=vgrow[:, par, tb * 512:(tb + 1) * 512],
                                             in0=ps_g[:, tb * 512:(tb + 1) * 512],
                                             scalar=bvc[:, rc:rc + 1],
                                             in1=grow[:, par, tb * 512:(tb + 1) * 512],
                                             op0=ALU.add, op1=ALU.mult)
            ins.then_inc(S["sVE4"], 1)
        v.reduce_sum(Sboth[:, 0:16], S4[:].rearrange("p (rc tb) -> p rc tb", tb=4),
                     axis=AX.X).then_inc(S["sVE4"], 1)

    def p4_sync(sy):
        for rc in range(NK):
            par = rc % 2
            sy.wait_ge(S["sVE4"], 3 * rc + 3)
            sy.dma_start(out=vgh[rc * 128:(rc + 1) * 128, :],
                         in_=vgrow[:, par, :]).then_inc(S["sVG4a"] if par == 0 else S["sVG4b"], 16)

    with nc.Block() as blk:
        blk.gpsimd(p4_gpsimd)
        blk.tensor(p4_tensor)
        blk.scalar(p4_scalar)
        blk.vector(p4_vector)
        blk.sync(p4_sync)

    es_p4.close()

    es_ra1 = ExitStack()
    xstgr = sb(es_ra1, "xstgr", [128, 2, 4 * D], BF16)
    ps_xr = [psum(es_ra1, f"ps_xr{i}", [128, 512], BF16) for i in range(2)]

    with nc.Block() as blk:
        @blk.gpsimd
        def _(g):
            g.wait_ge(S["sPE4"], 48)
            g.wait_ge(S["sVE4"], 49)
            g.wait_ge(S["sSC4"], 16)
            g.wait_ge(S["sVG4a"], 128)
            g.wait_ge(S["sVG4b"], 128)
            for b in range(NTB):
                par = b % 2
                if b >= 2:
                    g.wait_ge(S["sPER_t"], (b - 1) * 16)
                sx = S["sXRa"] if par == 0 else S["sXRb"]
                for j in range(4):
                    g.dma_start(out=xstgr[:, par, j * D:(j + 1) * D],
                                in_=xrep[(b * 4 + j) * 128:(b * 4 + j + 1) * 128, :]).then_inc(sx, 16)

        @blk.tensor
        def _(t):
            t.wait_ge(S["sVE4"], 49)
            n = 0
            for b in range(NTB):
                par = b % 2
                t.wait_ge(S["sXRa"] if par == 0 else S["sXRb"], 64 * (b // 2 + 1))
                for dj in range(NK):
                    pp = n % 2
                    if n >= 2:
                        t.wait_ge(S["sVER_t"], n - 1)
                    for j in range(4):
                        ins = t.transpose(ps_xr[pp][:, j * 128:(j + 1) * 128],
                                          xstgr[:, par, j * D + dj * 128: j * D + dj * 128 + 128],
                                          idb[:])
                    ins.then_inc(S["sPER_t"], 1)
                    n += 1

        @blk.vector
        def _(v):
            n = 0
            for b in range(NTB):
                for dj in range(NK):
                    pp = n % 2
                    v.wait_ge(S["sPER_t"], n + 1)
                    v.tensor_copy(xT[:, dj, 1 + b * 512: 1 + b * 512 + 512],
                                  ps_xr[pp][:]).then_inc(S["sVER_t"], 1)
                    n += 1

    es_ra1.close()

    es_ra2 = ExitStack()
    wg1r = sb(es_ra2, "wg1r", [128, 4, T], BF16)
    ps_hr = [psum(es_ra2, f"ps_hr{i}", [128, T], F32) for i in range(2)]

    with nc.Block() as blk:
        @blk.gpsimd
        def _(g):
            g.wait_ge(S["sPER_t"], 64)
            g.dma_start(out=wg1r[:], in_=wg1h).then_inc(S["sWR1"], 16)

        @blk.tensor
        def _(t):
            t.wait_ge(S["sWR1"], 16)
            t.wait_ge(S["sVER_t"], 64)
            for m in range(NDH):
                pp = m % 2
                if m >= 2:
                    t.wait_ge(S["sSCR"], m - 1)
                for k in range(NK):
                    for tb in range(NTB):
                        ins = t.matmul(ps_hr[pp][:, tb * 512:(tb + 1) * 512],
                                       wg1r[:, k // 4, (k % 4) * DH + m * 128:(k % 4) * DH + (m + 1) * 128],
                                       xT[:, k, 1 + tb * 512: 1 + tb * 512 + 512],
                                       start=(k == 0), stop=(k == NK - 1))
                ins.then_inc(S["sPER"], 1)

        @blk.scalar
        def _(s):
            s.wait_ge(S["sPE4"], 48)
            for m in range(NDH):
                pp = m % 2
                s.wait_ge(S["sPER"], m + 1)
                for tb in range(NTB):
                    ins = s.activation(hT[:, m, tb * 512:(tb + 1) * 512],
                                       ps_hr[pp][:, tb * 512:(tb + 1) * 512],
                                       AF.Relu, bias=bg1c[:, m:m + 1])
                ins.then_inc(S["sSCR"], 1)

    es_ra2.close()

    es_rb = ExitStack()
    wur_stg = sb(es_rb, "wur_stg", [128, 2, D], BF16)
    growr = sb(es_rb, "growr", [128, 2, T], BF16)
    ugrowr = sb(es_rb, "ugrowr", [128, T], BF16)
    ps_gr = psum(es_rb, "ps_gr", [128, T], F32)
    ps_ur = psum(es_rb, "ps_ur", [128, T], F32)

    with nc.Block() as blk:
        @blk.gpsimd
        def _(g):
            g.wait_ge(S["sPER"], 4)
            for rc in range(NK):
                par = rc % 2
                if rc >= 2:
                    g.wait_ge(S["sPER"], 4 + 2 * (rc - 2) + 2)
                g.dma_start(out=wur_stg[:, par, :],
                            in_=wuch[rc]).then_inc(S["sWRa"] if par == 0 else S["sWRb"], 16)

        @blk.tensor
        def _(t):
            t.wait_ge(S["sSCR"], 4)
            for rc in range(NK):
                par = rc % 2
                if rc >= 1:
                    t.wait_ge(S["sSCR"], 4 + rc)
                for k in range(NDH):
                    for tb in range(NTB):
                        ins = t.matmul(ps_gr[:, tb * 512:(tb + 1) * 512],
                                       wg2[:, k, rc * 128:(rc + 1) * 128],
                                       hT[:, k, tb * 512:(tb + 1) * 512],
                                       start=(k == 0), stop=(k == NDH - 1))
                ins.then_inc(S["sPER"], 1)
                t.wait_ge(S["sWRa"] if rc % 2 == 0 else S["sWRb"], 16 * (rc // 2 + 1))
                if rc >= 1:
                    t.wait_ge(S["sVER"], rc)
                for k in range(NK):
                    for tb in range(NTB):
                        ins = t.matmul(ps_ur[:, tb * 512:(tb + 1) * 512],
                                       wur_stg[:, par, k * 128:(k + 1) * 128],
                                       xT[:, k, 1 + tb * 512: 1 + tb * 512 + 512],
                                       start=(k == 0), stop=(k == NK - 1))
                ins.then_inc(S["sPER"], 1)

        @blk.scalar
        def _(s):
            for rc in range(NK):
                par = rc % 2
                s.wait_ge(S["sPER"], 4 + 2 * rc + 1)
                if rc >= 2:
                    s.wait_ge(S["sVER"], rc - 1)
                for tb in range(NTB):
                    ins = s.activation(growr[:, par, tb * 512:(tb + 1) * 512],
                                       ps_gr[:, tb * 512:(tb + 1) * 512],
                                       AF.Sigmoid, bias=bg2c[:, rc:rc + 1])
                ins.then_inc(S["sSCR"], 1)

        @blk.vector
        def _(v):
            for rc in range(NK):
                par = rc % 2
                v.wait_ge(S["sPER"], 4 + 2 * rc + 2)
                v.wait_ge(S["sSCR"], 4 + rc + 1)
                for tb in range(NTB):
                    ins = v.scalar_tensor_tensor(out=ugrowr[:, tb * 512:(tb + 1) * 512],
                                                 in0=ps_ur[:, tb * 512:(tb + 1) * 512],
                                                 scalar=buc[:, rc:rc + 1],
                                                 in1=growr[:, par, tb * 512:(tb + 1) * 512],
                                                 op0=ALU.add, op1=ALU.mult,
                                                 accum_out=S4r[:, rc * 4 + tb: rc * 4 + tb + 1])
                ins.then_inc(S["sVER"], 1)
            v.reduce_sum(Sboth[:, 16:32], S4r[:].rearrange("p (rc tb) -> p rc tb", tb=4),
                         axis=AX.X).then_inc(S["sVER"], 1)

    es_rb.close()
    es_b.close()
    es_xT.close()

    es_56 = ExitStack()
    wur = sb(es_56, "wur", [128, NK, D], BF16)
    vgld = sb(es_56, "vgld", [128, 2, T], BF16)
    cxt_stg = sb(es_56, "cxt_stg", [128, 2, D], BF16, is_transpose_dest=True)
    z1t = sb(es_56, "z1t", [128, D], F32)
    gotmp = sb(es_56, "gotmp", [128, T], BF16)
    zsq = sb(es_56, "zsq", [128, D], BF16)
    y1stg = sb(es_56, "y1stg", [128, 2, D], BF16)
    ps_uv = [psum(es_56, f"ps_uv{i}", [128, D], F32) for i in range(2)]

    with nc.Block() as blk:
        @blk.gpsimd
        def _(g):
            g.wait_ge(S["sLD0"], 16 * N_LD0)
            g.wait_ge(S["sPER"], 36)
            g.wait_ge(S["sSCR"], 20)
            g.wait_ge(S["sVER"], 17)
            g.dma_start(out=wur[:], in_=wurh).then_inc(S["sWUR"], 16)
            g.indirect_dma_start(out=shSB, out_offset=IndirectOffsetOnAxis(widx[:], 0),
                                 in_=Sboth[:, 16:32], in_offset=None).then_inc(S["sEXw"], 16)
            g.wait_ge(S["sEXw"], 48)
            g.indirect_dma_start(out=shFB, out_offset=IndirectOffsetOnAxis(widx[:], 0),
                                 in_=tagc[:], in_offset=None).then_inc(S["sEXw"], 16)
            with g.register("nd") as nd, g.register("tgt") as tgt, g.register("f") as f:
                tv = g.value_load(tagc[0:1, 0:1])
                nda = g.value_load(ndab[0:1, 0:1])
                g.reg_mov(nd, nda)
                g.reg_mov(tgt, 0)
                with g.While(nd):
                    g.reg_add(tgt, tgt, 16)
                    g.indirect_dma_start(out=flg[:], out_offset=None,
                                         in_=shFA,
                                         in_offset=IndirectOffsetOnAxis(ridx[:], 0)).then_inc(S["sPL"], 16)
                    g.wait_ge(S["sPL"], tgt)
                    fv = g.value_load(flg[0:1, 0:1])
                    g.reg_alu(f, fv, tv, ALU.is_equal)
                    g.reg_alu(nd, f, 1, ALU.bitwise_xor)
                ndb = g.value_load(ndab[0:1, 1:2])
                g.reg_mov(nd, ndb)
                with g.While(nd):
                    g.reg_add(tgt, tgt, 16)
                    g.indirect_dma_start(out=flg[:], out_offset=None,
                                         in_=shFB,
                                         in_offset=IndirectOffsetOnAxis(ridx[:], 0)).then_inc(S["sPL"], 16)
                    g.wait_ge(S["sPL"], tgt)
                    fv = g.value_load(flg[0:1, 0:1])
                    g.reg_alu(f, fv, tv, ALU.is_equal)
                    g.reg_alu(nd, f, 1, ALU.bitwise_xor)
            g.indirect_dma_start(out=Pboth[:, 0:16], out_offset=None,
                                 in_=shSA,
                                 in_offset=IndirectOffsetOnAxis(ridxA[:], 0)).then_inc(S["sEXr"], 16)
            g.indirect_dma_start(out=Pboth[:, 16:32], out_offset=None,
                                 in_=shSB,
                                 in_offset=IndirectOffsetOnAxis(ridxB[:], 0)).then_inc(S["sEXr"], 16)

        @blk.vector
        def _(v):
            v.wait_ge(S["sEXr"], 32)
            v.tensor_scalar_mul(Pcol[:], Sboth[:, 16:32], amc[:, 0:1])
            v.scalar_tensor_tensor(out=Pcol[:], in0=Pboth[:, 16:32], scalar=amc[:, 1:2],
                                   in1=Pcol[:], op0=ALU.mult, op1=ALU.add)
            v.scalar_tensor_tensor(out=Pcol[:], in0=Pboth[:, 0:16], scalar=amc[:, 2:3],
                                   in1=Pcol[:], op0=ALU.mult, op1=ALU.add).then_inc(S["sVEX"], 1)

    with nc.Block() as blk:
        @blk.sync
        def _(sy):
            sy.wait_ge(S["sPER"], 36)
            sy.wait_ge(S["sSCR"], 20)
            sy.wait_ge(S["sVER"], 17)
            for rc in range(NK):
                par = rc % 2
                if rc >= 2:
                    sy.wait_ge(S["sVE5"], rc - 1)
                sy.dma_start(out=vgld[:, par, :],
                             in_=vgh[rc * 128:(rc + 1) * 128, :]).then_inc(S["sVG5a"] if par == 0 else S["sVG5b"], 16)

        @blk.vector
        def _(v):
            v.wait_ge(S["sVEX"], 1)
            for rc in range(NK):
                par = rc % 2
                v.wait_ge(S["sVG5a"] if rc % 2 == 0 else S["sVG5b"], 16 * (rc // 2 + 1))
                v.scalar_tensor_tensor(out=gotmp[:], in0=cs[:, rc, :],
                                       scalar=Pcol[:, rc:rc + 1], in1=vgld[:, par, :],
                                       op0=ALU.add, op1=ALU.mult)
                v.tensor_copy(cs[:, rc, :], gotmp[:]).then_inc(S["sVE5"], 1)

    with nc.Block() as blk:
        @blk.sync
        def _(sy):
            for tt in range(NTT):
                par = tt % 2
                if tt >= 2:
                    sy.wait_ge(S["sVE6"], 4 * (tt - 2) + 1)
                sy.dma_start_transpose(cxt_stg[:, par, :],
                                       cxh[:, tt * 128:(tt + 1) * 128]).then_inc(S["sCXTa"] if par == 0 else S["sCXTb"], 16)
                if tt >= 2:
                    ts = tt - 2
                    sy.wait_ge(S["sVE6"], 4 * ts + 4)
                    sy.dma_start(out=y1h[ts * 128:(ts + 1) * 128, :],
                                 in_=y1stg[:, ts % 2, :]).then_inc(S["sY1a"] if ts % 2 == 0 else S["sY1b"], 16)
            for ts in (NTT - 2, NTT - 1):
                sy.wait_ge(S["sVE6"], 4 * ts + 4)
                sy.dma_start(out=y1h[ts * 128:(ts + 1) * 128, :],
                             in_=y1stg[:, ts % 2, :]).then_inc(S["sY1a"] if ts % 2 == 0 else S["sY1b"], 16)

        @blk.tensor
        def _(t):
            t.wait_ge(S["sVE5"], 16)
            t.wait_ge(S["sWUR"], 16)
            for tt in range(NTT):
                pp = tt % 2
                if tt >= 2:
                    t.wait_ge(S["sVE6"], 4 * (tt - 2) + 1)
                for k in range(NK):
                    for db in range(4):
                        ins = t.matmul(ps_uv[pp][:, db * 512:(db + 1) * 512],
                                       cs[:, k, tt * 128:(tt + 1) * 128],
                                       wur[:, k, db * 512:(db + 1) * 512],
                                       start=(k == 0), stop=(k == NK - 1))
                ins.then_inc(S["sPE6"], 1)

        @blk.vector
        def _(v):
            for tt in range(NTT):
                pp = tt % 2
                v.wait_ge(S["sPE6"], tt + 1)
                v.wait_ge(S["sCXTa"] if tt % 2 == 0 else S["sCXTb"], 16 * (tt // 2 + 1))
                if tt >= 1:
                    v.wait_ge(S["sSC6"], 3 * (tt - 1) + 1)
                v.scalar_tensor_tensor(out=z1t[:], in0=ps_uv[pp][:], scalar=0.0,
                                       in1=cxt_stg[:, pp, :], op0=ALU.add,
                                       op1=ALU.add).then_inc(S["sVE6"], 1)
                v.wait_ge(S["sSC6"], 3 * tt + 1)
                v.tensor_scalar_mul(rowP[:], rowmS[:], rowmS[:])
                v.tensor_scalar_add(rowQ[:], rowtS2[:], EPS).then_inc(S["sVE6"], 1)
                v.wait_ge(S["sSC6"], 3 * tt + 2)
                v.tensor_scalar_mul(rowH[:], rowL[:], -0.5).then_inc(S["sVE6"], 1)
                v.wait_ge(S["sSC6"], 3 * tt + 3)
                if tt >= 2:
                    v.wait_ge(S["sY1a"] if tt % 2 == 0 else S["sY1b"], 16 * (tt // 2))
                v.tensor_scalar(out=y1stg[:, pp, :], in0=z1t[:], scalar1=rowmS[:],
                                scalar2=rowRS[:], op0=ALU.subtract, op1=ALU.mult).then_inc(S["sVE6"], 1)

        @blk.scalar
        def _(s):
            s.wait_ge(S["sPER"], 36)
            for tt in range(NTT):
                s.wait_ge(S["sVE6"], 4 * tt + 1)
                s.activation(zsq[:], z1t[:], AF.Square, accum_out=rowsq[:])
                s.activation(zsq[:], z1t[:], AF.Copy, accum_out=rowsumS[:])
                s.mul(rowmS[:], rowsumS[:], 1.0 / D)
                s.mul(rowtS2[:], rowsq[:], 1.0 / D).then_inc(S["sSC6"], 1)
                s.wait_ge(S["sVE6"], 4 * tt + 2)
                s.activation(rowL[:], rowP[:], AF.Ln, scale=-1.0, bias=rowQ[:]).then_inc(S["sSC6"], 1)
                s.wait_ge(S["sVE6"], 4 * tt + 3)
                s.activation(rowRS[:], rowH[:], AF.Exp).then_inc(S["sSC6"], 1)

    es_56.close()
    es_cs.close()

    es_p7 = ExitStack()
    y1f = sb(es_p7, "y1f", [128, NK, 1024], BF16, is_transpose_dest=True)
    hdn = sb(es_p7, "hdn", [128, NIC, 1024], BF16)
    wgf = sb(es_p7, "wgf", [128, 2, D], BF16)
    wuf = sb(es_p7, "wuf", [128, 2, D], BF16)
    wdf = sb(es_p7, "wdf", [128, 2, II], BF16)
    ggs = sb(es_p7, "ggs", [128, 1024], BF16)
    uus = sb(es_p7, "uus", [128, 1024], BF16)
    z2s = sb(es_p7, "z2s", [128, 2, 1024], BF16)
    z2t = sb(es_p7, "z2t", [128, 2, D], BF16, is_transpose_dest=True)
    zsq7 = sb(es_p7, "zsq7", [128, D], BF16)
    y2s = sb(es_p7, "y2s", [128, 2, D], F32)
    ps_a = [psum(es_p7, f"ps_a{i}", [128, 1024], F32) for i in range(2)]
    ps_b = [psum(es_p7, f"ps_b{i}", [128, 1024], F32) for i in range(2)]

    PEH = NIC + NK

    def p7_gpsimd(g):
        g.wait_ge(S["sPE6"], 16)
        g.wait_ge(S["sVE6"], 64)
        g.wait_ge(S["sSC6"], 48)
        g.wait_ge(S["sY1a"], 128)
        g.wait_ge(S["sY1b"], 128)
        for h in range(2):
            for ic in range(NIC):
                par = ic % 2
                n = h * NIC + ic
                if n >= 2:
                    hp, icp = divmod(n - 2, NIC)
                    g.wait_ge(S["sPE7"], PEH * hp + icp + 1)
                swg = S["sW7ga"] if par == 0 else S["sW7gb"]
                g.dma_start(out=wgf[:, par, :], in_=wgch[ic]).then_inc(swg, 16)
                g.dma_start(out=wuf[:, par, :], in_=wupch[ic]).then_inc(swg, 16)
            for dm in range(NK):
                par = dm % 2
                n = h * NK + dm
                if n >= 2:
                    hp, dmp = divmod(n - 2, NK)
                    g.wait_ge(S["sPE7"], PEH * hp + NIC + dmp + 1)
                g.dma_start(out=wdf[:, par, :],
                            in_=wdch[dm]).then_inc(S["sW7da"] if par == 0 else S["sW7db"], 16)

    def p7_tensor(t):
        for h in range(2):
            base = PEH * h
            t.wait_ge(S["sY1F"], 256 * (h + 1))
            if h == 1:
                t.wait_ge(S["sVE7"], PEH)
            for ic in range(NIC):
                par = ic % 2
                if ic >= 2:
                    t.wait_ge(S["sVE7"], base + (ic - 2) + 1)
                nw = h * NIC + ic
                t.wait_ge(S["sW7ga"] if ic % 2 == 0 else S["sW7gb"], 32 * (nw // 2 + 1))
                for k in range(NK):
                    for tb in range(2):
                        t.matmul(ps_a[par][:, tb * 512:(tb + 1) * 512],
                                 wgf[:, par, k * 128:(k + 1) * 128],
                                 y1f[:, k, tb * 512:(tb + 1) * 512],
                                 start=(k == 0), stop=(k == NK - 1))
                for k in range(NK):
                    for tb in range(2):
                        ins = t.matmul(ps_b[par][:, tb * 512:(tb + 1) * 512],
                                       wuf[:, par, k * 128:(k + 1) * 128],
                                       y1f[:, k, tb * 512:(tb + 1) * 512],
                                       start=(k == 0), stop=(k == NK - 1))
                ins.then_inc(S["sPE7"], 1)
            t.wait_ge(S["sVE7"], base + NIC)
            for dm in range(NK):
                par = dm % 2
                if dm >= 2:
                    t.wait_ge(S["sVE7"], base + NIC + (dm - 2) + 1)
                mw = h * NK + dm
                t.wait_ge(S["sW7da"] if dm % 2 == 0 else S["sW7db"], 16 * (mw // 2 + 1))
                for ki in range(NIC):
                    for tb in range(2):
                        ins = t.matmul(ps_a[par][:, tb * 512:(tb + 1) * 512],
                                       wdf[:, par, ki * 128:(ki + 1) * 128],
                                       hdn[:, ki, tb * 512:(tb + 1) * 512],
                                       start=(ki == 0), stop=(ki == NIC - 1))
                ins.then_inc(S["sPE7"], 1)

    def p7_scalar(s):
        s.wait_ge(S["sVE6"], 64)
        s.wait_ge(S["sLD0"], 16 * N_LD0)
        for h in range(2):
            base = PEH * h
            for ic in range(NIC):
                par = ic % 2
                s.wait_ge(S["sPE7"], base + ic + 1)
                if h * NIC + ic >= 1:
                    s.wait_ge(S["sVE7"], base + ic if ic >= 1 else PEH * h)
                s.activation(ggs[:], ps_a[par][:], AF.Silu,
                             scale=gsc[:, ic:ic + 1]).then_inc(S["sSC7"], 1)
            for t8 in range(8):
                tt = h * 8 + t8
                s.wait_ge(S["sZ2Ta"] if tt % 2 == 0 else S["sZ2Tb"], 16 * (tt // 2 + 1))
                if tt >= 1:
                    s.wait_ge(S["sVE7L"], 3 * (tt - 1) + 3)
                s.activation(zsq7[:], z2t[:, tt % 2, :], AF.Square, accum_out=rowsq[:])
                s.activation(zsq7[:], z2t[:, tt % 2, :], AF.Copy, accum_out=rowsumS[:])
                s.mul(rowmS[:], rowsumS[:], 1.0 / D)
                s.mul(rowtS2[:], rowsq[:], 1.0 / D).then_inc(S["sSC7L"], 1)
                s.wait_ge(S["sVE7L"], 3 * tt + 1)
                s.activation(rowL[:], rowP[:], AF.Ln, scale=-1.0, bias=rowQ[:]).then_inc(S["sSC7L"], 1)
                s.wait_ge(S["sVE7L"], 3 * tt + 2)
                s.activation(rowRS[:], rowH[:], AF.Exp).then_inc(S["sSC7L"], 1)

    def p7_vector(v):
        v.wait_ge(S["sLD0"], 16 * N_LD0)
        for h in range(2):
            base = PEH * h
            for ic in range(NIC):
                par = ic % 2
                v.wait_ge(S["sPE7"], base + ic + 1)
                v.wait_ge(S["sSC7"], h * NIC + ic + 1)
                v.tensor_scalar_mul(uus[:], ps_b[par][:], usc[:, ic:ic + 1])
                v.tensor_tensor(out=hdn[:, ic, :], in0=ggs[:], in1=uus[:],
                                op=ALU.mult).then_inc(S["sVE7"], 1)
            for dm in range(NK):
                par = dm % 2
                v.wait_ge(S["sPE7"], base + NIC + dm + 1)
                if dm >= 2 or h == 1:
                    mz = h * NK + dm
                    v.wait_ge(S["sZ2a"] if dm % 2 == 0 else S["sZ2b"], 16 * (mz // 2))
                v.scalar_tensor_tensor(out=z2s[:, dm % 2, :], in0=ps_a[par][:],
                                       scalar=dsc[:, dm:dm + 1], in1=y1f[:, dm, :],
                                       op0=ALU.mult, op1=ALU.add).then_inc(S["sVE7"], 1)
            for t8 in range(8):
                tt = h * 8 + t8
                pp = tt % 2
                v.wait_ge(S["sSC7L"], 3 * tt + 1)
                v.tensor_scalar_mul(rowP[:], rowmS[:], rowmS[:])
                v.tensor_scalar_add(rowQ[:], rowtS2[:], EPS).then_inc(S["sVE7L"], 1)
                v.wait_ge(S["sSC7L"], 3 * tt + 2)
                v.tensor_scalar_mul(rowH[:], rowL[:], -0.5).then_inc(S["sVE7L"], 1)
                v.wait_ge(S["sSC7L"], 3 * tt + 3)
                if tt >= 2:
                    v.wait_ge(S["sOUTa"] if tt % 2 == 0 else S["sOUTb"], 16 * (tt // 2))
                v.tensor_scalar(out=y2s[:, pp, :], in0=z2t[:, pp, :], scalar1=rowmS[:],
                                scalar2=rowRS[:], op0=ALU.subtract, op1=ALU.mult).then_inc(S["sVE7L"], 1)

    def p7_sync(sy):
        sy.wait_ge(S["sVE6"], 64)
        sy.wait_ge(S["sSC6"], 48)
        sy.wait_ge(S["sY1a"], 128)
        sy.wait_ge(S["sY1b"], 128)
        for h in range(2):
            if h == 1:
                sy.wait_ge(S["sVE7"], PEH)
            for k in range(NK):
                sy.dma_start_transpose(y1f[:, k, :],
                                       y1h[h * 1024:(h + 1) * 1024, k * 128:(k + 1) * 128]).then_inc(S["sY1F"], 16)
            for dm in range(NK):
                sy.wait_ge(S["sVE7"], PEH * h + NIC + dm + 1)
                sy.dma_start(out=z2h[dm * 128:(dm + 1) * 128, h * 1024:(h + 1) * 1024],
                             in_=z2s[:, dm % 2, :]).then_inc(S["sZ2a"] if dm % 2 == 0 else S["sZ2b"], 16)
            for t8 in range(8):
                tt = h * 8 + t8
                pp = tt % 2
                sy.wait_ge(S["sZ2a"], 128 * (h + 1))
                sy.wait_ge(S["sZ2b"], 128 * (h + 1))
                if tt >= 2:
                    sy.wait_ge(S["sVE7L"], 3 * (tt - 2) + 3)
                    sy.wait_ge(S["sSC7L"], 3 * (tt - 2) + 1)
                sy.dma_start_transpose(z2t[:, pp, :],
                                       z2h[:, tt * 128:(tt + 1) * 128]).then_inc(S["sZ2Ta"] if pp == 0 else S["sZ2Tb"], 16)
                if tt >= 2:
                    ts = tt - 2
                    sy.wait_ge(S["sVE7L"], 3 * ts + 3)
                    sy.dma_start(out=y[ts * 128:(ts + 1) * 128, :],
                                 in_=y2s[:, ts % 2, :]).then_inc(S["sOUTa"] if ts % 2 == 0 else S["sOUTb"], 16)
        for ts in (NTT - 2, NTT - 1):
            sy.wait_ge(S["sVE7L"], 3 * ts + 3)
            sy.dma_start(out=y[ts * 128:(ts + 1) * 128, :],
                         in_=y2s[:, ts % 2, :]).then_inc(S["sOUTa"] if ts % 2 == 0 else S["sOUTb"], 16)
        sy.wait_ge(S["sOUTa"], 128)
        sy.wait_ge(S["sOUTb"], 128)

    with nc.Block() as blk:
        blk.gpsimd(p7_gpsimd)
        blk.tensor(p7_tensor)
        blk.scalar(p7_scalar)
        blk.vector(p7_vector)
        blk.sync(p7_sync)

    es_p7.close()
    es0.close()
    return nc


def _prep_host(inputs):
    x = np.ascontiguousarray(np.asarray(inputs["L0_attn_x"], dtype=np.float32).reshape(16384, D))
    Wu = np.asarray(inputs["L0_attn_Wu"], np.float32)
    Wv = np.asarray(inputs["L0_attn_Wv"], np.float32)
    Wg1 = np.asarray(inputs["L0_attn_Wg1"], np.float32)
    Wg2 = np.asarray(inputs["L0_attn_Wg2"], np.float32)
    bu = np.asarray(inputs["L0_attn_bu"], np.float32)
    bv = np.asarray(inputs["L0_attn_bv"], np.float32)
    bg1 = np.asarray(inputs["L0_attn_bg1"], np.float32)
    bg2 = np.asarray(inputs["L0_attn_bg2"], np.float32)
    conv_w = np.asarray(inputs["L0_attn_conv_w"], np.float32)
    conv_b = np.asarray(inputs["L0_attn_conv_b"], np.float32)
    gw = np.asarray(inputs["L0_ffn_gate_w"], np.float32)
    uw = np.asarray(inputs["L0_ffn_up_w"], np.float32)
    dw = np.asarray(inputs["L0_ffn_down_w"], np.float32)
    gs = np.asarray(inputs["L0_ffn_gate_s"], np.float32)
    us = np.asarray(inputs["L0_ffn_up_s"], np.float32)
    ds = np.asarray(inputs["L0_ffn_down_s"], np.float32)

    C = np.ascontiguousarray
    wg1h = C(Wg1.T.reshape(NK, 128, DH).transpose(1, 0, 2)).reshape(128, 4, T)
    wg2h = C(Wg2.T.reshape(NDH, 128, R).transpose(1, 0, 2))
    WuT = C(Wu.T)
    wuch = C(WuT.reshape(NK, 128, NK, 128).transpose(2, 1, 0, 3)).reshape(NK, 128, D)
    WvT = C(Wv.T)
    wvch = C(WvT.reshape(NK, 128, NK, 128).transpose(2, 1, 0, 3)).reshape(NK, 128, D)
    wurh = C(WuT.reshape(NK, 128, D).transpose(1, 0, 2))
    wgch = C(gw.T.reshape(NK, 128, NIC, 128).transpose(2, 1, 0, 3)).reshape(NIC, 128, D)
    wupch = C(uw.T.reshape(NK, 128, NIC, 128).transpose(2, 1, 0, 3)).reshape(NIC, 128, D)
    wdch = C(dw.T.reshape(NIC, 128, NK, 128).transpose(2, 1, 0, 3)).reshape(NK, 128, II)
    buh = C(bu.reshape(NK, 128).T)
    bvh = C(bv.reshape(NK, 128).T)
    bg1h = C(bg1.reshape(NDH, 128).T)
    bg2h = C(bg2.reshape(NK, 128).T)
    cwh = C(np.concatenate([conv_w[:, 0, j].reshape(NK, 128).T for j in range(3)], axis=1))
    cbh = C(conv_b.reshape(NK, 128).T)
    gsh = C(gs.reshape(NIC, 128).T)
    ush = C(us.reshape(NIC, 128).T)
    dsh = C(ds.reshape(NK, 128).T)
    identh = np.eye(128).astype(ml_dtypes.bfloat16)
    tag = np.int32(np.random.randint(1, 2 ** 31 - 1))
    tagh = np.full((128, 1), tag, np.int32)
    epsh = np.full((128, 1), EPS, np.float32)
    ar = np.arange(128, dtype=np.uint32).reshape(128, 1)

    shared = dict(wg1h=wg1h, wg2h=wg2h, wuch=wuch, wvch=wvch, wurh=wurh,
                  wgch=wgch, wupch=wupch, wdch=wdch, buh=buh, bvh=bvh,
                  bg1h=bg1h, bg2h=bg2h, cwh=cwh, cbh=cbh, gsh=gsh, ush=ush,
                  dsh=dsh, identh=identh, tagh=tagh, epsh=epsh)

    in_maps = []
    for c in range(8):
        lo = c * T
        xc_ = C(x[lo:lo + T])
        b0 = (c // 4) * 4 * T
        xhl_ = C(x[lo - 1:lo]) if (c % 4) != 0 else np.zeros((1, D), np.float32)
        xhr_ = C(x[lo + T:lo + T + 1]) if (c % 4) != 3 else np.zeros((1, D), np.float32)
        if c % 4 == 2:
            xrep_ = C(x[b0:b0 + T])
        elif c % 4 == 3:
            xrep_ = C(x[b0 + T:b0 + 2 * T])
        else:
            xrep_ = xc_
        m = c % 4
        a1 = 1.0 if m >= 2 else 0.0
        a2 = 1.0 if m >= 2 else 0.0
        a3 = 1.0 if m in (1, 3) else 0.0
        amh = np.tile(np.array([[a1, a2, a3]], np.float32), (128, 1))
        p = c & 1
        nda = np.array([[1 if m == 1 else 0]], np.int32)
        ndb = np.array([[1 if m >= 2 else 0]], np.int32)
        wi = p * 128 + ar
        ri = (1 - p) * 128 + ar
        ria = ri if m >= 1 else wi
        rib = ri if m >= 2 else wi
        in_maps.append(dict(shared, xc=xc_, xrep=xrep_, xhl=xhl_, xhr=xhr_, amh=amh,
                            ndah=nda, ndbh=ndb,
                            widxh=wi, ridxh=ri, ridxah=ria, ridxbh=rib))
    return in_maps


def kernel(**inputs):
    in_maps = _prep_host(inputs)
    nc = build_nc()
    res = run_bass_kernel_spmd(nc, in_maps, core_ids=list(range(8)))
    out = np.concatenate([res.results[c]["y"] for c in range(8)], axis=0)
    return np.ascontiguousarray(out.reshape(2, 8192, D).astype(np.float32))


# version 12
# speedup vs baseline: 1.6168x; 1.0021x over previous
import numpy as np
import ml_dtypes
import concourse.bass as bass
import concourse.mybir as mybir
from concourse.bass import IndirectOffsetOnAxis
from concourse.bass_utils import run_bass_kernel_spmd
from contextlib import ExitStack

F32 = mybir.dt.float32
BF16 = mybir.dt.bfloat16
I32 = mybir.dt.int32
U32 = mybir.dt.uint32
AF = mybir.ActivationFunctionType
ALU = mybir.AluOpType
AX = mybir.AxisListType

T = 2048
D = 2048
R = 2048
DH = 512
II = 5632
NK = 16
NDH = 4
NIC = 44
NTT = 16
NTB = 4
EPS = 1e-5


def build_nc(race_check=True):
    nc = bass.Bass("TRN2", target_bir_lowering=False, debug=False,
                   detect_race_conditions=race_check)

    xc = nc.dram_tensor("xc", [T, D], F32, kind="ExternalInput").ap()
    xrep = nc.dram_tensor("xrep", [T, D], F32, kind="ExternalInput").ap()
    xhl = nc.dram_tensor("xhl", [1, D], F32, kind="ExternalInput").ap()
    xhr = nc.dram_tensor("xhr", [1, D], F32, kind="ExternalInput").ap()
    wg1h = nc.dram_tensor("wg1h", [128, 4, T], F32, kind="ExternalInput").ap()
    wg2h = nc.dram_tensor("wg2h", [128, NDH, R], F32, kind="ExternalInput").ap()
    wuch = nc.dram_tensor("wuch", [NK, 128, D], F32, kind="ExternalInput").ap()
    wvch = nc.dram_tensor("wvch", [NK, 128, D], F32, kind="ExternalInput").ap()
    wurh = nc.dram_tensor("wurh", [128, NK, D], F32, kind="ExternalInput").ap()
    wgch = nc.dram_tensor("wgch", [NIC, 128, D], F32, kind="ExternalInput").ap()
    wupch = nc.dram_tensor("wupch", [NIC, 128, D], F32, kind="ExternalInput").ap()
    wdch = nc.dram_tensor("wdch", [NK, 128, II], F32, kind="ExternalInput").ap()
    buh = nc.dram_tensor("buh", [128, NK], F32, kind="ExternalInput").ap()
    bvh = nc.dram_tensor("bvh", [128, NK], F32, kind="ExternalInput").ap()
    bg1h = nc.dram_tensor("bg1h", [128, NDH], F32, kind="ExternalInput").ap()
    bg2h = nc.dram_tensor("bg2h", [128, NK], F32, kind="ExternalInput").ap()
    cwh = nc.dram_tensor("cwh", [128, 3 * NK], F32, kind="ExternalInput").ap()
    cbh = nc.dram_tensor("cbh", [128, NK], F32, kind="ExternalInput").ap()
    gsh = nc.dram_tensor("gsh", [128, NIC], F32, kind="ExternalInput").ap()
    ush = nc.dram_tensor("ush", [128, NIC], F32, kind="ExternalInput").ap()
    dsh = nc.dram_tensor("dsh", [128, NK], F32, kind="ExternalInput").ap()
    identh = nc.dram_tensor("identh", [128, 128], BF16, kind="ExternalInput").ap()
    amh = nc.dram_tensor("amh", [128, 3], F32, kind="ExternalInput").ap()
    widxh = nc.dram_tensor("widxh", [128, 1], U32, kind="ExternalInput").ap()
    ridxh = nc.dram_tensor("ridxh", [128, 1], U32, kind="ExternalInput").ap()
    ridxah = nc.dram_tensor("ridxah", [128, 1], U32, kind="ExternalInput").ap()
    ridxbh = nc.dram_tensor("ridxbh", [128, 1], U32, kind="ExternalInput").ap()
    tagh = nc.dram_tensor("tagh", [128, 1], I32, kind="ExternalInput").ap()
    epsh = nc.dram_tensor("epsh", [128, 1], F32, kind="ExternalInput").ap()
    ndah = nc.dram_tensor("ndah", [1, 1], I32, kind="ExternalInput").ap()
    ndbh = nc.dram_tensor("ndbh", [1, 1], I32, kind="ExternalInput").ap()

    y = nc.dram_tensor("y", [T, D], F32, kind="ExternalOutput").ap()

    cxh = nc.dram_tensor("cxh", [D, T], BF16, kind="Internal").ap()
    vgh = nc.dram_tensor("vgh", [R, T], BF16, kind="Internal").ap()
    y1h = nc.dram_tensor("y1h", [T, D], BF16, kind="Internal").ap()
    z2h = nc.dram_tensor("z2h", [D, T], BF16, kind="Internal").ap()
    shSA = nc.dram_tensor("shSA", [256, 16], F32, kind="Internal", addr_space="Shared").ap()
    shSB = nc.dram_tensor("shSB", [256, 16], F32, kind="Internal", addr_space="Shared").ap()
    shFA = nc.dram_tensor("shFA", [256, 1], I32, kind="Internal", addr_space="Shared").ap()
    shFB = nc.dram_tensor("shFB", [256, 1], I32, kind="Internal", addr_space="Shared").ap()

    es0 = ExitStack()
    sb = lambda st, n, shape, dt, **kw: st.enter_context(nc.sbuf_tensor(n, shape, dt, **kw))
    psum = lambda st, n, shape, dt: st.enter_context(nc.psum_tensor(n, shape, dt))

    sem_names = [
        "sLD0", "sVE0",
        "sX1a", "sX1b", "sPE1", "sVE1",
        "sVE2", "sCX2a", "sCX2b",
        "sW3", "sPE3", "sSC3",
        "sWU4a", "sWU4b", "sPE4", "sSC4", "sVE4", "sVG4a", "sVG4b",
        "sXRa", "sXRb", "sPER_t", "sVER_t", "sWR1", "sWRa", "sWRb", "sPER", "sSCR", "sVER",
        "sEXw", "sPL", "sEXr", "sVEX",
        "sWUR", "sVG5a", "sVG5b", "sVE5",
        "sCXTa", "sCXTb", "sPE6", "sSC6", "sVE6", "sY1a", "sY1b",
        "sW7ga", "sW7gb", "sW7da", "sW7db", "sY1F", "sPE7", "sSC7", "sVE7",
        "sZ2a", "sZ2b", "sZ2Ta", "sZ2Tb", "sVE7L", "sSC7L", "sOUTa", "sOUTb",
    ]
    S = {n: es0.enter_context(nc.semaphore(n)) for n in sem_names}

    idb = sb(es0, "idb", [128, 128], BF16)
    buc = sb(es0, "buc", [128, NK], F32)
    bvc = sb(es0, "bvc", [128, NK], F32)
    bg1c = sb(es0, "bg1c", [128, NDH], F32)
    bg2c = sb(es0, "bg2c", [128, NK], F32)
    cwc = sb(es0, "cwc", [128, 3 * NK], F32)
    cbc = sb(es0, "cbc", [128, NK], F32)
    w1p1 = sb(es0, "w1p1", [128, NK], F32)
    gsc = sb(es0, "gsc", [128, NIC], F32)
    usc = sb(es0, "usc", [128, NIC], F32)
    dsc = sb(es0, "dsc", [128, NK], F32)
    amc = sb(es0, "amc", [128, 3], F32)
    widx = sb(es0, "widx", [128, 1], U32)
    ridx = sb(es0, "ridx", [128, 1], U32)
    ridxA = sb(es0, "ridxA", [128, 1], U32)
    ridxB = sb(es0, "ridxB", [128, 1], U32)
    tagc = sb(es0, "tagc", [128, 1], I32)
    Sboth = sb(es0, "Sboth", [128, 32], F32)
    Pboth = sb(es0, "Pboth", [128, 32], F32)
    Pcol = sb(es0, "Pcol", [128, NK], F32)
    S4 = sb(es0, "S4", [128, 64], F32)
    S4r = sb(es0, "S4r", [128, 64], F32)
    flg = sb(es0, "flg", [128, 1], I32)
    ndab = sb(es0, "ndab", [1, 2], I32)
    rowm = sb(es0, "rowm", [128, 1], F32)
    rowt = sb(es0, "rowt", [128, 1], F32)
    rowv = sb(es0, "rowv", [128, 1], F32)
    rowsd = sb(es0, "rowsd", [128, 1], F32)
    rowrs = sb(es0, "rowrs", [128, 1], F32)
    rowsum = sb(es0, "rowsum", [128, 1], F32)
    rowsq = sb(es0, "rowsq", [128, 1], F32)
    rowsumS = sb(es0, "rowsumS", [128, 1], F32)
    epsB = sb(es0, "epsB", [128, 1], F32)
    rowmS = sb(es0, "rowmS", [128, 1], F32)
    rowtS2 = sb(es0, "rowtS2", [128, 1], F32)
    rowP = sb(es0, "rowP", [128, 1], F32)
    rowQ = sb(es0, "rowQ", [128, 1], F32)
    rowL = sb(es0, "rowL", [128, 1], F32)
    rowH = sb(es0, "rowH", [128, 1], F32)
    rowRS = sb(es0, "rowRS", [128, 1], F32)

    N_LD0 = 19 + 32

    es_cs = ExitStack()
    cs = sb(es_cs, "cs", [128, NK, T], BF16)
    wg1 = cs[:, 0:4, :]
    es_xT = ExitStack()
    xT = sb(es_xT, "xT", [128, NK, T + 2], BF16)

    with nc.Block() as blk:
        @blk.gpsimd
        def _(g):
            for dst, src in [(idb, identh), (buc, buh), (bvc, bvh), (bg1c, bg1h),
                             (bg2c, bg2h), (cwc, cwh), (cbc, cbh), (gsc, gsh),
                             (usc, ush), (dsc, dsh), (amc, amh), (widx, widxh),
                             (ridx, ridxh), (ridxA, ridxah), (ridxB, ridxbh),
                             (tagc, tagh), (epsB, epsh)]:
                g.dma_start(out=dst[:], in_=src).then_inc(S["sLD0"], 16)
            g.dma_start(out=ndab[:, 0:1], in_=ndah).then_inc(S["sLD0"], 16)
            g.dma_start(out=ndab[:, 1:2], in_=ndbh).then_inc(S["sLD0"], 16)
            g.dma_start(out=wg1[:], in_=wg1h).then_inc(S["sW3"], 16)
            for dj in range(NK):
                g.dma_start(out=xT[:, dj, 0:1],
                            in_=xhl[0:1, dj * 128:(dj + 1) * 128].rearrange("o (p u) -> (o p) u", u=1)).then_inc(S["sLD0"], 16)
                g.dma_start(out=xT[:, dj, T + 1:T + 2],
                            in_=xhr[0:1, dj * 128:(dj + 1) * 128].rearrange("o (p u) -> (o p) u", u=1)).then_inc(S["sLD0"], 16)

        @blk.vector
        def _(v):
            v.wait_ge(S["sLD0"], 16 * N_LD0)
            v.tensor_scalar_add(w1p1[:], cwc[:, NK:2 * NK], 1.0).then_inc(S["sVE0"], 1)

    es_p1 = ExitStack()
    xstg = sb(es_p1, "xstg", [128, 2, 4 * D], BF16)
    ps_x = [psum(es_p1, f"ps_x{i}", [128, 512], BF16) for i in range(2)]

    with nc.Block() as blk:
        @blk.gpsimd
        def _(g):
            for b in range(NTB):
                par = b % 2
                if b >= 2:
                    g.wait_ge(S["sPE1"], (b - 1) * 16)
                sx = S["sX1a"] if par == 0 else S["sX1b"]
                for j in range(4):
                    g.dma_start(out=xstg[:, par, j * D:(j + 1) * D],
                                in_=xc[(b * 4 + j) * 128:(b * 4 + j + 1) * 128, :]).then_inc(sx, 16)

        @blk.tensor
        def _(t):
            t.wait_ge(S["sLD0"], 16 * N_LD0)
            n = 0
            for b in range(NTB):
                par = b % 2
                t.wait_ge(S["sX1a"] if b % 2 == 0 else S["sX1b"], 64 * (b // 2 + 1))
                for dj in range(NK):
                    pp = n % 2
                    if n >= 2:
                        t.wait_ge(S["sVE1"], n - 1)
                    for j in range(4):
                        ins = t.transpose(ps_x[pp][:, j * 128:(j + 1) * 128],
                                          xstg[:, par, j * D + dj * 128: j * D + dj * 128 + 128],
                                          idb[:])
                    ins.then_inc(S["sPE1"], 1)
                    n += 1

        @blk.vector
        def _(v):
            n = 0
            for b in range(NTB):
                for dj in range(NK):
                    pp = n % 2
                    v.wait_ge(S["sPE1"], n + 1)
                    v.tensor_copy(xT[:, dj, 1 + b * 512: 1 + b * 512 + 512],
                                  ps_x[pp][:]).then_inc(S["sVE1"], 1)
                    n += 1

    es_p1.close()

    es_b = ExitStack()
    hT = sb(es_b, "hT", [128, NDH, T], BF16)
    wg2 = sb(es_b, "wg2", [128, NDH, R], BF16)
    es_p3 = ExitStack()
    ps_h = [psum(es_p3, f"ps_h{i}", [128, T], F32) for i in range(2)]

    es_p2 = ExitStack()
    cxstg = sb(es_p2, "cxstg", [128, 2, T], BF16)
    tmpa = sb(es_p2, "tmpa", [128, 512], BF16)
    tmpb = sb(es_p2, "tmpb", [128, 512], BF16)

    with nc.Block() as blk:
        @blk.vector
        def _(v):
            v.wait_ge(S["sVE0"], 1)
            for dj in range(NK):
                par = dj % 2
                if dj >= 2:
                    v.wait_ge(S["sCX2a"] if dj % 2 == 0 else S["sCX2b"], 16 * (dj // 2))
                for tb in range(NTB):
                    x_m1 = xT[:, dj, tb * 512: tb * 512 + 512]
                    x_0 = xT[:, dj, 1 + tb * 512: 1 + tb * 512 + 512]
                    x_p1 = xT[:, dj, 2 + tb * 512: 2 + tb * 512 + 512]
                    v.tensor_scalar(out=tmpa[:], in0=x_p1,
                                    scalar1=cwc[:, 2 * NK + dj:2 * NK + dj + 1],
                                    scalar2=cbc[:, dj:dj + 1], op0=ALU.mult, op1=ALU.add)
                    v.scalar_tensor_tensor(out=tmpb[:], in0=x_m1, scalar=cwc[:, dj:dj + 1],
                                           in1=tmpa[:], op0=ALU.mult, op1=ALU.add)
                    ins = v.scalar_tensor_tensor(out=cxstg[:, par, tb * 512:(tb + 1) * 512], in0=x_0,
                                                 scalar=w1p1[:, dj:dj + 1], in1=tmpb[:],
                                                 op0=ALU.mult, op1=ALU.add)
                ins.then_inc(S["sVE2"], 1)

        @blk.sync
        def _(sy):
            for dj in range(NK):
                par = dj % 2
                sy.wait_ge(S["sVE2"], dj + 1)
                sy.dma_start(out=cxh[dj * 128:(dj + 1) * 128, :],
                             in_=cxstg[:, par, :]).then_inc(S["sCX2a"] if par == 0 else S["sCX2b"], 16)

    es_p2.close()

    with nc.Block() as blk:
        @blk.gpsimd
        def _(g):
            g.wait_ge(S["sPE1"], 64)
            g.dma_start(out=wg2[:], in_=wg2h).then_inc(S["sW3"], 16)

        @blk.tensor
        def _(t):
            t.wait_ge(S["sW3"], 16)
            t.wait_ge(S["sVE1"], 64)
            for m in range(NDH):
                pp = m % 2
                if m >= 2:
                    t.wait_ge(S["sSC3"], (m - 1) * 4)
                for k in range(NK):
                    for tb in range(NTB):
                        ins = t.matmul(ps_h[pp][:, tb * 512:(tb + 1) * 512],
                                       wg1[:, k // 4, (k % 4) * DH + m * 128:(k % 4) * DH + (m + 1) * 128],
                                       xT[:, k, 1 + tb * 512: 1 + tb * 512 + 512],
                                       start=(k == 0), stop=(k == NK - 1))
                ins.then_inc(S["sPE3"], 1)

        @blk.scalar
        def _(s):
            for m in range(NDH):
                pp = m % 2
                s.wait_ge(S["sPE3"], m + 1)
                for tb in range(NTB):
                    s.activation(hT[:, m, tb * 512:(tb + 1) * 512],
                                 ps_h[pp][:, tb * 512:(tb + 1) * 512],
                                 AF.Relu, bias=bg1c[:, m:m + 1]).then_inc(S["sSC3"], 1)

    es_p3.close()

    es_p4 = ExitStack()
    wu_stg = sb(es_p4, "wu_stg", [128, 2, D], BF16)
    wv_stg = sb(es_p4, "wv_stg", [128, 2, D], BF16)
    grow = sb(es_p4, "grow", [128, 2, T], BF16)
    ugrow = sb(es_p4, "ugrow", [128, T], BF16)
    vgrow = sb(es_p4, "vgrow", [128, 2, T], BF16)
    ps_g = psum(es_p4, "ps_g", [128, T], F32)
    ps_u = psum(es_p4, "ps_u", [128, T], F32)

    def p4_gpsimd(g):
        g.wait_ge(S["sPE3"], 4)
        for rc in range(NK):
            par = rc % 2
            if rc >= 2:
                g.wait_ge(S["sPE4"], 3 * (rc - 2) + 3)
            sw = S["sWU4a"] if par == 0 else S["sWU4b"]
            g.dma_start(out=wu_stg[:, par, :], in_=wuch[rc]).then_inc(sw, 16)
            g.dma_start(out=wv_stg[:, par, :], in_=wvch[rc]).then_inc(sw, 16)
        g.wait_ge(S["sVE4"], 49)
        g.indirect_dma_start(out=shSA, out_offset=IndirectOffsetOnAxis(widx[:], 0),
                             in_=Sboth[:, 0:16], in_offset=None).then_inc(S["sEXw"], 16)
        g.wait_ge(S["sEXw"], 16)
        g.indirect_dma_start(out=shFA, out_offset=IndirectOffsetOnAxis(widx[:], 0),
                             in_=tagc[:], in_offset=None).then_inc(S["sEXw"], 16)

    def p4_tensor(t):
        t.wait_ge(S["sSC3"], 16)
        t.wait_ge(S["sW3"], 32)
        for rc in range(NK):
            par = rc % 2
            if rc >= 1:
                t.wait_ge(S["sVE4"], 3 * (rc - 1) + 3)
            for k in range(NDH):
                for tb in range(NTB):
                    ins = t.matmul(ps_g[:, tb * 512:(tb + 1) * 512],
                                   wg2[:, k, rc * 128:(rc + 1) * 128],
                                   hT[:, k, tb * 512:(tb + 1) * 512],
                                   start=(k == 0), stop=(k == NDH - 1))
            ins.then_inc(S["sPE4"], 1)
            t.wait_ge(S["sWU4a"] if rc % 2 == 0 else S["sWU4b"], 32 * (rc // 2 + 1))
            if rc >= 1:
                t.wait_ge(S["sVE4"], 3 * (rc - 1) + 1)
            for k in range(NK):
                for tb in range(NTB):
                    ins = t.matmul(ps_u[:, tb * 512:(tb + 1) * 512],
                                   wu_stg[:, par, k * 128:(k + 1) * 128],
                                   xT[:, k, 1 + tb * 512: 1 + tb * 512 + 512],
                                   start=(k == 0), stop=(k == NK - 1))
            ins.then_inc(S["sPE4"], 1)
            t.wait_ge(S["sSC4"], rc + 1)
            for k in range(NK):
                for tb in range(NTB):
                    ins = t.matmul(ps_g[:, tb * 512:(tb + 1) * 512],
                                   wv_stg[:, par, k * 128:(k + 1) * 128],
                                   xT[:, k, 1 + tb * 512: 1 + tb * 512 + 512],
                                   start=(k == 0), stop=(k == NK - 1))
            ins.then_inc(S["sPE4"], 1)

    def p4_scalar(s):
        s.wait_ge(S["sLD0"], 16 * N_LD0)
        for rc in range(NK):
            par = rc % 2
            s.wait_ge(S["sPE4"], 3 * rc + 1)
            if rc >= 2:
                s.wait_ge(S["sVE4"], 3 * (rc - 2) + 3)
            for tb in range(NTB):
                ins = s.activation(grow[:, par, tb * 512:(tb + 1) * 512],
                                   ps_g[:, tb * 512:(tb + 1) * 512],
                                   AF.Sigmoid, bias=bg2c[:, rc:rc + 1])
            ins.then_inc(S["sSC4"], 1)

    def p4_vector(v):
        v.wait_ge(S["sLD0"], 16 * N_LD0)
        for rc in range(NK):
            par = rc % 2
            v.wait_ge(S["sPE4"], 3 * rc + 2)
            v.wait_ge(S["sSC4"], rc + 1)
            for tb in range(NTB):
                ins = v.scalar_tensor_tensor(out=ugrow[:, tb * 512:(tb + 1) * 512],
                                             in0=ps_u[:, tb * 512:(tb + 1) * 512],
                                             scalar=buc[:, rc:rc + 1],
                                             in1=grow[:, par, tb * 512:(tb + 1) * 512],
                                             op0=ALU.add, op1=ALU.mult,
                                             accum_out=S4[:, rc * 4 + tb: rc * 4 + tb + 1])
            ins.then_inc(S["sVE4"], 1)
            v.tensor_tensor_scan(out=cs[:, rc, :], data0=ugrow[:], data1=ugrow[:],
                                 initial=0.0, op0=ALU.add, op1=ALU.bypass).then_inc(S["sVE4"], 1)
            v.wait_ge(S["sPE4"], 3 * rc + 3)
            if rc >= 2:
                v.wait_ge(S["sVG4a"] if rc % 2 == 0 else S["sVG4b"], 16 * (rc // 2))
            for tb in range(NTB):
                ins = v.scalar_tensor_tensor(out=vgrow[:, par, tb * 512:(tb + 1) * 512],
                                             in0=ps_g[:, tb * 512:(tb + 1) * 512],
                                             scalar=bvc[:, rc:rc + 1],
                                             in1=grow[:, par, tb * 512:(tb + 1) * 512],
                                             op0=ALU.add, op1=ALU.mult)
            ins.then_inc(S["sVE4"], 1)
        v.reduce_sum(Sboth[:, 0:16], S4[:].rearrange("p (rc tb) -> p rc tb", tb=4),
                     axis=AX.X).then_inc(S["sVE4"], 1)

    def p4_sync(sy):
        for rc in range(NK):
            par = rc % 2
            sy.wait_ge(S["sVE4"], 3 * rc + 3)
            sy.dma_start(out=vgh[rc * 128:(rc + 1) * 128, :],
                         in_=vgrow[:, par, :]).then_inc(S["sVG4a"] if par == 0 else S["sVG4b"], 16)

    with nc.Block() as blk:
        blk.gpsimd(p4_gpsimd)
        blk.tensor(p4_tensor)
        blk.scalar(p4_scalar)
        blk.vector(p4_vector)
        blk.sync(p4_sync)

    es_p4.close()

    es_ra1 = ExitStack()
    xstgr = sb(es_ra1, "xstgr", [128, 2, 4 * D], BF16)
    ps_xr = [psum(es_ra1, f"ps_xr{i}", [128, 512], BF16) for i in range(2)]

    with nc.Block() as blk:
        @blk.gpsimd
        def _(g):
            g.wait_ge(S["sPE4"], 48)
            g.wait_ge(S["sVE4"], 49)
            g.wait_ge(S["sSC4"], 16)
            g.wait_ge(S["sVG4a"], 128)
            g.wait_ge(S["sVG4b"], 128)
            for b in range(NTB):
                par = b % 2
                if b >= 2:
                    g.wait_ge(S["sPER_t"], (b - 1) * 16)
                sx = S["sXRa"] if par == 0 else S["sXRb"]
                for j in range(4):
                    g.dma_start(out=xstgr[:, par, j * D:(j + 1) * D],
                                in_=xrep[(b * 4 + j) * 128:(b * 4 + j + 1) * 128, :]).then_inc(sx, 16)

        @blk.tensor
        def _(t):
            t.wait_ge(S["sVE4"], 49)
            n = 0
            for b in range(NTB):
                par = b % 2
                t.wait_ge(S["sXRa"] if par == 0 else S["sXRb"], 64 * (b // 2 + 1))
                for dj in range(NK):
                    pp = n % 2
                    if n >= 2:
                        t.wait_ge(S["sVER_t"], n - 1)
                    for j in range(4):
                        ins = t.transpose(ps_xr[pp][:, j * 128:(j + 1) * 128],
                                          xstgr[:, par, j * D + dj * 128: j * D + dj * 128 + 128],
                                          idb[:])
                    ins.then_inc(S["sPER_t"], 1)
                    n += 1

        @blk.vector
        def _(v):
            n = 0
            for b in range(NTB):
                for dj in range(NK):
                    pp = n % 2
                    v.wait_ge(S["sPER_t"], n + 1)
                    v.tensor_copy(xT[:, dj, 1 + b * 512: 1 + b * 512 + 512],
                                  ps_xr[pp][:]).then_inc(S["sVER_t"], 1)
                    n += 1

    es_ra1.close()

    es_ra2 = ExitStack()
    wg1r = sb(es_ra2, "wg1r", [128, 4, T], BF16)
    ps_hr = [psum(es_ra2, f"ps_hr{i}", [128, T], F32) for i in range(2)]

    with nc.Block() as blk:
        @blk.gpsimd
        def _(g):
            g.wait_ge(S["sPER_t"], 64)
            g.dma_start(out=wg1r[:], in_=wg1h).then_inc(S["sWR1"], 16)

        @blk.tensor
        def _(t):
            t.wait_ge(S["sWR1"], 16)
            t.wait_ge(S["sVER_t"], 64)
            for m in range(NDH):
                pp = m % 2
                if m >= 2:
                    t.wait_ge(S["sSCR"], m - 1)
                for k in range(NK):
                    for tb in range(NTB):
                        ins = t.matmul(ps_hr[pp][:, tb * 512:(tb + 1) * 512],
                                       wg1r[:, k // 4, (k % 4) * DH + m * 128:(k % 4) * DH + (m + 1) * 128],
                                       xT[:, k, 1 + tb * 512: 1 + tb * 512 + 512],
                                       start=(k == 0), stop=(k == NK - 1))
                ins.then_inc(S["sPER"], 1)

        @blk.scalar
        def _(s):
            s.wait_ge(S["sPE4"], 48)
            for m in range(NDH):
                pp = m % 2
                s.wait_ge(S["sPER"], m + 1)
                for tb in range(NTB):
                    ins = s.activation(hT[:, m, tb * 512:(tb + 1) * 512],
                                       ps_hr[pp][:, tb * 512:(tb + 1) * 512],
                                       AF.Relu, bias=bg1c[:, m:m + 1])
                ins.then_inc(S["sSCR"], 1)

    es_ra2.close()

    es_rb = ExitStack()
    wur_stg = sb(es_rb, "wur_stg", [128, 2, D], BF16)
    growr = sb(es_rb, "growr", [128, 2, T], BF16)
    ugrowr = sb(es_rb, "ugrowr", [128, T], BF16)
    ps_gr = psum(es_rb, "ps_gr", [128, T], F32)
    ps_ur = psum(es_rb, "ps_ur", [128, T], F32)

    with nc.Block() as blk:
        @blk.gpsimd
        def _(g):
            g.wait_ge(S["sPER"], 4)
            for rc in range(NK):
                par = rc % 2
                if rc >= 2:
                    g.wait_ge(S["sPER"], 4 + 2 * (rc - 2) + 2)
                g.dma_start(out=wur_stg[:, par, :],
                            in_=wuch[rc]).then_inc(S["sWRa"] if par == 0 else S["sWRb"], 16)

        @blk.tensor
        def _(t):
            t.wait_ge(S["sSCR"], 4)
            for rc in range(NK):
                par = rc % 2
                if rc >= 1:
                    t.wait_ge(S["sSCR"], 4 + rc)
                for k in range(NDH):
                    for tb in range(NTB):
                        ins = t.matmul(ps_gr[:, tb * 512:(tb + 1) * 512],
                                       wg2[:, k, rc * 128:(rc + 1) * 128],
                                       hT[:, k, tb * 512:(tb + 1) * 512],
                                       start=(k == 0), stop=(k == NDH - 1))
                ins.then_inc(S["sPER"], 1)
                t.wait_ge(S["sWRa"] if rc % 2 == 0 else S["sWRb"], 16 * (rc // 2 + 1))
                if rc >= 1:
                    t.wait_ge(S["sVER"], rc)
                for k in range(NK):
                    for tb in range(NTB):
                        ins = t.matmul(ps_ur[:, tb * 512:(tb + 1) * 512],
                                       wur_stg[:, par, k * 128:(k + 1) * 128],
                                       xT[:, k, 1 + tb * 512: 1 + tb * 512 + 512],
                                       start=(k == 0), stop=(k == NK - 1))
                ins.then_inc(S["sPER"], 1)

        @blk.scalar
        def _(s):
            for rc in range(NK):
                par = rc % 2
                s.wait_ge(S["sPER"], 4 + 2 * rc + 1)
                if rc >= 2:
                    s.wait_ge(S["sVER"], rc - 1)
                for tb in range(NTB):
                    ins = s.activation(growr[:, par, tb * 512:(tb + 1) * 512],
                                       ps_gr[:, tb * 512:(tb + 1) * 512],
                                       AF.Sigmoid, bias=bg2c[:, rc:rc + 1])
                ins.then_inc(S["sSCR"], 1)

        @blk.vector
        def _(v):
            for rc in range(NK):
                par = rc % 2
                v.wait_ge(S["sPER"], 4 + 2 * rc + 2)
                v.wait_ge(S["sSCR"], 4 + rc + 1)
                for tb in range(NTB):
                    ins = v.scalar_tensor_tensor(out=ugrowr[:, tb * 512:(tb + 1) * 512],
                                                 in0=ps_ur[:, tb * 512:(tb + 1) * 512],
                                                 scalar=buc[:, rc:rc + 1],
                                                 in1=growr[:, par, tb * 512:(tb + 1) * 512],
                                                 op0=ALU.add, op1=ALU.mult,
                                                 accum_out=S4r[:, rc * 4 + tb: rc * 4 + tb + 1])
                ins.then_inc(S["sVER"], 1)
            v.reduce_sum(Sboth[:, 16:32], S4r[:].rearrange("p (rc tb) -> p rc tb", tb=4),
                         axis=AX.X).then_inc(S["sVER"], 1)

    es_rb.close()
    es_b.close()
    es_xT.close()

    es_56 = ExitStack()
    wur = sb(es_56, "wur", [128, NK, D], BF16)
    vgld = sb(es_56, "vgld", [128, 2, T], BF16)
    cxt_stg = sb(es_56, "cxt_stg", [128, 2, D], BF16, is_transpose_dest=True)
    z1t = sb(es_56, "z1t", [128, D], F32)
    gotmp = sb(es_56, "gotmp", [128, T], BF16)
    zsq = sb(es_56, "zsq", [128, D], BF16)
    y1stg = sb(es_56, "y1stg", [128, 2, D], BF16)
    ps_uv = [psum(es_56, f"ps_uv{i}", [128, D], F32) for i in range(2)]

    with nc.Block() as blk:
        @blk.gpsimd
        def _(g):
            g.wait_ge(S["sLD0"], 16 * N_LD0)
            g.wait_ge(S["sPER"], 36)
            g.wait_ge(S["sSCR"], 20)
            g.wait_ge(S["sVER"], 17)
            g.dma_start(out=wur[:], in_=wurh).then_inc(S["sWUR"], 16)
            g.indirect_dma_start(out=shSB, out_offset=IndirectOffsetOnAxis(widx[:], 0),
                                 in_=Sboth[:, 16:32], in_offset=None).then_inc(S["sEXw"], 16)
            g.wait_ge(S["sEXw"], 48)
            g.indirect_dma_start(out=shFB, out_offset=IndirectOffsetOnAxis(widx[:], 0),
                                 in_=tagc[:], in_offset=None).then_inc(S["sEXw"], 16)
            with g.register("nd") as nd, g.register("tgt") as tgt, g.register("f") as f:
                tv = g.value_load(tagc[0:1, 0:1])
                nda = g.value_load(ndab[0:1, 0:1])
                g.reg_mov(nd, nda)
                g.reg_mov(tgt, 0)
                with g.While(nd):
                    g.reg_add(tgt, tgt, 16)
                    g.indirect_dma_start(out=flg[:], out_offset=None,
                                         in_=shFA,
                                         in_offset=IndirectOffsetOnAxis(ridx[:], 0)).then_inc(S["sPL"], 16)
                    g.wait_ge(S["sPL"], tgt)
                    fv = g.value_load(flg[0:1, 0:1])
                    g.reg_alu(f, fv, tv, ALU.is_equal)
                    g.reg_alu(nd, f, 1, ALU.bitwise_xor)
                ndb = g.value_load(ndab[0:1, 1:2])
                g.reg_mov(nd, ndb)
                with g.While(nd):
                    g.reg_add(tgt, tgt, 16)
                    g.indirect_dma_start(out=flg[:], out_offset=None,
                                         in_=shFB,
                                         in_offset=IndirectOffsetOnAxis(ridx[:], 0)).then_inc(S["sPL"], 16)
                    g.wait_ge(S["sPL"], tgt)
                    fv = g.value_load(flg[0:1, 0:1])
                    g.reg_alu(f, fv, tv, ALU.is_equal)
                    g.reg_alu(nd, f, 1, ALU.bitwise_xor)
            g.indirect_dma_start(out=Pboth[:, 0:16], out_offset=None,
                                 in_=shSA,
                                 in_offset=IndirectOffsetOnAxis(ridxA[:], 0)).then_inc(S["sEXr"], 16)
            g.indirect_dma_start(out=Pboth[:, 16:32], out_offset=None,
                                 in_=shSB,
                                 in_offset=IndirectOffsetOnAxis(ridxB[:], 0)).then_inc(S["sEXr"], 16)

        @blk.vector
        def _(v):
            v.wait_ge(S["sEXr"], 32)
            v.tensor_scalar_mul(Pcol[:], Sboth[:, 16:32], amc[:, 0:1])
            v.scalar_tensor_tensor(out=Pcol[:], in0=Pboth[:, 16:32], scalar=amc[:, 1:2],
                                   in1=Pcol[:], op0=ALU.mult, op1=ALU.add)
            v.scalar_tensor_tensor(out=Pcol[:], in0=Pboth[:, 0:16], scalar=amc[:, 2:3],
                                   in1=Pcol[:], op0=ALU.mult, op1=ALU.add).then_inc(S["sVEX"], 1)

    with nc.Block() as blk:
        @blk.sync
        def _(sy):
            sy.wait_ge(S["sPER"], 36)
            sy.wait_ge(S["sSCR"], 20)
            sy.wait_ge(S["sVER"], 17)
            for rc in range(NK):
                par = rc % 2
                if rc >= 2:
                    sy.wait_ge(S["sVE5"], rc - 1)
                sy.dma_start(out=vgld[:, par, :],
                             in_=vgh[rc * 128:(rc + 1) * 128, :]).then_inc(S["sVG5a"] if par == 0 else S["sVG5b"], 16)

        @blk.vector
        def _(v):
            v.wait_ge(S["sVEX"], 1)
            for rc in range(NK):
                par = rc % 2
                v.wait_ge(S["sVG5a"] if rc % 2 == 0 else S["sVG5b"], 16 * (rc // 2 + 1))
                v.scalar_tensor_tensor(out=gotmp[:], in0=cs[:, rc, :],
                                       scalar=Pcol[:, rc:rc + 1], in1=vgld[:, par, :],
                                       op0=ALU.add, op1=ALU.mult)
                v.tensor_copy(cs[:, rc, :], gotmp[:]).then_inc(S["sVE5"], 1)

    with nc.Block() as blk:
        @blk.sync
        def _(sy):
            for tt in range(NTT):
                par = tt % 2
                if tt >= 2:
                    sy.wait_ge(S["sVE6"], 4 * (tt - 2) + 1)
                sy.dma_start_transpose(cxt_stg[:, par, :],
                                       cxh[:, tt * 128:(tt + 1) * 128]).then_inc(S["sCXTa"] if par == 0 else S["sCXTb"], 16)
                if tt >= 2:
                    ts = tt - 2
                    sy.wait_ge(S["sVE6"], 4 * ts + 4)
                    sy.dma_start(out=y1h[ts * 128:(ts + 1) * 128, :],
                                 in_=y1stg[:, ts % 2, :]).then_inc(S["sY1a"] if ts % 2 == 0 else S["sY1b"], 16)
            for ts in (NTT - 2, NTT - 1):
                sy.wait_ge(S["sVE6"], 4 * ts + 4)
                sy.dma_start(out=y1h[ts * 128:(ts + 1) * 128, :],
                             in_=y1stg[:, ts % 2, :]).then_inc(S["sY1a"] if ts % 2 == 0 else S["sY1b"], 16)

        @blk.tensor
        def _(t):
            t.wait_ge(S["sVE5"], 16)
            t.wait_ge(S["sWUR"], 16)
            for tt in range(NTT):
                pp = tt % 2
                if tt >= 2:
                    t.wait_ge(S["sVE6"], 4 * (tt - 2) + 1)
                for k in range(NK):
                    for db in range(4):
                        ins = t.matmul(ps_uv[pp][:, db * 512:(db + 1) * 512],
                                       cs[:, k, tt * 128:(tt + 1) * 128],
                                       wur[:, k, db * 512:(db + 1) * 512],
                                       start=(k == 0), stop=(k == NK - 1))
                ins.then_inc(S["sPE6"], 1)

        @blk.vector
        def _(v):
            for tt in range(NTT):
                pp = tt % 2
                v.wait_ge(S["sPE6"], tt + 1)
                v.wait_ge(S["sCXTa"] if tt % 2 == 0 else S["sCXTb"], 16 * (tt // 2 + 1))
                if tt >= 1:
                    v.wait_ge(S["sSC6"], 3 * (tt - 1) + 1)
                v.scalar_tensor_tensor(out=z1t[:], in0=ps_uv[pp][:], scalar=0.0,
                                       in1=cxt_stg[:, pp, :], op0=ALU.add,
                                       op1=ALU.add).then_inc(S["sVE6"], 1)
                v.wait_ge(S["sSC6"], 3 * tt + 1)
                v.tensor_scalar_mul(rowP[:], rowmS[:], rowmS[:])
                v.tensor_scalar_add(rowQ[:], rowtS2[:], EPS).then_inc(S["sVE6"], 1)
                v.wait_ge(S["sSC6"], 3 * tt + 2)
                v.tensor_scalar_mul(rowH[:], rowL[:], -0.5).then_inc(S["sVE6"], 1)
                v.wait_ge(S["sSC6"], 3 * tt + 3)
                if tt >= 2:
                    v.wait_ge(S["sY1a"] if tt % 2 == 0 else S["sY1b"], 16 * (tt // 2))
                v.tensor_scalar(out=y1stg[:, pp, :], in0=z1t[:], scalar1=rowmS[:],
                                scalar2=rowRS[:], op0=ALU.subtract, op1=ALU.mult).then_inc(S["sVE6"], 1)

        @blk.scalar
        def _(s):
            s.wait_ge(S["sPER"], 36)
            for tt in range(NTT):
                s.wait_ge(S["sVE6"], 4 * tt + 1)
                s.activation(zsq[:], z1t[:], AF.Square, accum_out=rowsq[:])
                s.activation(zsq[:], z1t[:], AF.Copy, accum_out=rowsumS[:])
                s.mul(rowmS[:], rowsumS[:], 1.0 / D)
                s.mul(rowtS2[:], rowsq[:], 1.0 / D).then_inc(S["sSC6"], 1)
                s.wait_ge(S["sVE6"], 4 * tt + 2)
                s.activation(rowL[:], rowP[:], AF.Ln, scale=-1.0, bias=rowQ[:]).then_inc(S["sSC6"], 1)
                s.wait_ge(S["sVE6"], 4 * tt + 3)
                s.activation(rowRS[:], rowH[:], AF.Exp).then_inc(S["sSC6"], 1)

    es_56.close()
    es_cs.close()

    es_p7 = ExitStack()
    y1f = sb(es_p7, "y1f", [128, NK, 1024], BF16, is_transpose_dest=True)
    hdn = sb(es_p7, "hdn", [128, NIC, 1024], BF16)
    wgf = sb(es_p7, "wgf", [128, 2, D], BF16)
    wuf = sb(es_p7, "wuf", [128, 2, D], BF16)
    wdf = sb(es_p7, "wdf", [128, 2, II], BF16)
    ggs = sb(es_p7, "ggs", [128, 1024], BF16)
    uus = sb(es_p7, "uus", [128, 1024], BF16)
    z2s = sb(es_p7, "z2s", [128, 2, 1024], BF16)
    z2t = sb(es_p7, "z2t", [128, 2, D], BF16, is_transpose_dest=True)
    zsq7 = sb(es_p7, "zsq7", [128, D], BF16)
    y2s = sb(es_p7, "y2s", [128, 2, D], F32)
    ps_a = [psum(es_p7, f"ps_a{i}", [128, 1024], F32) for i in range(2)]
    ps_b = [psum(es_p7, f"ps_b{i}", [128, 1024], F32) for i in range(2)]

    PEH = NIC + NK

    def p7_gpsimd(g):
        g.wait_ge(S["sPE6"], 16)
        g.wait_ge(S["sVE6"], 64)
        g.wait_ge(S["sSC6"], 48)
        g.wait_ge(S["sY1a"], 128)
        g.wait_ge(S["sY1b"], 128)
        for h in range(2):
            for ic in range(NIC):
                par = ic % 2
                n = h * NIC + ic
                if n >= 2:
                    hp, icp = divmod(n - 2, NIC)
                    g.wait_ge(S["sPE7"], PEH * hp + icp + 1)
                swg = S["sW7ga"] if par == 0 else S["sW7gb"]
                g.dma_start(out=wgf[:, par, :], in_=wgch[ic]).then_inc(swg, 16)
                g.dma_start(out=wuf[:, par, :], in_=wupch[ic]).then_inc(swg, 16)
            for dm in range(NK):
                par = dm % 2
                n = h * NK + dm
                if n >= 2:
                    hp, dmp = divmod(n - 2, NK)
                    g.wait_ge(S["sPE7"], PEH * hp + NIC + dmp + 1)
                g.dma_start(out=wdf[:, par, :],
                            in_=wdch[dm]).then_inc(S["sW7da"] if par == 0 else S["sW7db"], 16)

    def p7_tensor(t):
        for h in range(2):
            base = PEH * h
            t.wait_ge(S["sY1F"], 256 * (h + 1))
            if h == 1:
                t.wait_ge(S["sVE7"], PEH)
            for ic in range(NIC):
                par = ic % 2
                if ic >= 2:
                    t.wait_ge(S["sVE7"], base + (ic - 2) + 1)
                nw = h * NIC + ic
                t.wait_ge(S["sW7ga"] if ic % 2 == 0 else S["sW7gb"], 32 * (nw // 2 + 1))
                for k in range(NK):
                    for tb in range(2):
                        t.matmul(ps_a[par][:, tb * 512:(tb + 1) * 512],
                                 wgf[:, par, k * 128:(k + 1) * 128],
                                 y1f[:, k, tb * 512:(tb + 1) * 512],
                                 start=(k == 0), stop=(k == NK - 1))
                for k in range(NK):
                    for tb in range(2):
                        ins = t.matmul(ps_b[par][:, tb * 512:(tb + 1) * 512],
                                       wuf[:, par, k * 128:(k + 1) * 128],
                                       y1f[:, k, tb * 512:(tb + 1) * 512],
                                       start=(k == 0), stop=(k == NK - 1))
                ins.then_inc(S["sPE7"], 1)
            t.wait_ge(S["sVE7"], base + NIC)
            for dm in range(NK):
                par = dm % 2
                if dm >= 2:
                    t.wait_ge(S["sVE7"], base + NIC + (dm - 2) + 1)
                mw = h * NK + dm
                t.wait_ge(S["sW7da"] if dm % 2 == 0 else S["sW7db"], 16 * (mw // 2 + 1))
                for ki in range(NIC):
                    for tb in range(2):
                        ins = t.matmul(ps_a[par][:, tb * 512:(tb + 1) * 512],
                                       wdf[:, par, ki * 128:(ki + 1) * 128],
                                       hdn[:, ki, tb * 512:(tb + 1) * 512],
                                       start=(ki == 0), stop=(ki == NIC - 1))
                ins.then_inc(S["sPE7"], 1)

    def p7_scalar(s):
        s.wait_ge(S["sVE6"], 64)
        s.wait_ge(S["sLD0"], 16 * N_LD0)
        for h in range(2):
            base = PEH * h
            for ic in range(NIC):
                par = ic % 2
                s.wait_ge(S["sPE7"], base + ic + 1)
                if h * NIC + ic >= 1:
                    s.wait_ge(S["sVE7"], base + ic if ic >= 1 else PEH * h)
                s.activation(ggs[:], ps_a[par][:], AF.Silu,
                             scale=gsc[:, ic:ic + 1]).then_inc(S["sSC7"], 1)
            for t8 in range(8):
                tt = h * 8 + t8
                s.wait_ge(S["sZ2Ta"] if tt % 2 == 0 else S["sZ2Tb"], 16 * (tt // 2 + 1))
                if tt >= 1:
                    s.wait_ge(S["sVE7L"], 3 * (tt - 1) + 3)
                s.activation(zsq7[:], z2t[:, tt % 2, :], AF.Square, accum_out=rowsq[:])
                s.activation(zsq7[:], z2t[:, tt % 2, :], AF.Copy, accum_out=rowsumS[:])
                s.mul(rowmS[:], rowsumS[:], 1.0 / D)
                s.mul(rowtS2[:], rowsq[:], 1.0 / D).then_inc(S["sSC7L"], 1)
                s.wait_ge(S["sVE7L"], 3 * tt + 1)
                s.activation(rowL[:], rowP[:], AF.Ln, scale=-1.0, bias=rowQ[:]).then_inc(S["sSC7L"], 1)
                s.wait_ge(S["sVE7L"], 3 * tt + 2)
                s.activation(rowRS[:], rowH[:], AF.Exp).then_inc(S["sSC7L"], 1)

    def p7_vector(v):
        v.wait_ge(S["sLD0"], 16 * N_LD0)
        for h in range(2):
            base = PEH * h
            for ic in range(NIC):
                par = ic % 2
                v.wait_ge(S["sPE7"], base + ic + 1)
                v.wait_ge(S["sSC7"], h * NIC + ic + 1)
                v.tensor_scalar_mul(uus[:], ps_b[par][:], usc[:, ic:ic + 1])
                v.tensor_tensor(out=hdn[:, ic, :], in0=ggs[:], in1=uus[:],
                                op=ALU.mult).then_inc(S["sVE7"], 1)
            for dm in range(NK):
                par = dm % 2
                v.wait_ge(S["sPE7"], base + NIC + dm + 1)
                if dm >= 2 or h == 1:
                    mz = h * NK + dm
                    v.wait_ge(S["sZ2a"] if dm % 2 == 0 else S["sZ2b"], 16 * (mz // 2))
                v.scalar_tensor_tensor(out=z2s[:, dm % 2, :], in0=ps_a[par][:],
                                       scalar=dsc[:, dm:dm + 1], in1=y1f[:, dm, :],
                                       op0=ALU.mult, op1=ALU.add).then_inc(S["sVE7"], 1)
            for t8 in range(8):
                tt = h * 8 + t8
                pp = tt % 2
                v.wait_ge(S["sSC7L"], 3 * tt + 1)
                v.tensor_scalar_mul(rowP[:], rowmS[:], rowmS[:])
                v.tensor_scalar_add(rowQ[:], rowtS2[:], EPS).then_inc(S["sVE7L"], 1)
                v.wait_ge(S["sSC7L"], 3 * tt + 2)
                v.tensor_scalar_mul(rowH[:], rowL[:], -0.5).then_inc(S["sVE7L"], 1)
                v.wait_ge(S["sSC7L"], 3 * tt + 3)
                if tt >= 2:
                    v.wait_ge(S["sOUTa"] if tt % 2 == 0 else S["sOUTb"], 16 * (tt // 2))
                v.tensor_scalar(out=y2s[:, pp, :], in0=z2t[:, pp, :], scalar1=rowmS[:],
                                scalar2=rowRS[:], op0=ALU.subtract, op1=ALU.mult).then_inc(S["sVE7L"], 1)

    def p7_sync(sy):
        sy.wait_ge(S["sVE6"], 64)
        sy.wait_ge(S["sSC6"], 48)
        sy.wait_ge(S["sY1a"], 128)
        sy.wait_ge(S["sY1b"], 128)
        for h in range(2):
            if h == 1:
                sy.wait_ge(S["sVE7"], PEH)
            for k in range(NK):
                sy.dma_start_transpose(y1f[:, k, :],
                                       y1h[h * 1024:(h + 1) * 1024, k * 128:(k + 1) * 128]).then_inc(S["sY1F"], 16)
            for dm in range(NK):
                sy.wait_ge(S["sVE7"], PEH * h + NIC + dm + 1)
                sy.dma_start(out=z2h[dm * 128:(dm + 1) * 128, h * 1024:(h + 1) * 1024],
                             in_=z2s[:, dm % 2, :]).then_inc(S["sZ2a"] if dm % 2 == 0 else S["sZ2b"], 16)
            for t8 in range(8):
                tt = h * 8 + t8
                pp = tt % 2
                sy.wait_ge(S["sZ2a"], 128 * (h + 1))
                sy.wait_ge(S["sZ2b"], 128 * (h + 1))
                if tt >= 2:
                    sy.wait_ge(S["sVE7L"], 3 * (tt - 2) + 3)
                    sy.wait_ge(S["sSC7L"], 3 * (tt - 2) + 1)
                sy.dma_start_transpose(z2t[:, pp, :],
                                       z2h[:, tt * 128:(tt + 1) * 128]).then_inc(S["sZ2Ta"] if pp == 0 else S["sZ2Tb"], 16)
                if tt >= 2:
                    ts = tt - 2
                    sy.wait_ge(S["sVE7L"], 3 * ts + 3)
                    sy.dma_start(out=y[ts * 128:(ts + 1) * 128, :],
                                 in_=y2s[:, ts % 2, :]).then_inc(S["sOUTa"] if ts % 2 == 0 else S["sOUTb"], 16)
        for ts in (NTT - 2, NTT - 1):
            sy.wait_ge(S["sVE7L"], 3 * ts + 3)
            sy.dma_start(out=y[ts * 128:(ts + 1) * 128, :],
                         in_=y2s[:, ts % 2, :]).then_inc(S["sOUTa"] if ts % 2 == 0 else S["sOUTb"], 16)
        sy.wait_ge(S["sOUTa"], 128)
        sy.wait_ge(S["sOUTb"], 128)

    with nc.Block() as blk:
        blk.gpsimd(p7_gpsimd)
        blk.tensor(p7_tensor)
        blk.scalar(p7_scalar)
        blk.vector(p7_vector)
        blk.sync(p7_sync)

    es_p7.close()
    es0.close()
    return nc


def _prep_host(inputs):
    x = np.ascontiguousarray(np.asarray(inputs["L0_attn_x"], dtype=np.float32).reshape(16384, D))
    Wu = np.asarray(inputs["L0_attn_Wu"], np.float32)
    Wv = np.asarray(inputs["L0_attn_Wv"], np.float32)
    Wg1 = np.asarray(inputs["L0_attn_Wg1"], np.float32)
    Wg2 = np.asarray(inputs["L0_attn_Wg2"], np.float32)
    bu = np.asarray(inputs["L0_attn_bu"], np.float32)
    bv = np.asarray(inputs["L0_attn_bv"], np.float32)
    bg1 = np.asarray(inputs["L0_attn_bg1"], np.float32)
    bg2 = np.asarray(inputs["L0_attn_bg2"], np.float32)
    conv_w = np.asarray(inputs["L0_attn_conv_w"], np.float32)
    conv_b = np.asarray(inputs["L0_attn_conv_b"], np.float32)
    gw = np.asarray(inputs["L0_ffn_gate_w"], np.float32)
    uw = np.asarray(inputs["L0_ffn_up_w"], np.float32)
    dw = np.asarray(inputs["L0_ffn_down_w"], np.float32)
    gs = np.asarray(inputs["L0_ffn_gate_s"], np.float32)
    us = np.asarray(inputs["L0_ffn_up_s"], np.float32)
    ds = np.asarray(inputs["L0_ffn_down_s"], np.float32)

    C = np.ascontiguousarray
    wg1h = C(Wg1.T.reshape(NK, 128, DH).transpose(1, 0, 2)).reshape(128, 4, T)
    wg2h = C(Wg2.T.reshape(NDH, 128, R).transpose(1, 0, 2))
    WuT = C(Wu.T)
    wuch = C(WuT.reshape(NK, 128, NK, 128).transpose(2, 1, 0, 3)).reshape(NK, 128, D)
    WvT = C(Wv.T)
    wvch = C(WvT.reshape(NK, 128, NK, 128).transpose(2, 1, 0, 3)).reshape(NK, 128, D)
    wurh = C(WuT.reshape(NK, 128, D).transpose(1, 0, 2))
    wgch = C(gw.T.reshape(NK, 128, NIC, 128).transpose(2, 1, 0, 3)).reshape(NIC, 128, D)
    wupch = C(uw.T.reshape(NK, 128, NIC, 128).transpose(2, 1, 0, 3)).reshape(NIC, 128, D)
    wdch = C(dw.T.reshape(NIC, 128, NK, 128).transpose(2, 1, 0, 3)).reshape(NK, 128, II)
    buh = C(bu.reshape(NK, 128).T)
    bvh = C(bv.reshape(NK, 128).T)
    bg1h = C(bg1.reshape(NDH, 128).T)
    bg2h = C(bg2.reshape(NK, 128).T)
    cwh = C(np.concatenate([conv_w[:, 0, j].reshape(NK, 128).T for j in range(3)], axis=1))
    cbh = C(conv_b.reshape(NK, 128).T)
    gsh = C(gs.reshape(NIC, 128).T)
    ush = C(us.reshape(NIC, 128).T)
    dsh = C(ds.reshape(NK, 128).T)
    identh = np.eye(128).astype(ml_dtypes.bfloat16)
    tag = np.int32(np.random.randint(1, 2 ** 31 - 1))
    tagh = np.full((128, 1), tag, np.int32)
    epsh = np.full((128, 1), EPS, np.float32)
    ar = np.arange(128, dtype=np.uint32).reshape(128, 1)

    shared = dict(wg1h=wg1h, wg2h=wg2h, wuch=wuch, wvch=wvch, wurh=wurh,
                  wgch=wgch, wupch=wupch, wdch=wdch, buh=buh, bvh=bvh,
                  bg1h=bg1h, bg2h=bg2h, cwh=cwh, cbh=cbh, gsh=gsh, ush=ush,
                  dsh=dsh, identh=identh, tagh=tagh, epsh=epsh)

    in_maps = []
    for c in range(8):
        lo = c * T
        xc_ = C(x[lo:lo + T])
        b0 = (c // 4) * 4 * T
        xhl_ = C(x[lo - 1:lo]) if (c % 4) != 0 else np.zeros((1, D), np.float32)
        xhr_ = C(x[lo + T:lo + T + 1]) if (c % 4) != 3 else np.zeros((1, D), np.float32)
        if c % 4 == 2:
            xrep_ = C(x[b0:b0 + T])
        elif c % 4 == 3:
            xrep_ = C(x[b0 + T:b0 + 2 * T])
        else:
            xrep_ = xc_
        m = c % 4
        a1 = 1.0 if m >= 2 else 0.0
        a2 = 1.0 if m >= 2 else 0.0
        a3 = 1.0 if m in (1, 3) else 0.0
        amh = np.tile(np.array([[a1, a2, a3]], np.float32), (128, 1))
        p = c & 1
        nda = np.array([[1 if m == 1 else 0]], np.int32)
        ndb = np.array([[1 if m >= 2 else 0]], np.int32)
        wi = p * 128 + ar
        ri = (1 - p) * 128 + ar
        ria = ri if m >= 1 else wi
        rib = ri if m >= 2 else wi
        in_maps.append(dict(shared, xc=xc_, xrep=xrep_, xhl=xhl_, xhr=xhr_, amh=amh,
                            ndah=nda, ndbh=ndb,
                            widxh=wi, ridxh=ri, ridxah=ria, ridxbh=rib))
    return in_maps


def kernel(**inputs):
    in_maps = _prep_host(inputs)
    nc = build_nc()
    res = run_bass_kernel_spmd(nc, in_maps, core_ids=list(range(8)))
    out = np.concatenate([res.results[c]["y"] for c in range(8)], axis=0)
    return np.ascontiguousarray(out.reshape(2, 8192, D).astype(np.float32))
